# version 24
# baseline (speedup 1.0000x reference)
import contextlib
import numpy as np
import concourse.bass as bass
import concourse.mybir as mybir
from concourse.bass_utils import run_bass_kernel_spmd

F32 = mybir.dt.float32
BF16 = mybir.dt.bfloat16
AF = mybir.ActivationFunctionType
ALU = mybir.AluOpType
AX = mybir.AxisListType

T = 4096
D = 1024
HID = 4096
NT = T // 128
NB = T // 512
EPS = 1e-5
NEG = -1.0e30


def set_dims(t):
    global T, NT, NB
    T, NT, NB = t, t // 128, t // 512


class MK:
    def __init__(self, phases=("l0a", "l0f", "l1a", "l1f"), debug=False):
        self.phases = phases
        self.debug = debug
        nc = bass.Bass("TRN2", target_bir_lowering=False)
        self.nc = nc
        self.cnt = {}
        self._stack = contextlib.ExitStack()
        self.last_postln_v = None
        self.last_postln_free = None
        self.psS_rel = [None, None]
        self.psO_rel = None
        self.psT_rel = [None, None]
        self.ppT_rel = [None, None]

        d = {}
        ei, eo = "ExternalInput", "ExternalOutput"
        d["x"] = nc.dram_tensor("x", [T, D], F32, kind=ei).ap()
        d["W0"] = nc.dram_tensor("W0", [D, 3 * D], F32, kind=ei).ap()
        d["w1_0"] = nc.dram_tensor("w1_0", [HID, D], F32, kind=ei).ap()
        d["b1_0"] = nc.dram_tensor("b1_0", [HID], F32, kind=ei).ap()
        d["w2_0"] = nc.dram_tensor("w2_0", [D, HID], F32, kind=ei).ap()
        d["b2_0"] = nc.dram_tensor("b2_0", [D], F32, kind=ei).ap()
        d["gam0"] = nc.dram_tensor("gam0", [D], F32, kind=ei).ap()
        d["bet0"] = nc.dram_tensor("bet0", [D], F32, kind=ei).ap()
        d["wq1"] = nc.dram_tensor("wq1", [D, D], F32, kind=ei).ap()
        d["wkd1"] = nc.dram_tensor("wkd1", [D, 64], F32, kind=ei).ap()
        d["wvd1"] = nc.dram_tensor("wvd1", [D, 64], F32, kind=ei).ap()
        d["wku1"] = nc.dram_tensor("wku1", [32, 128], F32, kind=ei).ap()
        d["wvu1"] = nc.dram_tensor("wvu1", [32, 128], F32, kind=ei).ap()
        d["wo1"] = nc.dram_tensor("wo1", [D, D], F32, kind=ei).ap()
        d["w1_1"] = nc.dram_tensor("w1_1", [HID, D], F32, kind=ei).ap()
        d["w2_1"] = nc.dram_tensor("w2_1", [D, HID], F32, kind=ei).ap()
        d["out"] = nc.dram_tensor("out", [T, D], F32, kind=eo).ap()
        d["xres"] = nc.dram_tensor("xres", [T, D], F32,
                                   kind=eo if debug else "Internal").ap()
        d["qT_h"] = nc.dram_tensor("qT_h", [D, T], BF16).ap()
        d["v_h"] = nc.dram_tensor("v_h", [T, D], BF16).ap()
        d["h_q"] = nc.dram_tensor("h_q", [T, HID], BF16).ap()
        if debug:
            d["dbg_q"] = nc.dram_tensor("dbg_q", [128, 8 * T], BF16, kind="ExternalOutput").ap()
            d["dbg_k"] = nc.dram_tensor("dbg_k", [128, 2 * T], BF16, kind="ExternalOutput").ap()
            d["dbg_v"] = nc.dram_tensor("dbg_v", [128, 2 * T], BF16, kind="ExternalOutput").ap()
            d["dbg_o"] = nc.dram_tensor("dbg_o", [128, 1024], BF16, kind="ExternalOutput").ap()
        self.d = d

        def sb(name, shape, dt, stack=None):
            return (stack or self._stack).enter_context(nc.sbuf_tensor(name, shape, dt))
        self.sb = sb

        def ps(name, shape, dt):
            return self._stack.enter_context(nc.psum_tensor(name, shape, dt))

        self.identb = sb("identb", [128, 128], BF16)
        self.cmask = sb("cmask", [128, 128], F32)
        self.amask = sb("amask", [128, 128], F32)
        self.ones1 = sb("ones1", [1, 512], F32)
        self.onesc = sb("onesc", [128, 1], F32)
        self.magic = sb("magic", [128, 1], F32)
        self.nmagic = sb("nmagic", [128, 1], F32)
        self.epsv = sb("epsv", [128, 1], F32)
        self.lnt = sb("lnt", [128, 1], F32)
        self.c127 = sb("c127", [128, 1], F32)
        self.st6 = sb("st6", [128, 12], F32)
        self.fst6 = sb("fst6", [128, 24], F32)
        self.fst2 = sb("fst2", [128, 4], F32)
        self.frstd = sb("frstd", [128, 2], F32)
        self.fnb = sb("fnb", [128, 2], F32)
        self.flnt = sb("flnt", [128, 2], F32)
        self.fam = sb("fam", [128, 2], F32)
        self.fsact = sb("fsact", [128, 4], F32)
        self.frsc = sb("frsc", [128, 2], F32)
        self.st2 = sb("st2", [128, 2], F32)
        self.rstd = sb("rstd", [128, 1], F32)
        self.nbias = sb("nbias", [128, 1], F32)
        self.lnsets = [
            (self.st6, self.st2, self.lnt, self.rstd, self.nbias),
            (sb("st6b", [128, 12], F32), sb("st2b", [128, 2], F32),
             sb("lntb", [128, 1], F32), sb("rstdb", [128, 1], F32),
             sb("nbiasb", [128, 1], F32)),
        ]
        self.negmh = sb("negmh", [128, 32], F32)
        self.negM = sb("negM", [128, 4], F32)
        self.fac = sb("fac", [128, 32], F32)
        self.fsum = sb("fsum", [128, 32], F32)
        self.ftmp = sb("ftmp", [128, 32], F32)
        self.rden = sb("rden", [128, 4], F32)
        self.rden8 = sb("rden8", [128, 8], F32)
        self.am = sb("am", [128, 1], F32)
        self.sact = sb("sact", [128, 1], F32)
        self.rsc1 = sb("rsc1", [128, 1], F32)
        self.rs2all = sb("rs2all", [128, 32], F32)
        self.swv = sb("swv", [128, 1], F32)
        self.swv2 = sb("swv2", [128, 1], F32)
        self.acc1 = sb("acc1", [128, 1], F32)
        self.sc11 = sb("sc11", [1, 1], F32)
        self.vec1 = sb("vec1", [1, 4096], F32)
        self.xa = sb("xa", [128, 2 * 1024], F32)
        self.xst = sb("xst", [128, 2 * 1024], F32)
        self.lnst = sb("lnst", [128, 2 * 1024], F32)

        self.psS = ps("psS", [128, 2048], F32)
        self.psO = ps("psO", [128, 1024], F32)
        self.psT = ps("psT", [128, 2048], BF16)

        class _ColOff:
            def __init__(self, base, off):
                self.base, self.off = base, off

            def __getitem__(self, idx):
                p, c = idx
                return self.base[p, c.start + self.off:c.stop + self.off]

        self.psM = _ColOff(self.psS, 1024)
        self.psN = _ColOff(self.psS, 1536)

        self.semh = {}
        for s in ("ld", "st", "v", "s", "t", "g", "b", "xa0", "xa1", "xs0", "xs1",
                  "qt0", "qt1", "hq0", "hq1", "ev0", "ev1", "sx0", "sx1", "lb",
                  "lc"):
            self.semh[s] = self._stack.enter_context(nc.semaphore(s))
            self.cnt[s] = 0

        self.block = self._stack.enter_context(nc.Block())

    def mark(self, instr, s, n=None):
        if n is None:
            n = 1 if s in ("v", "s", "t", "g", "b") else 16
        self.cnt[s] += n
        instr.then_inc(self.semh[s], n)
        return self.cnt[s]

    def w(self, eng, s, v=None):
        eng.wait_ge(self.semh[s], self.cnt[s] if v is None else v)

    def barrier(self):
        blk, me = self.block, self

        @blk.sync
        def _(e):
            me.mark(e.nop(), "b")
        snap = dict(me.cnt)
        for sec in (blk.sync, blk.vector, blk.scalar, blk.tensor, blk.gpsimd):
            @sec
            def _(e):
                for s in me.semh:
                    e.wait_ge(me.semh[s], snap[s])
        self.psS_rel = [None, None]
        self.psO_rel = None
        self.psT_rel = [None, None]
        self.ppT_rel = [None, None]
        self.last_postln_v = None

    def trans(self, te, out_ap, in_ap, start, stop):
        return te.matmul(out_ap, in_ap, self.identb[:], is_transpose=True,
                         start=start, stop=stop, skip_group_check=True)

    def ln_stats(self, x_ap):
        me, blk = self, self.block

        @blk.vector
        def _(v):
            v.drain()
            v.bn_stats(out=me.st6[:, 0:6], in_=x_ap[:, 0:512])
            v.bn_stats(out=me.st6[:, 6:12], in_=x_ap[:, 512:1024])
            v.drain()
            me.mark(v.bn_aggr(out=me.st2[:], in_=me.st6[:]), "v")
        vdone = me.cnt["v"]

        @blk.scalar
        def _(s):
            me.w(s, "v", vdone)
            s.activation(out=me.lnt[:], in_=me.st2[:, 1:2], func=AF.Ln,
                         bias=me.epsv[:], scale=1.0)
            s.drain()
            me.mark(s.activation(out=me.rstd[:], in_=me.lnt[:], func=AF.Exp,
                                 bias=0.0, scale=-0.5), "s")
        sdone = me.cnt["s"]

        @blk.vector
        def _(v):
            me.w(v, "s", sdone)
            me.mark(v.scalar_tensor_tensor(out=me.nbias[:], in0=me.st2[:, 0:1],
                                           scalar=-1.0, in1=me.rstd[:],
                                           op0=ALU.mult, op1=ALU.mult), "v")
        return me.cnt["v"]

    def round_(self, v, ap):
        v.tensor_scalar(out=ap, in0=ap, scalar1=self.magic[:], scalar2=None,
                        op0=ALU.add)
        v.drain()
        v.tensor_scalar(out=ap, in0=ap, scalar1=self.magic[:], scalar2=None,
                        op0=ALU.subtract)
        v.drain()

    def build(self):
        self.setup_consts()
        if "l0a" in self.phases:
            self.l0_attn()
        if "l0f" in self.phases:
            self.ffn(0)
        if "l1a" in self.phases:
            self.l1_attn()
        if "l1f" in self.phases:
            self.ffn(1)
        self.finish()
        self._stack.close()
        return self.nc

    def setup_consts(self):
        me, blk = self, self.block

        @blk.vector
        def _(v):
            v.memset(me.identb[:], 0.0)
            v.memset(me.cmask[:], 0.0)
            v.memset(me.amask[:], 0.0)
            v.memset(me.ones1[:], 1.0)
            v.memset(me.onesc[:], 1.0)
            v.memset(me.c127[:], 127.0)
            v.memset(me.epsv[:], EPS)
            me.mark(v.memset(me.magic[:], 16777216.0), "v")
        vdone = me.cnt["v"]

        @blk.gpsimd
        def _(g):
            me.w(g, "v", vdone)
            g.affine_select(out=me.identb[:], in_=me.identb[:], pattern=[[-1, 128]],
                            compare_op=ALU.not_equal, fill=1.0, channel_multiplier=1)
            g.affine_select(out=me.cmask[:], in_=me.cmask[:], pattern=[[-1, 128]],
                            compare_op=ALU.is_ge, fill=NEG, channel_multiplier=1)
            me.mark(g.affine_select(out=me.amask[:], in_=me.amask[:],
                                    pattern=[[1, 128]], compare_op=ALU.is_ge,
                                    fill=NEG, channel_multiplier=-1), "g")
        gdone = me.cnt["g"]

        @blk.vector
        def _(v):
            me.w(v, "g", gdone)
            v.tensor_scalar(out=me.magic[:], in0=me.magic[:], scalar1=1.0,
                            scalar2=16777216.0, op0=ALU.add, op1=ALU.subtract)
            v.drain()
            v.tensor_scalar(out=me.magic[:], in0=me.magic[:],
                            scalar1=(1.5 * 2.0**52 - 1.5 * 2.0**23),
                            scalar2=1.5 * 2.0**23,
                            op0=ALU.mult, op1=ALU.add)
            v.drain()
            me.mark(v.tensor_scalar(out=me.nmagic[:], in0=me.magic[:],
                                    scalar1=-1.0, scalar2=None, op0=ALU.mult), "v")
        self.barrier()

    def ln_tile_to_xlnT(self, x_ap, ti_in_block, lnb, xlnT, tt=0, lm=None,
                        xwait=None):
        me, blk = self, self.block
        if lm is None:
            lm = {k: {} for k in ("bn", "rstd", "nb", "nrm", "trans", "evac")}
        p2 = tt % 2
        st6, st2, lnt, rstd, nbias = me.lnsets[p2]

        @blk.vector
        def _(v):
            if xwait is not None:
                v.wait_ge(me.semh[xwait[0]], xwait[1])
            if tt - 2 in lm["rstd"]:
                v.wait_ge(me.semh["s"], lm["rstd"][tt - 2])
            v.drain()
            v.bn_stats(out=st6[:, 0:6], in_=x_ap[:, 0:512])
            v.bn_stats(out=st6[:, 6:12], in_=x_ap[:, 512:1024])
            v.drain()
            lm["bn"][tt] = me.mark(v.bn_aggr(out=st2[:], in_=st6[:]), "v")

        @blk.scalar
        def _(s):
            s.wait_ge(me.semh["v"], lm["bn"][tt])
            s.activation(out=lnt[:], in_=st2[:, 1:2], func=AF.Ln,
                         bias=me.epsv[:], scale=1.0)
            s.drain()
            lm["rstd"][tt] = me.mark(
                s.activation(out=rstd[:], in_=lnt[:], func=AF.Exp,
                             bias=0.0, scale=-0.5), "s")

        @blk.vector
        def _(v):
            v.wait_ge(me.semh["s"], lm["rstd"][tt])
            lm["nb"][tt] = me.mark(v.scalar_tensor_tensor(
                out=nbias[:], in0=st2[:, 0:1], scalar=-1.0, in1=rstd[:],
                op0=ALU.mult, op1=ALU.mult), "v")

        @blk.scalar
        def _(s):
            s.wait_ge(me.semh["v"], lm["nb"][tt])
            if tt - 2 in lm["trans"]:
                s.wait_ge(me.semh["t"], lm["trans"][tt - 2])
            lm["nrm"][tt] = me.mark(s.activation(
                out=lnb[:, p2 * 1024:p2 * 1024 + 1024], in_=x_ap,
                func=AF.Identity, bias=nbias[:], scale=rstd[:]), "s")

        @blk.tensor
        def _(t):
            t.wait_ge(me.semh["s"], lm["nrm"][tt])
            if tt - 2 in lm["evac"]:
                t.wait_ge(me.semh["v"], lm["evac"][tt - 2])
            last = None
            for dc in range(8):
                last = me.trans(t, me.psT[:, p2 * 1024 + dc * 128:
                                       p2 * 1024 + dc * 128 + 128],
                                lnb[:, p2 * 1024 + dc * 128:
                                    p2 * 1024 + dc * 128 + 128],
                                start=(dc == 0), stop=(dc == 7))
            lm["trans"][tt] = me.mark(last, "t")

        @blk.vector
        def _(v):
            v.wait_ge(me.semh["t"], lm["trans"][tt])
            iv = None
            for dc in range(8):
                iv = v.tensor_copy(
                    out=xlnT[:, dc * 512 + ti_in_block * 128:
                             dc * 512 + ti_in_block * 128 + 128],
                    in_=me.psT[:, p2 * 1024 + dc * 128:
                               p2 * 1024 + dc * 128 + 128])
            lm["evac"][tt] = me.mark(iv, "v")
        return me.cnt["v"]

    def postln_residual(self, y_ap, tt, src, dst):
        me, blk, d = self, self.block, self.d
        half = tt % 2
        v0, s0 = me.cnt["v"], me.cnt["s"]
        vm = {"bn": v0 + 1, "nb": v0 + 2, "add": v0 + 3}
        sm = {"st": s0 + 1, "nrm": s0 + 2}

        @blk.sync
        def _(sy):
            me.w(sy, "v")
            me.w(sy, "sx%d" % half)
            me.mark(sy.dma_start(out=me.xst[:, half * 1024:half * 1024 + 1024],
                                 in_=d[src][tt * 128:tt * 128 + 128, :]),
                    "xs%d" % half)
        lddone = me.cnt["xs%d" % half]

        @blk.vector
        def _(v):
            v.drain()
            v.bn_stats(out=me.st6[:, 0:6], in_=y_ap[:, 0:512])
            v.bn_stats(out=me.st6[:, 6:12], in_=y_ap[:, 512:1024])
            v.drain()
            assert me.mark(v.bn_aggr(out=me.st2[:], in_=me.st6[:]), "v") == vm["bn"]
            v.wait_ge(me.semh["s"], sm["st"])
            iv = v.scalar_tensor_tensor(out=me.nbias[:], in0=me.st2[:, 0:1],
                                        scalar=-1.0, in1=me.rstd[:],
                                        op0=ALU.mult, op1=ALU.mult)
            assert me.mark(iv, "v") == vm["nb"]
            v.wait_ge(me.semh["s"], sm["nrm"])
            v.wait_ge(me.semh["xs%d" % half], lddone)
            iv = v.tensor_tensor(out=me.xst[:, half * 1024:half * 1024 + 1024],
                                 in0=me.xst[:, half * 1024:half * 1024 + 1024],
                                 in1=me.lnst[:, half * 1024:half * 1024 + 1024],
                                 op=ALU.add)
            assert me.mark(iv, "v") == vm["add"]

        @blk.scalar
        def _(s):
            s.wait_ge(me.semh["v"], vm["bn"])
            s.activation(out=me.lnt[:], in_=me.st2[:, 1:2], func=AF.Ln,
                         bias=me.epsv[:], scale=1.0)
            s.drain()
            iv = s.activation(out=me.rstd[:], in_=me.lnt[:], func=AF.Exp,
                              bias=0.0, scale=-0.5)
            assert me.mark(iv, "s") == sm["st"]
            s.wait_ge(me.semh["v"], vm["nb"])
            iv = s.activation(out=me.lnst[:, half * 1024:half * 1024 + 1024],
                              in_=y_ap, func=AF.Identity,
                              bias=me.nbias[:], scale=me.rstd[:])
            assert me.mark(iv, "s") == sm["nrm"]
        self.last_postln_v = vm["add"]
        self.last_postln_free = (("v", vm["bn"]), ("s", sm["nrm"]))

        @blk.sync
        def _(sy):
            me.w(sy, "v", self.last_postln_v)
            me.mark(sy.dma_start(out=d[dst][tt * 128:tt * 128 + 128, :],
                                 in_=me.xst[:, half * 1024:half * 1024 + 1024]),
                    "sx%d" % half)

    def flash_rows(self, rows, nchunk, kT_ap, klen, diag, zerofix, pre=None,
                   normalize=True):
        me, blk = self, self.block
        CW = 1024
        nb = (klen + CW - 1) // CW
        R = len(rows)
        N = R * nb
        slots = [(me.psS, 0), (me.psS, 1024)]

        def step_rc(step):
            return step % R, step // R

        t0, v0, s0 = me.cnt["t"], me.cnt["v"], me.cnt["s"]
        t_mark = [t0 + k + 1 for k in range(N)]
        s_mark = [s0 + k + 1 for k in range(N)]
        v_items = []
        for step in range(N):
            if zerofix and step >= 2:
                v_items.append(("stt", step - 2))
            v_items.append(("red", step))
        if zerofix:
            for j in range(max(0, N - 2), N):
                v_items.append(("stt", j))
        red_mark, stt_mark = {}, {}
        for pos, (kind, j) in enumerate(v_items):
            (red_mark if kind == "red" else stt_mark)[j] = v0 + pos + 1

        def release(step):
            return ("v", stt_mark[step]) if zerofix else ("s", s_mark[step])

        @blk.tensor
        def _(t):
            for sem, val in (pre or []):
                t.wait_ge(me.semh[sem], val)
            for step in range(N):
                r, c = step_rc(step)
                sl = step % 2
                wdt = min(CW, klen - CW * c)
                ten, off = slots[sl]
                if step >= 2:
                    sem, val = release(step - 2)
                    me.w(t, sem, val)
                elif me.psS_rel[sl] is not None:
                    sem, val = me.psS_rel[sl]
                    me.w(t, sem, val)
                qap = rows[r][0]
                last = None
                for sub in range(0, wdt, 512):
                    sw = min(512, wdt - sub)
                    for dc in range(nchunk):
                        last = t.matmul(ten[:, off + sub:off + sub + sw],
                                        qap(dc),
                                        kT_ap(dc, CW * c + sub,
                                              CW * c + sub + sw),
                                        start=(dc == 0),
                                        stop=(dc == nchunk - 1))
                assert me.mark(last, "t") == t_mark[step]
        for k in range(2):
            st = N - 1 - k
            if st >= 0:
                me.psS_rel[st % 2] = release(st)

        @blk.vector
        def _(v):
            for pos, (kind, j) in enumerate(v_items):
                if kind == "red":
                    step = j
                    r, c = step_rc(step)
                    sl = step % 2
                    wdt = min(CW, klen - CW * c)
                    ten, off = slots[sl]
                    v.wait_ge(me.semh["t"], t_mark[step])
                    if step == 0:
                        v.drain()
                        v.memset(me.fsum[:, 0:N], 0.0)
                    if diag[0] == "first" and c == 0:
                        v.tensor_tensor(out=ten[:, off:off + 128],
                                        in0=ten[:, off:off + 128],
                                        in1=diag[1], op=ALU.add)
                        v.drain()
                    if diag[0] == "last" and c == nb - 1:
                        v.tensor_tensor(out=ten[:, off + wdt - 128:off + wdt],
                                        in0=ten[:, off + wdt - 128:off + wdt],
                                        in1=diag[1], op=ALU.add)
                        v.drain()
                    iv = v.reduce_max(out=me.negmh[:, r * nb + c:r * nb + c + 1],
                                      in_=ten[:, off:off + wdt],
                                      axis=AX.X, negate=True)
                    assert me.mark(iv, "v") == red_mark[step]
                else:
                    step = j
                    r, c = step_rc(step)
                    sl = step % 2
                    wdt = min(CW, klen - CW * c)
                    ten, off = slots[sl]
                    pp_row = rows[r][1]
                    v.wait_ge(me.semh["s"], s_mark[step])
                    iv = v.scalar_tensor_tensor(
                        out=pp_row[:, CW * c:CW * c + wdt],
                        in0=ten[:, off:off + wdt], scalar=0.0,
                        in1=pp_row[:, CW * c:CW * c + wdt],
                        op0=ALU.not_equal, op1=ALU.mult)
                    assert me.mark(iv, "v") == stt_mark[step]

        @blk.scalar
        def _(s):
            for step in range(N):
                r, c = step_rc(step)
                sl = step % 2
                wdt = min(CW, klen - CW * c)
                ten, off = slots[sl]
                pp_row = rows[r][1]
                s.wait_ge(me.semh["v"], red_mark[step])
                iv = s.activation(
                    out=pp_row[:, CW * c:CW * c + wdt],
                    in_=ten[:, off:off + wdt], func=AF.Exp,
                    bias=me.negmh[:, r * nb + c:r * nb + c + 1],
                    scale=1.0, accum_out=me.fsum[:, r * nb + c:r * nb + c + 1])
                assert me.mark(iv, "s") == s_mark[step]

        @blk.vector
        def _(v):
            sem, val = me.psS_rel[(N - 1) % 2]
            me.w(v, sem, val)
            v.drain()
            for r in range(R):
                v.tensor_reduce(out=me.negM[:, r:r + 1],
                                in_=me.negmh[:, r * nb:r * nb + nb],
                                axis=AX.X, op=ALU.min)
            v.drain()
            iv = None
            for r in range(R):
                iv = v.tensor_scalar(out=me.fac[:, r * nb:r * nb + nb],
                                     in0=me.negmh[:, r * nb:r * nb + nb],
                                     scalar1=me.negM[:, r:r + 1], scalar2=None,
                                     op0=ALU.subtract)
            me.mark(iv, "v")
        vdone = me.cnt["v"]

        @blk.scalar
        def _(s):
            me.w(s, "v", vdone)
            me.mark(s.activation(out=me.fac[:, 0:N], in_=me.fac[:, 0:N],
                                 func=AF.Exp, bias=0.0, scale=-1.0), "s")
        sdone = me.cnt["s"]

        rc_marks = []

        @blk.vector
        def _(v):
            me.w(v, "s", sdone)
            if normalize:
                v.tensor_tensor(out=me.ftmp[:, 0:N], in0=me.fac[:, 0:N],
                                in1=me.fsum[:, 0:N], op=ALU.mult)
                v.drain()
                for r in range(R):
                    v.tensor_reduce(out=me.rden[:, r:r + 1],
                                    in_=me.ftmp[:, r * nb:r * nb + nb],
                                    axis=AX.X, op=ALU.add)
                v.drain()
                v.reciprocal(out=me.rden[:, 0:R], in_=me.rden[:, 0:R])
                v.drain()
                for r in range(R):
                    v.tensor_scalar(out=me.fac[:, r * nb:r * nb + nb],
                                    in0=me.fac[:, r * nb:r * nb + nb],
                                    scalar1=me.rden[:, r:r + 1], scalar2=None,
                                    op0=ALU.mult)
            v.drain()
            for c in range(nb):
                wdt = min(CW, klen - CW * c)
                iv = None
                for r in range(R):
                    pp_row = rows[r][1]
                    iv = v.tensor_scalar(out=pp_row[:, CW * c:CW * c + wdt],
                                         in0=pp_row[:, CW * c:CW * c + wdt],
                                         scalar1=me.fac[:, r * nb + c:r * nb + c + 1],
                                         scalar2=None, op0=ALU.mult)
                rc_marks.append(me.mark(iv, "v"))
        return rc_marks

    def l0_attn(self):
        me, blk, d = self, self.block, self.d
        with contextlib.ExitStack() as outer:
            kT = self.sb("kT", [128, 8 * T], BF16, outer)

            with contextlib.ExitStack() as ph:
                wbig = self.sb("wbig", [128, 8 * 3072], BF16, ph)
                xlnT = self.sb("xlnT", [128, 8 * 512], BF16, ph)
                wtmp = self.sb("wtmp", [128, 2 * 3072], F32, ph)
                lnb = self.sb("lnb", [128, 2 * 1024], BF16, ph)
                evb = self.sb("evb", [128, 2 * 512], BF16, ph)

                for dc in range(8):
                    half = dc % 2

                    @blk.sync
                    def _(sy):
                        me.w(sy, "v")
                        me.mark(sy.dma_start(
                            out=wtmp[:, half * 3072:half * 3072 + 3072],
                            in_=d["W0"][dc * 128:dc * 128 + 128, :]), "ld")
                    lddone = me.cnt["ld"]

                    @blk.vector
                    def _(v):
                        me.w(v, "ld", lddone)
                        me.mark(v.tensor_copy(
                            out=wbig[:, dc * 3072:dc * 3072 + 3072],
                            in_=wtmp[:, half * 3072:half * 3072 + 3072]), "v")

                lm = {k: {} for k in ("bn", "rstd", "nb", "nrm", "trans",
                                      "evac")}
                for B in range(NB):
                    for ti in range(4):
                        tt = B * 4 + ti
                        half = tt % 2

                        @blk.sync
                        def _(sy):
                            if tt - 2 in lm["nrm"]:
                                sy.wait_ge(me.semh["s"], lm["nrm"][tt - 2])
                                sy.wait_ge(me.semh["v"], lm["bn"][tt - 2])
                            else:
                                me.w(sy, "s")
                            me.mark(sy.dma_start(
                                out=me.xa[:, half * 1024:half * 1024 + 1024],
                                in_=d["x"][tt * 128:tt * 128 + 128, :]),
                                "xa%d" % half)
                        lddone = me.cnt["xa%d" % half]
                        self.ln_tile_to_xlnT(
                            me.xa[:, half * 1024:half * 1024 + 1024], ti, lnb,
                            xlnT, tt=tt, lm=lm, xwait=("xa%d" % half, lddone))

                    for region in ("q", "k"):
                        base = 0 if region == "q" else 1024
                        for fq in range(8):
                            half = fq % 2

                            @blk.tensor
                            def _(t):
                                me.w(t, "v")
                                last = None
                                for dc in range(8):
                                    last = t.matmul(
                                        me.psS[:, half * 512:half * 512 + 512],
                                        wbig[:, dc * 3072 + base + fq * 128:
                                             dc * 3072 + base + fq * 128 + 128],
                                        xlnT[:, dc * 512:dc * 512 + 512],
                                        start=(dc == 0), stop=(dc == 7))
                                me.mark(last, "t")
                            tdone = me.cnt["t"]

                            @blk.vector
                            def _(v):
                                me.w(v, "t", tdone)
                                if region == "k":
                                    me.mark(v.tensor_copy(
                                        out=kT[:, fq * T + B * 512:
                                               fq * T + B * 512 + 512],
                                        in_=me.psS[:, half * 512:half * 512 + 512]),
                                        "v")
                                else:
                                    me.w(v, "ev%d" % half)
                                    me.mark(v.tensor_scalar(
                                        out=evb[:, half * 512:half * 512 + 512],
                                        in0=me.psS[:, half * 512:half * 512 + 512],
                                        scalar1=1.0 / 32.0, scalar2=None,
                                        op0=ALU.mult), "v")
                            vdone = me.cnt["v"]

                            if region == "q":
                                @blk.sync
                                def _(sy):
                                    me.w(sy, "v", vdone)
                                    me.mark(sy.dma_start(
                                        out=d["qT_h"][fq * 128:fq * 128 + 128,
                                                      B * 512:B * 512 + 512],
                                        in_=evb[:, half * 512:half * 512 + 512]),
                                        "ev%d" % half)

                    for ti in range(4):
                        tt = B * 4 + ti
                        for e in range(2):
                            @blk.tensor
                            def _(t):
                                me.w(t, "v")
                                last = None
                                for dc in range(8):
                                    last = t.matmul(
                                        me.psO[:, e * 512:e * 512 + 512],
                                        xlnT[:, dc * 512 + ti * 128:
                                             dc * 512 + ti * 128 + 128],
                                        wbig[:, dc * 3072 + 2048 + e * 512:
                                             dc * 3072 + 2048 + e * 512 + 512],
                                        start=(dc == 0), stop=(dc == 7))
                                me.mark(last, "t")
                            tdone = me.cnt["t"]

                            @blk.vector
                            def _(v):
                                me.w(v, "t", tdone)
                                me.w(v, "ev%d" % e)
                                me.mark(v.tensor_copy(
                                    out=evb[:, e * 512:e * 512 + 512],
                                    in_=me.psO[:, e * 512:e * 512 + 512]), "v")
                            vdone = me.cnt["v"]

                            @blk.sync
                            def _(sy):
                                me.w(sy, "v", vdone)
                                me.mark(sy.dma_start(
                                    out=d["v_h"][tt * 128:tt * 128 + 128,
                                                 e * 512:e * 512 + 512],
                                    in_=evb[:, e * 512:e * 512 + 512]), "ev%d" % e)
                self.barrier()

            with contextlib.ExitStack() as ph:
                vres = self.sb("vres", [128, NT * 1024], BF16, ph)
                pp = self.sb("pp", [128, T], BF16, ph)
                ppT = self.sb("ppT", [128, 2 * 512], BF16, ph)
                qt = self.sb("qt", [128, 2 * 1024], BF16, ph)

                for j in range(NT):
                    @blk.sync
                    def _(sy):
                        me.mark(sy.dma_start(
                            out=vres[:, j * 1024:j * 1024 + 1024],
                            in_=d["v_h"][j * 128:j * 128 + 128, :]), "lb")

                qt_ld = {}
                qS_last = {}

                def emit_qt_ld(i):
                    qhalf = i % 2

                    @blk.sync
                    def _(sy):
                        if i - 1 >= 0:
                            sy.wait_ge(me.semh["t"], qS_last[i - 1])
                        else:
                            me.w(sy, "t")
                        qt_ld[i] = me.mark(sy.dma_start(
                            out=qt[:, qhalf * 1024:qhalf * 1024 + 1024]
                            .rearrange("p (k c) -> p k c", k=8),
                            in_=d["qT_h"][:, i * 128:i * 128 + 128]
                            .rearrange("(k p) c -> p k c", k=8)),
                            "qt%d" % qhalf)

                emit_qt_ld(0)
                for i in range(NT):
                    qhalf = i % 2
                    k0 = 128 * i
                    klen = T - k0

                    qS_last[i] = me.cnt["t"] + (klen + 1023) // 1024
                    if i + 1 < NT:
                        emit_qt_ld(i + 1)

                    rc = self.flash_rows(
                        rows=[(lambda dc: qt[:, qhalf * 1024 + dc * 128:
                                             qhalf * 1024 + dc * 128 + 128],
                               pp[:, 0:klen])],
                        nchunk=8,
                        kT_ap=lambda dc, a, b: kT[:, dc * T + k0 + a:dc * T + k0 + b],
                        klen=klen, diag=("first", me.amask[:]), zerofix=True,
                        pre=[("qt%d" % qhalf, qt_ld[i])], normalize=False)

                    nk = klen // 128
                    ngrp = (nk + 3) // 4
                    t0 = me.cnt["t"]
                    v0 = me.cnt["v"]
                    t_items = []
                    for grp in range(ngrp):
                        t_items.append(("tr", grp))
                        if grp >= 1:
                            t_items.append(("av", grp - 1))
                    t_items.append(("av", ngrp - 1))
                    tr_mark, av_mark = {}, {}
                    for pos, (kind, g2) in enumerate(t_items):
                        (tr_mark if kind == "tr" else av_mark)[g2] = t0 + pos + 1
                    ev_mark = {grp: v0 + grp + 1 for grp in range(ngrp)}

                    @blk.tensor
                    def _(t):
                        me.w(t, "lb")
                        for kind, grp in t_items:
                            gw = min(4, nk - grp * 4)
                            ph2 = grp % 2
                            if kind == "tr":
                                t.wait_ge(me.semh["v"], rc[grp // 2])
                                if grp < 2:
                                    if me.psT_rel[ph2] is not None:
                                        sem, val = me.psT_rel[ph2]
                                        me.w(t, sem, val)
                                else:
                                    t.wait_ge(me.semh["v"], ev_mark[grp - 2])
                                last = None
                                for jj in range(gw):
                                    last = me.trans(
                                        t, me.psT[:, ph2 * 1024 + jj * 128:
                                                  ph2 * 1024 + jj * 128 + 128],
                                        pp[:, (grp * 4 + jj) * 128:
                                           (grp * 4 + jj) * 128 + 128],
                                        start=(jj == 0), stop=(jj == gw - 1))
                                assert me.mark(last, "t") == tr_mark[grp]
                            else:
                                t.wait_ge(me.semh["v"], ev_mark[grp])
                                if grp == 0 and me.psO_rel is not None:
                                    sem, val = me.psO_rel
                                    me.w(t, sem, val)
                                last = None
                                for jj in range(gw):
                                    j = grp * 4 + jj
                                    jabs = i + j
                                    for e in range(2):
                                        last = t.matmul(
                                            me.psO[:, e * 512:e * 512 + 512],
                                            ppT[:, ph2 * 512 + jj * 128:
                                                ph2 * 512 + jj * 128 + 128],
                                            vres[:, jabs * 1024 + e * 512:
                                                 jabs * 1024 + e * 512 + 512],
                                            start=(j == 0), stop=(j == nk - 1))
                                assert me.mark(last, "t") == av_mark[grp]

                    @blk.vector
                    def _(v):
                        for grp in range(ngrp):
                            gw = min(4, nk - grp * 4)
                            ph2 = grp % 2
                            v.wait_ge(me.semh["t"], tr_mark[grp])
                            if grp >= 2:
                                v.wait_ge(me.semh["t"], av_mark[grp - 2])
                            elif me.ppT_rel[ph2] is not None:
                                sem, val = me.ppT_rel[ph2]
                                me.w(v, sem, val)
                            iv = v.tensor_copy(
                                out=ppT[:, ph2 * 512:ph2 * 512 + gw * 128],
                                in_=me.psT[:, ph2 * 1024:ph2 * 1024 + gw * 128])
                            assert me.mark(iv, "v") == ev_mark[grp]
                    for g2 in range(max(0, ngrp - 2), ngrp):
                        me.psT_rel[g2 % 2] = ("v", ev_mark[g2])
                        me.ppT_rel[g2 % 2] = ("t", av_mark[g2])
                    tdone = me.cnt["t"]

                    @blk.vector
                    def _(v):
                        me.w(v, "t", tdone)
                        me.mark(v.nop(), "v")

                    self.postln_residual(me.psO[:, 0:1024], i, src="x", dst="xres")
                    me.psO_rel = ("v", me.last_postln_v)
                self.barrier()

    def l1_attn(self):
        me, blk, d = self, self.block, self.d
        with contextlib.ExitStack() as outer:
            qTb = self.sb("qTb", [128, 8 * T], BF16, outer)
            kl1 = self.sb("kl1", [128, 2 * T], BF16, outer)
            vl1 = self.sb("vl1", [128, 2 * T], BF16, outer)
            wol = self.sb("wol", [128, 8 * 1024], BF16, outer)

            with contextlib.ExitStack() as ph:
                wql = self.sb("wql", [128, 8 * 1024], BF16, ph)
                wkv = self.sb("wkv", [128, 8 * 128], BF16, ph)
                wu = self.sb("wu", [32, 256], BF16, ph)
                xlnT = self.sb("xlnT1", [128, 8 * 512], BF16, ph)
                wtmp = self.sb("wtmp1", [128, 2 * 1024], F32, ph)
                lnb = self.sb("lnb1", [128, 2 * 1024], BF16, ph)
                kst = [self.sb(f"kst{g}", [32, 512], BF16, ph) for g in range(2)]
                vst = [self.sb(f"vst{g}", [32, 512], BF16, ph) for g in range(2)]

                for name, dest, cols in (("wq1", wql, 1024), ("wo1", wol, 1024)):
                    for dc in range(8):
                        half = dc % 2

                        @blk.sync
                        def _(sy):
                            me.w(sy, "v")
                            me.mark(sy.dma_start(
                                out=wtmp[:, half * 1024:half * 1024 + cols],
                                in_=d[name][dc * 128:dc * 128 + 128, :]), "ld")
                        lddone = me.cnt["ld"]

                        @blk.vector
                        def _(v):
                            me.w(v, "ld", lddone)
                            me.mark(v.tensor_copy(
                                out=dest[:, dc * 1024:dc * 1024 + cols],
                                in_=wtmp[:, half * 1024:half * 1024 + cols]), "v")
                for dc in range(8):
                    half = dc % 2

                    @blk.sync
                    def _(sy):
                        me.w(sy, "v")
                        me.mark(sy.dma_start(
                            out=wtmp[:, half * 1024:half * 1024 + 64],
                            in_=d["wkd1"][dc * 128:dc * 128 + 128, :]), "ld")
                        me.mark(sy.dma_start(
                            out=wtmp[:, half * 1024 + 64:half * 1024 + 128],
                            in_=d["wvd1"][dc * 128:dc * 128 + 128, :]), "ld")
                    lddone = me.cnt["ld"]

                    @blk.vector
                    def _(v):
                        me.w(v, "ld", lddone)
                        me.mark(v.tensor_copy(
                            out=wkv[:, dc * 128:dc * 128 + 128],
                            in_=wtmp[:, half * 1024:half * 1024 + 128]), "v")

                @blk.sync
                def _(sy):
                    me.w(sy, "v")
                    me.mark(sy.dma_start(out=wtmp[0:32, 0:128], in_=d["wku1"][:]), "ld")
                    me.mark(sy.dma_start(out=wtmp[0:32, 128:256], in_=d["wvu1"][:]),
                            "ld")
                lddone = me.cnt["ld"]

                @blk.vector
                def _(v):
                    me.w(v, "ld", lddone)
                    me.mark(v.tensor_copy(out=wu[:], in_=wtmp[0:32, 0:256]), "v")

                scl = float(1.0 / np.sqrt(128.0))
                lm = {k: {} for k in ("bn", "rstd", "nb", "nrm", "trans",
                                      "evac")}
                for B in range(NB):
                    for ti in range(4):
                        tt = B * 4 + ti
                        half = tt % 2

                        @blk.sync
                        def _(sy):
                            if tt - 2 in lm["nrm"]:
                                sy.wait_ge(me.semh["s"], lm["nrm"][tt - 2])
                                sy.wait_ge(me.semh["v"], lm["bn"][tt - 2])
                            else:
                                me.w(sy, "s")
                            me.mark(sy.dma_start(
                                out=me.xa[:, half * 1024:half * 1024 + 1024],
                                in_=d["xres"][tt * 128:tt * 128 + 128, :]),
                                "xa%d" % half)
                        lddone = me.cnt["xa%d" % half]
                        self.ln_tile_to_xlnT(
                            me.xa[:, half * 1024:half * 1024 + 1024], ti, lnb,
                            xlnT, tt=tt, lm=lm, xwait=("xa%d" % half, lddone))

                    for h in range(8):
                        half = h % 2

                        @blk.tensor
                        def _(t):
                            me.w(t, "v")
                            last = None
                            for dc in range(8):
                                last = t.matmul(
                                    me.psS[:, half * 512:half * 512 + 512],
                                    wql[:, dc * 1024 + h * 128:
                                        dc * 1024 + h * 128 + 128],
                                    xlnT[:, dc * 512:dc * 512 + 512],
                                    start=(dc == 0), stop=(dc == 7))
                            me.mark(last, "t")
                        tdone = me.cnt["t"]

                        @blk.vector
                        def _(v):
                            me.w(v, "t", tdone)
                            me.mark(v.tensor_scalar(
                                out=qTb[:, h * T + B * 512:h * T + B * 512 + 512],
                                in0=me.psS[:, half * 512:half * 512 + 512],
                                scalar1=scl, scalar2=None, op0=ALU.mult), "v")

                    @blk.tensor
                    def _(t):
                        me.w(t, "v")
                        last = None
                        for dc in range(8):
                            last = t.matmul(me.psM[:, 0:512],
                                            wkv[:, dc * 128:dc * 128 + 128],
                                            xlnT[:, dc * 512:dc * 512 + 512],
                                            start=(dc == 0), stop=(dc == 7))
                        me.mark(last, "t")
                    tdone = me.cnt["t"]

                    @blk.vector
                    def _(v):
                        me.w(v, "t", tdone)
                        v.tensor_copy(out=kst[0][:], in_=me.psM[0:32, 0:512])
                        v.tensor_copy(out=kst[1][:], in_=me.psM[32:64, 0:512])
                        v.tensor_copy(out=vst[0][:], in_=me.psM[64:96, 0:512])
                        me.mark(v.tensor_copy(out=vst[1][:], in_=me.psM[96:128, 0:512]),
                                "v")
                    vdone = me.cnt["v"]

                    for g in range(2):
                        @blk.tensor
                        def _(t):
                            me.w(t, "v")
                            me.mark(t.matmul(me.psN[:, 0:512], wu[0:32, 0:128],
                                             kst[g][:], start=True, stop=True), "t")
                        tdone = me.cnt["t"]

                        @blk.vector
                        def _(v):
                            me.w(v, "t", tdone)
                            me.mark(v.tensor_copy(
                                out=kl1[:, g * T + B * 512:g * T + B * 512 + 512],
                                in_=me.psN[:, 0:512]), "v")

                        for ti in range(4):
                            @blk.tensor
                            def _(t):
                                me.w(t, "v")
                                me.mark(t.matmul(
                                    me.psN[:, 0:128],
                                    vst[g][:, ti * 128:ti * 128 + 128],
                                    wu[0:32, 128:256], start=True, stop=True), "t")
                            tdone = me.cnt["t"]
                            col = (g * NT + B * 4 + ti) * 128

                            @blk.vector
                            def _(v):
                                me.w(v, "t", tdone)
                                me.mark(v.tensor_copy(out=vl1[:, col:col + 128],
                                                      in_=me.psN[:, 0:128]), "v")
                if self.debug:
                    @blk.sync
                    def _(sy):
                        me.w(sy, "v")
                        me.mark(sy.dma_start(out=d["dbg_q"][:], in_=qTb[:]), "lb")
                        me.mark(sy.dma_start(out=d["dbg_k"][:], in_=kl1[:]), "lb")
                        me.mark(sy.dma_start(out=d["dbg_v"][:], in_=vl1[:]), "lb")
                self.barrier()

            with contextlib.ExitStack() as ph:
                pp = self.sb("pp4", [128, 4 * T], BF16, ph)
                ppT = self.sb("ppT1", [128, 2 * 512], BF16, ph)
                oT = self.sb("oT", [128, 1024], BF16, ph)

                for i in range(NT):
                    klen = 128 * (i + 1)
                    nk = i + 1
                    for g in range(2):
                        def qap_for(h):
                            return lambda dc: qTb[:, h * T + i * 128:
                                                  h * T + i * 128 + 128]
                        rc = self.flash_rows(
                            rows=[(qap_for(4 * g + r), pp[:, r * T:r * T + klen])
                                  for r in range(4)],
                            nchunk=1,
                            kT_ap=lambda dc, a, b: kl1[:, g * T + a:g * T + b],
                            klen=klen, diag=("last", me.cmask[:]), zerofix=False)

                        t0 = me.cnt["t"]
                        v0 = me.cnt["v"]
                        t_items = []
                        for j in range(nk):
                            t_items.append(("tr", j))
                            if j >= 1:
                                t_items.append(("av", j - 1))
                        t_items.append(("av", nk - 1))
                        tr_mark, av_mark = {}, {}
                        for pos, (kind, j2) in enumerate(t_items):
                            (tr_mark if kind == "tr" else av_mark)[j2] = t0 + pos + 1
                        ev_mark = {j: v0 + j + 1 for j in range(nk)}

                        @blk.tensor
                        def _(t):
                            for kind, j in t_items:
                                ph2 = j % 2
                                if kind == "tr":
                                    t.wait_ge(me.semh["v"], rc[j // 8])
                                    if j < 2:
                                        if me.psT_rel[ph2] is not None:
                                            sem, val = me.psT_rel[ph2]
                                            me.w(t, sem, val)
                                    else:
                                        t.wait_ge(me.semh["v"], ev_mark[j - 2])
                                    last = None
                                    for r in range(4):
                                        last = me.trans(
                                            t, me.psT[:, ph2 * 1024 + r * 128:
                                                      ph2 * 1024 + r * 128 + 128],
                                            pp[:, r * T + j * 128:
                                               r * T + j * 128 + 128],
                                            start=(r == 0), stop=(r == 3))
                                    assert me.mark(last, "t") == tr_mark[j]
                                else:
                                    t.wait_ge(me.semh["v"], ev_mark[j])
                                    if j == 0 and g == 0 and me.psO_rel is not None:
                                        sem, val = me.psO_rel
                                        me.w(t, sem, val)
                                    iv = t.matmul(
                                        me.psO[:, g * 512:g * 512 + 512],
                                        vl1[:, (g * NT + j) * 128:
                                            (g * NT + j) * 128 + 128],
                                        ppT[:, ph2 * 512:ph2 * 512 + 512],
                                        start=(j == 0), stop=(j == nk - 1))
                                    assert me.mark(iv, "t") == av_mark[j]

                        @blk.vector
                        def _(v):
                            for j in range(nk):
                                ph2 = j % 2
                                v.wait_ge(me.semh["t"], tr_mark[j])
                                if j >= 2:
                                    v.wait_ge(me.semh["t"], av_mark[j - 2])
                                elif me.ppT_rel[ph2] is not None:
                                    sem, val = me.ppT_rel[ph2]
                                    me.w(v, sem, val)
                                iv = v.tensor_copy(
                                    out=ppT[:, ph2 * 512:ph2 * 512 + 512],
                                    in_=me.psT[:, ph2 * 1024:ph2 * 1024 + 512])
                                assert me.mark(iv, "v") == ev_mark[j]
                        for j2 in range(max(0, nk - 2), nk):
                            me.psT_rel[j2 % 2] = ("v", ev_mark[j2])
                            me.ppT_rel[j2 % 2] = ("t", av_mark[j2])
                    tdone = me.cnt["t"]

                    @blk.vector
                    def _(v):
                        me.w(v, "t", tdone)
                        if self.debug:
                            me.w(v, "lb")
                        me.mark(v.tensor_copy(out=oT[:], in_=me.psO[:, 0:1024]), "v")
                    vdone = me.cnt["v"]
                    me.psO_rel = ("v", vdone)
                    if self.debug and i == 0:
                        @blk.sync
                        def _(sy):
                            me.w(sy, "v", vdone)
                            me.mark(sy.dma_start(out=d["dbg_o"][:], in_=oT[:]), "lb")

                    @blk.tensor
                    def _(t):
                        me.w(t, "v", vdone)
                        if me.psO_rel is not None:
                            sem, val = me.psO_rel
                            me.w(t, sem, val)
                        last = None
                        for e in range(2):
                            for h in range(8):
                                last = t.matmul(
                                    me.psO[:, e * 512:e * 512 + 512],
                                    oT[:, h * 128:h * 128 + 128],
                                    wol[:, h * 1024 + e * 512:
                                        h * 1024 + e * 512 + 512],
                                    start=(h == 0), stop=(h == 7))
                        me.mark(last, "t")
                    tdone = me.cnt["t"]

                    @blk.vector
                    def _(v):
                        me.w(v, "t", tdone)
                        me.mark(v.nop(), "v")

                    self.postln_residual(me.psO[:, 0:1024], i, src="xres", dst="xres")
                    me.psO_rel = ("v", me.last_postln_v)
                self.barrier()

    def ffn(self, L):
        me, blk, d = self, self.block, self.d
        w1name = "w1_0" if L == 0 else "w1_1"
        w2name = "w2_0" if L == 0 else "w2_1"
        dstname = "xres" if L == 0 else "out"

        with contextlib.ExitStack() as ph:
            wq1T = self.sb("wq1T%d" % L, [128, 8 * HID], BF16, ph)
            ha = self.sb("ha%d" % L, [128, 4096], F32, ph)
            hb = self.sb("hbuf%d" % L, [128, 4096], F32, ph)
            hq = self.sb("hq%d" % L, [128, 4096], BF16, ph)
            xq = self.sb("xq%d" % L, [128, 1024], BF16, ph)
            xqT = self.sb("xqT%d" % L, [128, 2 * 1024], BF16, ph)
            lna = self.sb("lna%d" % L, [128, 3 * 1024], F32, ph)
            lnx = self.sb("lnx%d" % L, [128, 1024], F32, ph)
            brep = self.sb("brep", [128, 4096], F32, ph) if L == 0 else None
            grep = self.sb("grep", [128, 1024], F32, ph) if L == 0 else None
            erep = self.sb("erep", [128, 1024], F32, ph) if L == 0 else None

            with contextlib.ExitStack() as wph:
                wtmp = self.sb("wtmp2%d" % L, [128, 3 * 1024], F32, wph)
                self.weight_quant_pass(w1name, which=1, dest=wq1T, wtmp=wtmp,
                                       ha=ha, hq=hq, dstv=me.swv)
                if L == 0:
                    self.replicate_vec("b1_0", brep, 4096)
                    self.replicate_vec("gam0", grep, 1024)
                    self.replicate_vec("bet0", erep, 1024)
                self.barrier()

            ffn1_rel = [None, None]
            carry = {"xa": [None, None], "trq_prev": None, "gel7_prev": None,
                     "hqst": None}
            plans = {}

            _plan = {"v": 0, "s": 0, "t": 0}

            def seg_marks(P, eng, items):
                base = max(me.cnt[eng], _plan[eng])
                mm = {k: base + i + 1 for i, k in enumerate(items)}
                _plan[eng] = base + len(items)
                key = {"v": "vm", "s": "sm", "t": "tm"}[eng]
                P.setdefault(key, {}).update(mm)
                return mm

            def buf(tt):
                p = tt % 2
                return dict(
                    lnap=lna[:, p * 1024:p * 1024 + 1024],
                    ln2p=lna[:, 2048:3072],
                    lnxp=lnx[:, 0:1024],
                    xqp=xq[:, 0:1024],
                    xqTp=xqT[:, p * 1024:p * 1024 + 1024],
                    st6p=me.fst6[:, p * 12:p * 12 + 12],
                    st2p=me.fst2[:, p * 2:p * 2 + 2],
                    rstdp=me.frstd[:, p:p + 1],
                    nbp=me.fnb[:, p:p + 1],
                    lntp=me.flnt[:, p:p + 1],
                    amp=me.fam[:, p:p + 1],
                    sc1p=me.fsact[:, p * 2:p * 2 + 1],
                    sc2p=me.fsact[:, p * 2 + 1:p * 2 + 2],
                    rscp=me.frsc[:, p:p + 1],
                    xap=me.xa[:, p * 1024:p * 1024 + 1024])

            def front(tt):
                P, B = plans[tt], buf(tt)
                p = tt % 2
                v_items = ["bn1", "nb1"]
                if L == 0:
                    v_items += ["bn2", "nb2", "gb"]
                v_items += ["qnt", "clp1", "evq"]
                seg_marks(P, "v", v_items)
                s_items = ["st1", "nrm1"]
                if L == 0:
                    s_items += ["st2", "nrm2"]
                s_items += ["t1", "rnd1"]
                seg_marks(P, "s", s_items)
                seg_marks(P, "t", ["trq"])
                vm, sm = P["vm"], P["sm"]
                lnfin = B["ln2p"] if L == 0 else B["lnap"]

                @blk.sync
                def _(sy):
                    if carry["xa"][p] is not None:
                        vmk, smk = carry["xa"][p]
                        sy.wait_ge(me.semh["v"], vmk)
                        sy.wait_ge(me.semh["s"], smk)
                    me.mark(sy.dma_start(
                        out=B["xap"],
                        in_=d["xres"][tt * 128:tt * 128 + 128, :]), "xa%d" % p)
                P["xadone"] = me.cnt["xa%d" % p]
                carry["xa"][p] = (vm["nb1"], sm["nrm1"])

                @blk.vector
                def _(v):
                    v.wait_ge(me.semh["xa%d" % p], P["xadone"])
                    v.drain()
                    v.bn_stats(out=B["st6p"][:, 0:6], in_=B["xap"][:, 0:512])
                    v.bn_stats(out=B["st6p"][:, 6:12], in_=B["xap"][:, 512:1024])
                    v.drain()
                    assert me.mark(v.bn_aggr(out=B["st2p"], in_=B["st6p"]), "v") == vm["bn1"]
                    v.wait_ge(me.semh["s"], sm["st1"])
                    iv = v.scalar_tensor_tensor(out=B["nbp"], in0=B["st2p"][:, 0:1],
                                                scalar=-1.0, in1=B["rstdp"],
                                                op0=ALU.mult, op1=ALU.mult)
                    assert me.mark(iv, "v") == vm["nb1"]
                    if L == 0:
                        v.wait_ge(me.semh["s"], sm["nrm1"])
                        v.drain()
                        v.bn_stats(out=B["st6p"][:, 0:6], in_=B["lnap"][:, 0:512])
                        v.bn_stats(out=B["st6p"][:, 6:12], in_=B["lnap"][:, 512:1024])
                        v.drain()
                        assert me.mark(v.bn_aggr(out=B["st2p"], in_=B["st6p"]), "v") == vm["bn2"]
                        v.wait_ge(me.semh["s"], sm["st2"])
                        iv = v.scalar_tensor_tensor(out=B["nbp"], in0=B["st2p"][:, 0:1],
                                                    scalar=-1.0, in1=B["rstdp"],
                                                    op0=ALU.mult, op1=ALU.mult)
                        assert me.mark(iv, "v") == vm["nb2"]
                        v.wait_ge(me.semh["s"], sm["nrm2"])
                        v.tensor_tensor(out=B["lnxp"], in0=B["lnxp"], in1=grep[:],
                                        op=ALU.mult)
                        v.drain()
                        iv = v.tensor_tensor(out=lnfin, in0=B["lnxp"], in1=erep[:],
                                             op=ALU.add)
                        assert me.mark(iv, "v") == vm["gb"]
                        v.drain()
                    else:
                        v.wait_ge(me.semh["s"], sm["nrm1"])
                    v.tensor_reduce(out=B["amp"], in_=lnfin, axis=AX.X, op=ALU.max,
                                    apply_absolute_value=True)
                    v.drain()
                    v.tensor_scalar(out=B["amp"], in0=B["amp"], scalar1=EPS,
                                    scalar2=None, op0=ALU.max)
                    v.drain()
                    v.reciprocal(out=B["sc1p"], in_=B["amp"])
                    v.drain()
                    v.tensor_scalar(out=B["sc1p"], in0=B["sc1p"], scalar1=127.0,
                                    scalar2=None, op0=ALU.mult)
                    v.scalar_tensor_tensor(out=B["rscp"], in0=B["amp"],
                                           scalar=1.0 / 127.0, in1=me.swv[:],
                                           op0=ALU.mult, op1=ALU.mult)
                    v.drain()
                    assert me.mark(v.nop(), "v") == vm["qnt"]
                    v.wait_ge(me.semh["s"], sm["rnd1"])
                    if carry["trq_prev"] is not None:
                        v.wait_ge(me.semh["t"], carry["trq_prev"])
                    iv = v.tensor_scalar(out=B["xqp"], in0=B["lnxp"], scalar1=-128.0,
                                         scalar2=127.0, op0=ALU.max, op1=ALU.min)
                    assert me.mark(iv, "v") == vm["clp1"]
                    v.wait_ge(me.semh["t"], P["tm"]["trq"])
                    if plans.get(tt - 2) is not None:
                        v.wait_ge(me.semh["t"], plans[tt - 2]["tm"]["mm7"])
                    assert me.mark(v.tensor_copy(out=B["xqTp"], in_=me.psT[:, 0:1024]),
                                   "v") == vm["evq"]

                @blk.scalar
                def _(s):
                    s.wait_ge(me.semh["v"], vm["bn1"])
                    s.activation(out=B["lntp"], in_=B["st2p"][:, 1:2], func=AF.Ln,
                                 bias=me.epsv[:], scale=1.0)
                    s.drain()
                    iv = s.activation(out=B["rstdp"], in_=B["lntp"], func=AF.Exp,
                                      bias=0.0, scale=-0.5)
                    assert me.mark(iv, "s") == sm["st1"]
                    s.wait_ge(me.semh["v"], vm["nb1"])
                    iv = s.activation(out=B["lnap"], in_=B["xap"], func=AF.Identity,
                                      bias=B["nbp"], scale=B["rstdp"])
                    assert me.mark(iv, "s") == sm["nrm1"]
                    if L == 0:
                        s.wait_ge(me.semh["v"], vm["bn2"])
                        s.activation(out=B["lntp"], in_=B["st2p"][:, 1:2], func=AF.Ln,
                                     bias=me.epsv[:], scale=1.0)
                        s.drain()
                        iv = s.activation(out=B["rstdp"], in_=B["lntp"], func=AF.Exp,
                                          bias=0.0, scale=-0.5)
                        assert me.mark(iv, "s") == sm["st2"]
                        s.wait_ge(me.semh["v"], vm["nb2"])
                        iv = s.activation(out=B["lnxp"], in_=B["lnap"],
                                          func=AF.Identity,
                                          bias=B["nbp"], scale=B["rstdp"])
                        assert me.mark(iv, "s") == sm["nrm2"]
                    s.wait_ge(me.semh["v"], vm["qnt"])
                    iv = s.activation(out=B["lnxp"], in_=lnfin, func=AF.Identity,
                                      bias=me.magic[:], scale=B["sc1p"])
                    assert me.mark(iv, "s") == sm["t1"]
                    s.drain()
                    iv = s.activation(out=B["lnxp"], in_=B["lnxp"], func=AF.Identity,
                                      bias=me.nmagic[:], scale=1.0)
                    assert me.mark(iv, "s") == sm["rnd1"]

                @blk.tensor
                def _(t):
                    t.wait_ge(me.semh["v"], vm["clp1"])
                    if plans.get(tt - 1) is not None:
                        t.wait_ge(me.semh["v"], plans[tt - 1]["vm"]["evq"])
                    last = None
                    for dc in range(8):
                        last = me.trans(t, me.psT[:, dc * 128:dc * 128 + 128],
                                        B["xqp"][:, dc * 128:dc * 128 + 128],
                                        start=(dc == 0), stop=(dc == 7))
                    assert me.mark(last, "t") == P["tm"]["trq"]
                carry["trq_prev"] = P["tm"]["trq"]

            def back_a(tt):
                P, B = plans[tt], buf(tt)
                seg_marks(P, "t", ["mm%d" % k for k in range(8)])
                va = []
                if L == 0:
                    va += ["epi%d" % k for k in range(8)]
                va += ["qn2"]
                seg_marks(P, "v", va)
                seg_marks(P, "s", ["gel%d" % k for k in range(8)] +
                          (["t2", "rnd2"] if L == 0 else ["t2"]))
                vm, sm, tm = P["vm"], P["sm"], P["tm"]

                @blk.tensor
                def _(t):
                    for k in range(8):
                        hf = k % 2
                        t.wait_ge(me.semh["v"], vm["evq"])
                        if ffn1_rel[hf] is not None:
                            sem, val = ffn1_rel[hf]
                            me.w(t, sem, val)
                        last = None
                        for dc in range(8):
                            last = t.matmul(
                                me.psS[:, hf * 512:hf * 512 + 512],
                                B["xqTp"][:, dc * 128:dc * 128 + 128],
                                wq1T[:, dc * HID + k * 512:dc * HID + k * 512 + 512],
                                start=(dc == 0), stop=(dc == 7))
                        assert me.mark(last, "t") == tm["mm%d" % k]
                        ffn1_rel[hf] = ("v", vm["epi%d" % k]) if L == 0 else \
                                       ("s", sm["gel%d" % k])

                @blk.vector
                def _(v):
                    if L == 0:
                        for k in range(8):
                            v.wait_ge(me.semh["t"], tm["mm%d" % k])
                            if k == 0 and carry["gel7_prev"] is not None:
                                v.wait_ge(me.semh["s"], carry["gel7_prev"])
                            iv = v.scalar_tensor_tensor(
                                out=ha[:, k * 512:k * 512 + 512],
                                in0=me.psS[:, (k % 2) * 512:(k % 2) * 512 + 512],
                                scalar=B["rscp"],
                                in1=brep[:, k * 512:k * 512 + 512],
                                op0=ALU.mult, op1=ALU.add)
                            assert me.mark(iv, "v") == vm["epi%d" % k]
                    v.wait_ge(me.semh["s"], sm["gel7"])
                    v.drain()
                    v.tensor_reduce(out=B["amp"], in_=hb[:], axis=AX.X, op=ALU.max,
                                    apply_absolute_value=True)
                    v.drain()
                    v.tensor_scalar(out=B["amp"], in0=B["amp"], scalar1=EPS,
                                    scalar2=None, op0=ALU.max)
                    v.drain()
                    v.reciprocal(out=B["sc2p"], in_=B["amp"])
                    v.drain()
                    v.tensor_scalar(out=B["sc2p"], in0=B["sc2p"], scalar1=127.0,
                                    scalar2=None, op0=ALU.mult)
                    v.tensor_scalar(out=me.rs2all[:, tt:tt + 1], in0=B["amp"],
                                    scalar1=1.0 / 127.0, scalar2=None, op0=ALU.mult)
                    v.drain()
                    assert me.mark(v.nop(), "v") == vm["qn2"]

                @blk.scalar
                def _(s):
                    for k in range(8):
                        if L == 0:
                            s.wait_ge(me.semh["v"], vm["epi%d" % k])
                            iv = s.activation(out=hb[:, k * 512:k * 512 + 512],
                                              in_=ha[:, k * 512:k * 512 + 512],
                                              func=AF.Gelu)
                        else:
                            s.wait_ge(me.semh["t"], tm["mm%d" % k])
                            if k == 0:
                                s.wait_ge(me.semh["v"], vm["qnt"])
                                if carry.get("hbfree") is not None:
                                    s.wait_ge(me.semh["v"], carry["hbfree"])
                            iv = s.activation(
                                out=hb[:, k * 512:k * 512 + 512],
                                in_=me.psS[:, (k % 2) * 512:(k % 2) * 512 + 512],
                                func=AF.Gelu, bias=0.0, scale=B["rscp"])
                        assert me.mark(iv, "s") == sm["gel%d" % k]
                    s.wait_ge(me.semh["v"], vm["qn2"])
                    iv = s.activation(out=hb[:], in_=hb[:], func=AF.Identity,
                                      bias=me.magic[:], scale=B["sc2p"])
                    assert me.mark(iv, "s") == sm["t2"]
                    if L == 0:
                        s.drain()
                        iv = s.activation(out=hb[:], in_=hb[:], func=AF.Identity,
                                          bias=me.nmagic[:], scale=1.0)
                        assert me.mark(iv, "s") == sm["rnd2"]
                carry["gel7_prev"] = sm["gel7"]

            def back_b(tt):
                P = plans[tt]
                seg_marks(P, "v", ["clp2"])
                vm, sm = P["vm"], P["sm"]

                @blk.vector
                def _(v):
                    if L == 0:
                        v.wait_ge(me.semh["s"], sm["rnd2"])
                    else:
                        v.wait_ge(me.semh["s"], sm["t2"])
                    if carry.get("hqst") is not None:
                        v.wait_ge(me.semh["ev0"], carry["hqst"])
                    if L == 0:
                        iv = v.tensor_scalar(out=hq[:], in0=hb[:], scalar1=-128.0,
                                             scalar2=127.0, op0=ALU.max,
                                             op1=ALU.min)
                    else:
                        v.drain()
                        v.tensor_scalar(out=hb[:], in0=hb[:],
                                        scalar1=me.magic[:], scalar2=-128.0,
                                        op0=ALU.subtract, op1=ALU.max)
                        v.drain()
                        iv = v.tensor_scalar(out=hq[:], in0=hb[:], scalar1=127.0,
                                             scalar2=None, op0=ALU.min)
                    assert me.mark(iv, "v") == vm["clp2"]
                carry["hbfree"] = vm["clp2"]

                @blk.sync
                def _(sy):
                    sy.wait_ge(me.semh["v"], vm["clp2"])
                    carry["hqst"] = me.mark(sy.dma_start(
                        out=d["h_q"][tt * 128:tt * 128 + 128, :],
                        in_=hq[:]), "ev0")

            plans[0] = {}
            front(0)
            for k in range(NT + 1):
                if k + 1 < NT:
                    plans[k + 1] = {}
                    front(k + 1)
                if k >= 1:
                    back_b(k - 1)
                if k < NT:
                    back_a(k)
            self.barrier()

        with contextlib.ExitStack() as ph:
            wq2T = self.sb("wq2T%d" % L, [128, 32 * 1024], BF16, ph)
            hqs = self.sb("hqs%d" % L, [128, 2 * 4096], BF16, ph)
            ha = self.sb("ha2%d" % L, [128, 4096], F32, ph)
            hq = self.sb("hq2%d" % L, [128, 4096], BF16, ph)
            brep2 = self.sb("brep2", [128, 1024], F32, ph) if L == 0 else None

            with contextlib.ExitStack() as wph:
                wtmp = self.sb("wtmp3%d" % L, [128, 2 * 4096], F32, wph)
                self.weight_quant_pass(w2name, which=2, dest=wq2T, wtmp=wtmp,
                                       ha=ha, hq=hq, dstv=me.swv2)
                if L == 0:
                    self.replicate_vec("b2_0", brep2, 1024)
                self.barrier()
            hqT = self.sb("hqT%d" % L, [128, 2 * 4096], BF16, ph)

            psO_rel = None
            trm = {}
            evm = {}
            mmk = {}
            ldm = {}

            def emit_ld(k):
                kp = k % 2

                @blk.sync
                def _(sy):
                    if k - 2 >= 0:
                        sy.wait_ge(me.semh["t"], trm[k - 2][3])
                    ldm[k] = me.mark(sy.dma_start(
                        out=hqs[:, kp * 4096:kp * 4096 + 4096],
                        in_=d["h_q"][k * 128:k * 128 + 128, :]), "hq%d" % kp)

            emit_ld(0)
            for tt in range(NT):
                p = tt % 2
                if tt + 1 < NT:
                    emit_ld(tt + 1)
                evm[tt] = [me.cnt["v"] + b + 1 for b in range(4)]

                @blk.tensor
                def _(t):
                    t.wait_ge(me.semh["hq%d" % p], ldm[tt])
                    marks = []
                    for b in range(4):
                        hf = b % 2
                        if b >= 2:
                            t.wait_ge(me.semh["v"], evm[tt][b - 2])
                        elif tt - 1 >= 0:
                            t.wait_ge(me.semh["v"], evm[tt - 1][b + 2])
                        last = None
                        for k in range(8):
                            hc = b * 8 + k
                            last = me.trans(t, me.psT[:, hf * 1024 + k * 128:
                                                   hf * 1024 + k * 128 + 128],
                                            hqs[:, p * 4096 + hc * 128:
                                                p * 4096 + hc * 128 + 128],
                                            start=(k == 0), stop=(k == 7))
                        marks.append(me.mark(last, "t"))
                    trm[tt] = marks

                @blk.vector
                def _(v):
                    for b in range(4):
                        hf = b % 2
                        v.wait_ge(me.semh["t"], trm[tt][b])
                        if tt - 2 >= 0 and b == 0:
                            v.wait_ge(me.semh["t"], mmk[tt - 2])
                        iv = v.tensor_copy(
                            out=hqT[:, p * 4096 + b * 1024:
                                    p * 4096 + b * 1024 + 1024],
                            in_=me.psT[:, hf * 1024:hf * 1024 + 1024])
                        assert me.mark(iv, "v") == evm[tt][b]

                @blk.tensor
                def _(t):
                    t.wait_ge(me.semh["v"], evm[tt][3])
                    if psO_rel is not None:
                        for sem, val in psO_rel:
                            me.w(t, sem, val)
                    last = None
                    for e in range(2):
                        for hc in range(32):
                            last = t.matmul(
                                me.psO[:, e * 512:e * 512 + 512],
                                hqT[:, p * 4096 + hc * 128:p * 4096 + hc * 128 + 128],
                                wq2T[:, hc * 1024 + e * 512:hc * 1024 + e * 512 + 512],
                                start=(hc == 0), stop=(hc == 31))
                    mmk[tt] = me.mark(last, "t")
                tdone = mmk[tt]

                @blk.vector
                def _(v):
                    me.w(v, "t", tdone)
                    if me.last_postln_free is not None:
                        for sem, val in me.last_postln_free:
                            me.w(v, sem, val)
                    v.drain()
                    v.tensor_tensor(out=me.rsc1[:], in0=me.rs2all[:, tt:tt + 1],
                                    in1=me.swv2[:], op=ALU.mult)
                    v.drain()
                    if L == 0:
                        iv = v.scalar_tensor_tensor(
                            out=ha[:, 0:1024], in0=me.psO[:, 0:1024],
                            scalar=me.rsc1[:], in1=brep2[:],
                            op0=ALU.mult, op1=ALU.add)
                    else:
                        iv = v.tensor_scalar(
                            out=ha[:, 0:1024], in0=me.psO[:, 0:1024],
                            scalar1=me.rsc1[:], scalar2=None,
                            op0=ALU.mult)
                    me.mark(iv, "v")
                epidone = me.cnt["v"]

                self.postln_residual(ha[:, 0:1024], tt, src="xres", dst=dstname)
                psO_rel = [("v", epidone)]
            self.barrier()

    def replicate_vec(self, name, dest, n):
        me, blk, d = self, self.block, self.d

        @blk.sync
        def _(sy):
            me.w(sy, "v")
            me.mark(sy.dma_start(out=me.vec1[0:1, 0:n],
                                 in_=d[name].rearrange("(o n) -> o n", o=1)), "lb")
        lddone = me.cnt["lb"]
        for b in range((n + 511) // 512):
            wdt = min(512, n - b * 512)

            @blk.tensor
            def _(t):
                me.w(t, "lb", lddone)
                me.w(t, "v")
                me.mark(t.matmul(me.psM[:, 0:wdt], me.ones1[0:1, 0:128],
                                 me.vec1[0:1, b * 512:b * 512 + wdt],
                                 start=True, stop=True), "t")
            tdone = me.cnt["t"]

            @blk.vector
            def _(v):
                me.w(v, "t", tdone)
                me.mark(v.tensor_copy(out=dest[:, b * 512:b * 512 + wdt],
                                      in_=me.psM[:, 0:wdt]), "v")

    def weight_quant_pass(self, name, which, dest, wtmp, ha, hq, dstv):
        me, blk, d = self, self.block, self.d
        w = d[name]
        rows, cols = (HID, D) if which == 1 else (D, HID)
        ntile = rows // 128
        nslot = wtmp.shape[1] // cols

        v0 = me.cnt["v"]

        def lsem(it):
            return ("ld", "lb", "lc")[it % 3]
        ldmA = {}

        @blk.sync
        def _(sy):
            me.w(sy, "v")
            for it in range(ntile):
                if it >= nslot:
                    sy.wait_ge(me.semh["v"], v0 + (it - nslot) + 1)
                sl = (it % nslot) * cols
                ldmA[it] = me.mark(sy.dma_start(out=wtmp[:, sl:sl + cols],
                                                in_=w[it * 128:it * 128 + 128, :]),
                                   lsem(it))

        @blk.vector
        def _(v):
            for it in range(ntile):
                v.wait_ge(me.semh[lsem(it)], ldmA[it])
                sl = (it % nslot) * cols
                v.drain()
                v.tensor_reduce(out=me.am[:], in_=wtmp[:, sl:sl + cols],
                                axis=AX.X, op=ALU.add, apply_absolute_value=True)
                v.drain()
                if it == 0:
                    iv = v.tensor_copy(out=me.acc1[:], in_=me.am[:])
                else:
                    iv = v.tensor_tensor(out=me.acc1[:], in0=me.acc1[:],
                                         in1=me.am[:], op=ALU.add)
                v.drain()
                assert me.mark(iv, "v") == v0 + it + 1
        vdone = me.cnt["v"]

        @blk.tensor
        def _(t):
            me.w(t, "v", vdone)
            me.w(t, "s")
            me.mark(t.matmul(me.psM[0:1, 0:1], me.onesc[:], me.acc1[:],
                             start=True, stop=True), "t")
        tdone = me.cnt["t"]

        @blk.vector
        def _(v):
            me.w(v, "t", tdone)
            me.mark(v.tensor_scalar(out=me.sc11[:], in0=me.psM[0:1, 0:1],
                                    scalar1=1.0 / (rows * cols), scalar2=EPS,
                                    op0=ALU.mult, op1=ALU.max), "v")
        vdone = me.cnt["v"]

        @blk.tensor
        def _(t):
            me.w(t, "v", vdone)
            me.mark(t.matmul(me.psM[:, 0:1], me.ones1[0:1, 0:128],
                             me.sc11[0:1, 0:1], start=True, stop=True), "t")
        tdone = me.cnt["t"]

        @blk.vector
        def _(v):
            me.w(v, "t", tdone)
            v.tensor_copy(out=dstv[:], in_=me.psM[:, 0:1])
            v.drain()
            v.reciprocal(out=me.am[:], in_=dstv[:])
            v.drain()
            me.mark(v.nop(), "v")
        recipdone = me.cnt["v"]

        hslot = min(ha.shape[1] // cols, 4)
        qslot = min(hq.shape[1] // cols, 4)
        g = max(1, cols // 1024)
        s0 = me.cnt["s"]
        v0b = me.cnt["v"]
        t0 = me.cnt["t"]
        ldmB = {}

        def sm_t1(it):
            return s0 + 2 * it + 1

        def sm_rnd(it):
            return s0 + 2 * it + 2

        def vm_clip(it):
            return v0b + it * (1 + g) + 1

        def vm_evac(it, bb):
            return v0b + it * (1 + g) + 2 + bb

        def tm(it, bb):
            return t0 + it * g + bb + 1

        @blk.sync
        def _(sy):
            me.w(sy, "v")
            for it in range(ntile):
                if it >= nslot:
                    sy.wait_ge(me.semh["s"], sm_t1(it - nslot))
                sl = (it % nslot) * cols
                ldmB[it] = me.mark(sy.dma_start(out=wtmp[:, sl:sl + cols],
                                                in_=w[it * 128:it * 128 + 128, :]),
                                   lsem(it))

        @blk.scalar
        def _(s):
            s.wait_ge(me.semh["v"], recipdone)
            for it in range(ntile):
                s.wait_ge(me.semh[lsem(it)], ldmB[it])
                if it >= hslot:
                    s.wait_ge(me.semh["v"], vm_clip(it - hslot))
                sl = (it % nslot) * cols
                hs = (it % hslot) * cols
                iv = s.activation(out=ha[:, hs:hs + cols],
                                  in_=wtmp[:, sl:sl + cols], func=AF.Identity,
                                  bias=me.magic[:], scale=me.am[:])
                assert me.mark(iv, "s") == sm_t1(it)
                s.drain()
                iv = s.activation(out=ha[:, hs:hs + cols],
                                  in_=ha[:, hs:hs + cols], func=AF.Identity,
                                  bias=me.nmagic[:], scale=1.0)
                assert me.mark(iv, "s") == sm_rnd(it)
                s.drain()

        @blk.vector
        def _(v):
            for it in range(ntile):
                v.wait_ge(me.semh["s"], sm_rnd(it))
                if it >= qslot:
                    v.wait_ge(me.semh["t"], tm(it - qslot, g - 1))
                hs = (it % hslot) * cols
                qs = (it % qslot) * cols
                v.drain()
                iv = v.tensor_scalar(out=hq[:, qs:qs + cols],
                                     in0=ha[:, hs:hs + cols],
                                     scalar1=-1.0, scalar2=1.0,
                                     op0=ALU.max, op1=ALU.min)
                assert me.mark(iv, "v") == vm_clip(it)
                for bb in range(g):
                    v.wait_ge(me.semh["t"], tm(it, bb))
                    gi = it * g + bb
                    ph2 = gi % 2
                    v.drain()
                    iv = None
                    for k in range(8):
                        cc = bb * 8 + k
                        if which == 1:
                            o = dest[:, cc * HID + it * 128:
                                     cc * HID + it * 128 + 128]
                        else:
                            o = dest[:, cc * 1024 + it * 128:
                                     cc * 1024 + it * 128 + 128]
                        iv = v.tensor_copy(
                            out=o, in_=me.psT[:, ph2 * 1024 + k * 128:
                                              ph2 * 1024 + k * 128 + 128])
                    assert me.mark(iv, "v") == vm_evac(it, bb)

        @blk.tensor
        def _(t):
            for it in range(ntile):
                t.wait_ge(me.semh["v"], vm_clip(it))
                qs = (it % qslot) * cols
                for bb in range(g):
                    gi = it * g + bb
                    ph2 = gi % 2
                    if gi >= 2:
                        pit, pbb = divmod(gi - 2, g)
                        t.wait_ge(me.semh["v"], vm_evac(pit, pbb))
                    last = None
                    for k in range(8):
                        cc = bb * 8 + k
                        last = me.trans(t, me.psT[:, ph2 * 1024 + k * 128:
                                               ph2 * 1024 + k * 128 + 128],
                                        hq[:, qs + cc * 128:qs + cc * 128 + 128],
                                        start=(k == 0), stop=(k == 7))
                    assert me.mark(last, "t") == tm(it, bb)

    def finish(self):
        me, blk = self, self.block
        snap = dict(me.cnt)

        @blk.sync
        def _(sy):
            for s in me.semh:
                sy.wait_ge(me.semh[s], snap[s])


def build_program(phases=("l0a", "l0f", "l1a", "l1f"), debug=False):
    return MK(phases, debug).build()


def _in_maps(kw):
    x = np.ascontiguousarray(np.asarray(kw["L0_attn_x"], np.float32))
    names = {
        "W0": "L0_attn_W", "w1_0": "L0_ffn_w1", "b1_0": "L0_ffn_b1",
        "w2_0": "L0_ffn_w2", "b2_0": "L0_ffn_b2", "gam0": "L0_ffn_gamma",
        "bet0": "L0_ffn_beta", "wq1": "L1_attn_wq", "wkd1": "L1_attn_wk_down",
        "wvd1": "L1_attn_wv_down", "wku1": "L1_attn_wk_up", "wvu1": "L1_attn_wv_up",
        "wo1": "L1_attn_wo", "w1_1": "L1_ffn_w1", "w2_1": "L1_ffn_w2",
    }
    weights = {k: np.ascontiguousarray(np.asarray(kw[v], np.float32))
               for k, v in names.items()}
    return [dict(weights, x=x[b]) for b in range(8)]


def kernel(**kw):
    nc = build_program()
    res = run_bass_kernel_spmd(nc, _in_maps(kw), core_ids=list(range(8)))
    return np.stack([np.asarray(res.results[b]["out"], np.float32)
                     for b in range(8)], axis=0)


# version 25
# speedup vs baseline: 1.0228x; 1.0228x over previous
import contextlib
import numpy as np
import concourse.bass as bass
import concourse.mybir as mybir
from concourse.bass_utils import run_bass_kernel_spmd

F32 = mybir.dt.float32
BF16 = mybir.dt.bfloat16
AF = mybir.ActivationFunctionType
ALU = mybir.AluOpType
AX = mybir.AxisListType

T = 4096
D = 1024
HID = 4096
NT = T // 128
NB = T // 512
EPS = 1e-5
NEG = -1.0e30


def set_dims(t):
    global T, NT, NB
    T, NT, NB = t, t // 128, t // 512


class MK:
    def __init__(self, phases=("l0a", "l0f", "l1a", "l1f"), debug=False):
        self.phases = phases
        self.debug = debug
        nc = bass.Bass("TRN2", target_bir_lowering=False)
        self.nc = nc
        self.cnt = {}
        self._stack = contextlib.ExitStack()
        self.last_postln_v = None
        self.last_postln_free = None
        self.psS_rel = [None, None]
        self.psO_rel = None
        self.psT_rel = [None, None]
        self.ppT_rel = [None, None]

        d = {}
        ei, eo = "ExternalInput", "ExternalOutput"
        d["x"] = nc.dram_tensor("x", [T, D], F32, kind=ei).ap()
        d["W0"] = nc.dram_tensor("W0", [D, 3 * D], F32, kind=ei).ap()
        d["w1_0"] = nc.dram_tensor("w1_0", [HID, D], F32, kind=ei).ap()
        d["b1_0"] = nc.dram_tensor("b1_0", [HID], F32, kind=ei).ap()
        d["w2_0"] = nc.dram_tensor("w2_0", [D, HID], F32, kind=ei).ap()
        d["b2_0"] = nc.dram_tensor("b2_0", [D], F32, kind=ei).ap()
        d["gam0"] = nc.dram_tensor("gam0", [D], F32, kind=ei).ap()
        d["bet0"] = nc.dram_tensor("bet0", [D], F32, kind=ei).ap()
        d["wq1"] = nc.dram_tensor("wq1", [D, D], F32, kind=ei).ap()
        d["wkd1"] = nc.dram_tensor("wkd1", [D, 64], F32, kind=ei).ap()
        d["wvd1"] = nc.dram_tensor("wvd1", [D, 64], F32, kind=ei).ap()
        d["wku1"] = nc.dram_tensor("wku1", [32, 128], F32, kind=ei).ap()
        d["wvu1"] = nc.dram_tensor("wvu1", [32, 128], F32, kind=ei).ap()
        d["wo1"] = nc.dram_tensor("wo1", [D, D], F32, kind=ei).ap()
        d["w1_1"] = nc.dram_tensor("w1_1", [HID, D], F32, kind=ei).ap()
        d["w2_1"] = nc.dram_tensor("w2_1", [D, HID], F32, kind=ei).ap()
        d["out"] = nc.dram_tensor("out", [T, D], F32, kind=eo).ap()
        d["xres"] = nc.dram_tensor("xres", [T, D], F32,
                                   kind=eo if debug else "Internal").ap()
        d["qT_h"] = nc.dram_tensor("qT_h", [D, T], BF16).ap()
        d["v_h"] = nc.dram_tensor("v_h", [T, D], BF16).ap()
        d["h_q"] = nc.dram_tensor("h_q", [T, HID], BF16).ap()
        if debug:
            d["dbg_q"] = nc.dram_tensor("dbg_q", [128, 8 * T], BF16, kind="ExternalOutput").ap()
            d["dbg_k"] = nc.dram_tensor("dbg_k", [128, 2 * T], BF16, kind="ExternalOutput").ap()
            d["dbg_v"] = nc.dram_tensor("dbg_v", [128, 2 * T], BF16, kind="ExternalOutput").ap()
            d["dbg_o"] = nc.dram_tensor("dbg_o", [128, 1024], BF16, kind="ExternalOutput").ap()
        self.d = d

        def sb(name, shape, dt, stack=None):
            return (stack or self._stack).enter_context(nc.sbuf_tensor(name, shape, dt))
        self.sb = sb

        def ps(name, shape, dt):
            return self._stack.enter_context(nc.psum_tensor(name, shape, dt))

        self.identb = sb("identb", [128, 128], BF16)
        self.cmask = sb("cmask", [128, 128], F32)
        self.amask = sb("amask", [128, 128], F32)
        self.ones1 = sb("ones1", [1, 512], F32)
        self.onesc = sb("onesc", [128, 1], F32)
        self.magic = sb("magic", [128, 1], F32)
        self.nmagic = sb("nmagic", [128, 1], F32)
        self.epsv = sb("epsv", [128, 1], F32)
        self.lnt = sb("lnt", [128, 1], F32)
        self.c127 = sb("c127", [128, 1], F32)
        self.st6 = sb("st6", [128, 12], F32)
        self.fst6 = sb("fst6", [128, 24], F32)
        self.fst2 = sb("fst2", [128, 4], F32)
        self.frstd = sb("frstd", [128, 2], F32)
        self.fnb = sb("fnb", [128, 2], F32)
        self.flnt = sb("flnt", [128, 2], F32)
        self.fam = sb("fam", [128, 2], F32)
        self.fsact = sb("fsact", [128, 4], F32)
        self.frsc = sb("frsc", [128, 2], F32)
        self.st2 = sb("st2", [128, 2], F32)
        self.rstd = sb("rstd", [128, 1], F32)
        self.nbias = sb("nbias", [128, 1], F32)
        self.lnsets = [
            (self.st6, self.st2, self.lnt, self.rstd, self.nbias),
            (sb("st6b", [128, 12], F32), sb("st2b", [128, 2], F32),
             sb("lntb", [128, 1], F32), sb("rstdb", [128, 1], F32),
             sb("nbiasb", [128, 1], F32)),
        ]
        self.negmh = sb("negmh", [128, 32], F32)
        self.negM = sb("negM", [128, 4], F32)
        self.fac = sb("fac", [128, 32], F32)
        self.fsum = sb("fsum", [128, 32], F32)
        self.ftmp = sb("ftmp", [128, 32], F32)
        self.rden = sb("rden", [128, 4], F32)
        self.rden8 = sb("rden8", [128, 8], F32)
        self.fam8 = sb("fam8", [128, 8], F32)
        self.am = sb("am", [128, 1], F32)
        self.sact = sb("sact", [128, 1], F32)
        self.rsc1 = sb("rsc1", [128, 1], F32)
        self.rs2all = sb("rs2all", [128, 32], F32)
        self.swv = sb("swv", [128, 1], F32)
        self.swv2 = sb("swv2", [128, 1], F32)
        self.acc1 = sb("acc1", [128, 1], F32)
        self.sc11 = sb("sc11", [1, 1], F32)
        self.vec1 = sb("vec1", [1, 4096], F32)
        self.xa = sb("xa", [128, 2 * 1024], F32)
        self.xst = sb("xst", [128, 2 * 1024], F32)
        self.lnst = sb("lnst", [128, 2 * 1024], F32)

        self.psS = ps("psS", [128, 2048], F32)
        self.psO = ps("psO", [128, 1024], F32)
        self.psT = ps("psT", [128, 2048], BF16)

        class _ColOff:
            def __init__(self, base, off):
                self.base, self.off = base, off

            def __getitem__(self, idx):
                p, c = idx
                return self.base[p, c.start + self.off:c.stop + self.off]

        self.psM = _ColOff(self.psS, 1024)
        self.psN = _ColOff(self.psS, 1536)

        self.semh = {}
        for s in ("ld", "st", "v", "s", "t", "g", "b", "xa0", "xa1", "xs0", "xs1",
                  "qt0", "qt1", "hq0", "hq1", "ev0", "ev1", "sx0", "sx1", "lb",
                  "lc"):
            self.semh[s] = self._stack.enter_context(nc.semaphore(s))
            self.cnt[s] = 0

        self.block = self._stack.enter_context(nc.Block())

    def mark(self, instr, s, n=None):
        if n is None:
            n = 1 if s in ("v", "s", "t", "g", "b") else 16
        self.cnt[s] += n
        instr.then_inc(self.semh[s], n)
        return self.cnt[s]

    def w(self, eng, s, v=None):
        eng.wait_ge(self.semh[s], self.cnt[s] if v is None else v)

    def barrier(self):
        blk, me = self.block, self

        @blk.sync
        def _(e):
            me.mark(e.nop(), "b")
        snap = dict(me.cnt)
        for sec in (blk.sync, blk.vector, blk.scalar, blk.tensor, blk.gpsimd):
            @sec
            def _(e):
                for s in me.semh:
                    e.wait_ge(me.semh[s], snap[s])
        self.psS_rel = [None, None]
        self.psO_rel = None
        self.psT_rel = [None, None]
        self.ppT_rel = [None, None]
        self.last_postln_v = None

    def trans(self, te, out_ap, in_ap, start, stop):
        return te.matmul(out_ap, in_ap, self.identb[:], is_transpose=True,
                         start=start, stop=stop, skip_group_check=True)

    def ln_stats(self, x_ap):
        me, blk = self, self.block

        @blk.vector
        def _(v):
            v.drain()
            v.bn_stats(out=me.st6[:, 0:6], in_=x_ap[:, 0:512])
            v.bn_stats(out=me.st6[:, 6:12], in_=x_ap[:, 512:1024])
            v.drain()
            me.mark(v.bn_aggr(out=me.st2[:], in_=me.st6[:]), "v")
        vdone = me.cnt["v"]

        @blk.scalar
        def _(s):
            me.w(s, "v", vdone)
            s.activation(out=me.lnt[:], in_=me.st2[:, 1:2], func=AF.Ln,
                         bias=me.epsv[:], scale=1.0)
            s.drain()
            me.mark(s.activation(out=me.rstd[:], in_=me.lnt[:], func=AF.Exp,
                                 bias=0.0, scale=-0.5), "s")
        sdone = me.cnt["s"]

        @blk.vector
        def _(v):
            me.w(v, "s", sdone)
            me.mark(v.scalar_tensor_tensor(out=me.nbias[:], in0=me.st2[:, 0:1],
                                           scalar=-1.0, in1=me.rstd[:],
                                           op0=ALU.mult, op1=ALU.mult), "v")
        return me.cnt["v"]

    def round_(self, v, ap):
        v.tensor_scalar(out=ap, in0=ap, scalar1=self.magic[:], scalar2=None,
                        op0=ALU.add)
        v.drain()
        v.tensor_scalar(out=ap, in0=ap, scalar1=self.magic[:], scalar2=None,
                        op0=ALU.subtract)
        v.drain()

    def build(self):
        self.setup_consts()
        if "l0a" in self.phases:
            self.l0_attn()
        if "l0f" in self.phases:
            self.ffn(0)
        if "l1a" in self.phases:
            self.l1_attn()
        if "l1f" in self.phases:
            self.ffn(1)
        self.finish()
        self._stack.close()
        return self.nc

    def setup_consts(self):
        me, blk = self, self.block

        @blk.vector
        def _(v):
            v.memset(me.identb[:], 0.0)
            v.memset(me.cmask[:], 0.0)
            v.memset(me.amask[:], 0.0)
            v.memset(me.ones1[:], 1.0)
            v.memset(me.onesc[:], 1.0)
            v.memset(me.c127[:], 127.0)
            v.memset(me.epsv[:], EPS)
            me.mark(v.memset(me.magic[:], 16777216.0), "v")
        vdone = me.cnt["v"]

        @blk.gpsimd
        def _(g):
            me.w(g, "v", vdone)
            g.affine_select(out=me.identb[:], in_=me.identb[:], pattern=[[-1, 128]],
                            compare_op=ALU.not_equal, fill=1.0, channel_multiplier=1)
            g.affine_select(out=me.cmask[:], in_=me.cmask[:], pattern=[[-1, 128]],
                            compare_op=ALU.is_ge, fill=NEG, channel_multiplier=1)
            me.mark(g.affine_select(out=me.amask[:], in_=me.amask[:],
                                    pattern=[[1, 128]], compare_op=ALU.is_ge,
                                    fill=NEG, channel_multiplier=-1), "g")
        gdone = me.cnt["g"]

        @blk.vector
        def _(v):
            me.w(v, "g", gdone)
            v.tensor_scalar(out=me.magic[:], in0=me.magic[:], scalar1=1.0,
                            scalar2=16777216.0, op0=ALU.add, op1=ALU.subtract)
            v.drain()
            v.tensor_scalar(out=me.magic[:], in0=me.magic[:],
                            scalar1=(1.5 * 2.0**52 - 1.5 * 2.0**23),
                            scalar2=1.5 * 2.0**23,
                            op0=ALU.mult, op1=ALU.add)
            v.drain()
            me.mark(v.tensor_scalar(out=me.nmagic[:], in0=me.magic[:],
                                    scalar1=-1.0, scalar2=None, op0=ALU.mult), "v")
        self.barrier()

    def ln_tile_to_xlnT(self, x_ap, ti_in_block, lnb, xlnT, tt=0, lm=None,
                        xwait=None):
        me, blk = self, self.block
        if lm is None:
            lm = {k: {} for k in ("bn", "rstd", "nb", "nrm", "trans", "evac")}
        p2 = tt % 2
        st6, st2, lnt, rstd, nbias = me.lnsets[p2]

        @blk.vector
        def _(v):
            if xwait is not None:
                v.wait_ge(me.semh[xwait[0]], xwait[1])
            if tt - 2 in lm["rstd"]:
                v.wait_ge(me.semh["s"], lm["rstd"][tt - 2])
            v.drain()
            v.bn_stats(out=st6[:, 0:6], in_=x_ap[:, 0:512])
            v.bn_stats(out=st6[:, 6:12], in_=x_ap[:, 512:1024])
            v.drain()
            lm["bn"][tt] = me.mark(v.bn_aggr(out=st2[:], in_=st6[:]), "v")

        @blk.scalar
        def _(s):
            s.wait_ge(me.semh["v"], lm["bn"][tt])
            s.activation(out=lnt[:], in_=st2[:, 1:2], func=AF.Ln,
                         bias=me.epsv[:], scale=1.0)
            s.drain()
            lm["rstd"][tt] = me.mark(
                s.activation(out=rstd[:], in_=lnt[:], func=AF.Exp,
                             bias=0.0, scale=-0.5), "s")

        @blk.vector
        def _(v):
            v.wait_ge(me.semh["s"], lm["rstd"][tt])
            lm["nb"][tt] = me.mark(v.scalar_tensor_tensor(
                out=nbias[:], in0=st2[:, 0:1], scalar=-1.0, in1=rstd[:],
                op0=ALU.mult, op1=ALU.mult), "v")

        @blk.scalar
        def _(s):
            s.wait_ge(me.semh["v"], lm["nb"][tt])
            if tt - 2 in lm["trans"]:
                s.wait_ge(me.semh["t"], lm["trans"][tt - 2])
            lm["nrm"][tt] = me.mark(s.activation(
                out=lnb[:, p2 * 1024:p2 * 1024 + 1024], in_=x_ap,
                func=AF.Identity, bias=nbias[:], scale=rstd[:]), "s")

        @blk.tensor
        def _(t):
            t.wait_ge(me.semh["s"], lm["nrm"][tt])
            if tt - 2 in lm["evac"]:
                t.wait_ge(me.semh["v"], lm["evac"][tt - 2])
            last = None
            for dc in range(8):
                last = me.trans(t, me.psT[:, p2 * 1024 + dc * 128:
                                       p2 * 1024 + dc * 128 + 128],
                                lnb[:, p2 * 1024 + dc * 128:
                                    p2 * 1024 + dc * 128 + 128],
                                start=(dc == 0), stop=(dc == 7))
            lm["trans"][tt] = me.mark(last, "t")

        @blk.vector
        def _(v):
            v.wait_ge(me.semh["t"], lm["trans"][tt])
            iv = None
            for dc in range(8):
                iv = v.tensor_copy(
                    out=xlnT[:, dc * 512 + ti_in_block * 128:
                             dc * 512 + ti_in_block * 128 + 128],
                    in_=me.psT[:, p2 * 1024 + dc * 128:
                               p2 * 1024 + dc * 128 + 128])
            lm["evac"][tt] = me.mark(iv, "v")
        return me.cnt["v"]

    def postln_residual(self, y_ap, tt, src, dst):
        me, blk, d = self, self.block, self.d
        half = tt % 2
        v0, s0 = me.cnt["v"], me.cnt["s"]
        vm = {"bn": v0 + 1, "nb": v0 + 2, "add": v0 + 3}
        sm = {"st": s0 + 1, "nrm": s0 + 2}

        @blk.sync
        def _(sy):
            me.w(sy, "v")
            me.w(sy, "sx%d" % half)
            me.mark(sy.dma_start(out=me.xst[:, half * 1024:half * 1024 + 1024],
                                 in_=d[src][tt * 128:tt * 128 + 128, :]),
                    "xs%d" % half)
        lddone = me.cnt["xs%d" % half]

        @blk.vector
        def _(v):
            v.drain()
            v.bn_stats(out=me.st6[:, 0:6], in_=y_ap[:, 0:512])
            v.bn_stats(out=me.st6[:, 6:12], in_=y_ap[:, 512:1024])
            v.drain()
            assert me.mark(v.bn_aggr(out=me.st2[:], in_=me.st6[:]), "v") == vm["bn"]
            v.wait_ge(me.semh["s"], sm["st"])
            iv = v.scalar_tensor_tensor(out=me.nbias[:], in0=me.st2[:, 0:1],
                                        scalar=-1.0, in1=me.rstd[:],
                                        op0=ALU.mult, op1=ALU.mult)
            assert me.mark(iv, "v") == vm["nb"]
            v.wait_ge(me.semh["s"], sm["nrm"])
            v.wait_ge(me.semh["xs%d" % half], lddone)
            iv = v.tensor_tensor(out=me.xst[:, half * 1024:half * 1024 + 1024],
                                 in0=me.xst[:, half * 1024:half * 1024 + 1024],
                                 in1=me.lnst[:, half * 1024:half * 1024 + 1024],
                                 op=ALU.add)
            assert me.mark(iv, "v") == vm["add"]

        @blk.scalar
        def _(s):
            s.wait_ge(me.semh["v"], vm["bn"])
            s.activation(out=me.lnt[:], in_=me.st2[:, 1:2], func=AF.Ln,
                         bias=me.epsv[:], scale=1.0)
            s.drain()
            iv = s.activation(out=me.rstd[:], in_=me.lnt[:], func=AF.Exp,
                              bias=0.0, scale=-0.5)
            assert me.mark(iv, "s") == sm["st"]
            s.wait_ge(me.semh["v"], vm["nb"])
            iv = s.activation(out=me.lnst[:, half * 1024:half * 1024 + 1024],
                              in_=y_ap, func=AF.Identity,
                              bias=me.nbias[:], scale=me.rstd[:])
            assert me.mark(iv, "s") == sm["nrm"]
        self.last_postln_v = vm["add"]
        self.last_postln_free = (("v", vm["bn"]), ("s", sm["nrm"]))

        @blk.sync
        def _(sy):
            me.w(sy, "v", self.last_postln_v)
            me.mark(sy.dma_start(out=d[dst][tt * 128:tt * 128 + 128, :],
                                 in_=me.xst[:, half * 1024:half * 1024 + 1024]),
                    "sx%d" % half)

    def flash_rows(self, rows, nchunk, kT_ap, klen, diag, zerofix, pre=None,
                   normalize=True):
        me, blk = self, self.block
        CW = 1024
        nb = (klen + CW - 1) // CW
        R = len(rows)
        N = R * nb
        slots = [(me.psS, 0), (me.psS, 1024)]

        def step_rc(step):
            return step % R, step // R

        t0, v0, s0 = me.cnt["t"], me.cnt["v"], me.cnt["s"]
        t_mark = [t0 + k + 1 for k in range(N)]
        s_mark = [s0 + k + 1 for k in range(N)]
        v_items = []
        for step in range(N):
            if zerofix and step >= 2:
                v_items.append(("stt", step - 2))
            v_items.append(("red", step))
        if zerofix:
            for j in range(max(0, N - 2), N):
                v_items.append(("stt", j))
        red_mark, stt_mark = {}, {}
        for pos, (kind, j) in enumerate(v_items):
            (red_mark if kind == "red" else stt_mark)[j] = v0 + pos + 1

        def release(step):
            return ("v", stt_mark[step]) if zerofix else ("s", s_mark[step])

        @blk.tensor
        def _(t):
            for sem, val in (pre or []):
                t.wait_ge(me.semh[sem], val)
            for step in range(N):
                r, c = step_rc(step)
                sl = step % 2
                wdt = min(CW, klen - CW * c)
                ten, off = slots[sl]
                if step >= 2:
                    sem, val = release(step - 2)
                    me.w(t, sem, val)
                elif me.psS_rel[sl] is not None:
                    sem, val = me.psS_rel[sl]
                    me.w(t, sem, val)
                qap = rows[r][0]
                last = None
                for sub in range(0, wdt, 512):
                    sw = min(512, wdt - sub)
                    for dc in range(nchunk):
                        last = t.matmul(ten[:, off + sub:off + sub + sw],
                                        qap(dc),
                                        kT_ap(dc, CW * c + sub,
                                              CW * c + sub + sw),
                                        start=(dc == 0),
                                        stop=(dc == nchunk - 1))
                assert me.mark(last, "t") == t_mark[step]
        for k in range(2):
            st = N - 1 - k
            if st >= 0:
                me.psS_rel[st % 2] = release(st)

        @blk.vector
        def _(v):
            for pos, (kind, j) in enumerate(v_items):
                if kind == "red":
                    step = j
                    r, c = step_rc(step)
                    sl = step % 2
                    wdt = min(CW, klen - CW * c)
                    ten, off = slots[sl]
                    v.wait_ge(me.semh["t"], t_mark[step])
                    if step == 0:
                        v.drain()
                        v.memset(me.fsum[:, 0:N], 0.0)
                    if diag[0] == "first" and c == 0:
                        v.tensor_tensor(out=ten[:, off:off + 128],
                                        in0=ten[:, off:off + 128],
                                        in1=diag[1], op=ALU.add)
                        v.drain()
                    if diag[0] == "last" and c == nb - 1:
                        v.tensor_tensor(out=ten[:, off + wdt - 128:off + wdt],
                                        in0=ten[:, off + wdt - 128:off + wdt],
                                        in1=diag[1], op=ALU.add)
                        v.drain()
                    iv = v.reduce_max(out=me.negmh[:, r * nb + c:r * nb + c + 1],
                                      in_=ten[:, off:off + wdt],
                                      axis=AX.X, negate=True)
                    assert me.mark(iv, "v") == red_mark[step]
                else:
                    step = j
                    r, c = step_rc(step)
                    sl = step % 2
                    wdt = min(CW, klen - CW * c)
                    ten, off = slots[sl]
                    pp_row = rows[r][1]
                    v.wait_ge(me.semh["s"], s_mark[step])
                    iv = v.scalar_tensor_tensor(
                        out=pp_row[:, CW * c:CW * c + wdt],
                        in0=ten[:, off:off + wdt], scalar=0.0,
                        in1=pp_row[:, CW * c:CW * c + wdt],
                        op0=ALU.not_equal, op1=ALU.mult)
                    assert me.mark(iv, "v") == stt_mark[step]

        @blk.scalar
        def _(s):
            for step in range(N):
                r, c = step_rc(step)
                sl = step % 2
                wdt = min(CW, klen - CW * c)
                ten, off = slots[sl]
                pp_row = rows[r][1]
                s.wait_ge(me.semh["v"], red_mark[step])
                iv = s.activation(
                    out=pp_row[:, CW * c:CW * c + wdt],
                    in_=ten[:, off:off + wdt], func=AF.Exp,
                    bias=me.negmh[:, r * nb + c:r * nb + c + 1],
                    scale=1.0, accum_out=me.fsum[:, r * nb + c:r * nb + c + 1])
                assert me.mark(iv, "s") == s_mark[step]

        @blk.vector
        def _(v):
            sem, val = me.psS_rel[(N - 1) % 2]
            me.w(v, sem, val)
            v.drain()
            for r in range(R):
                v.tensor_reduce(out=me.negM[:, r:r + 1],
                                in_=me.negmh[:, r * nb:r * nb + nb],
                                axis=AX.X, op=ALU.min)
            v.drain()
            iv = None
            for r in range(R):
                iv = v.tensor_scalar(out=me.fac[:, r * nb:r * nb + nb],
                                     in0=me.negmh[:, r * nb:r * nb + nb],
                                     scalar1=me.negM[:, r:r + 1], scalar2=None,
                                     op0=ALU.subtract)
            me.mark(iv, "v")
        vdone = me.cnt["v"]

        @blk.scalar
        def _(s):
            me.w(s, "v", vdone)
            me.mark(s.activation(out=me.fac[:, 0:N], in_=me.fac[:, 0:N],
                                 func=AF.Exp, bias=0.0, scale=-1.0), "s")
        sdone = me.cnt["s"]

        rc_marks = []

        @blk.vector
        def _(v):
            me.w(v, "s", sdone)
            if normalize:
                v.tensor_tensor(out=me.ftmp[:, 0:N], in0=me.fac[:, 0:N],
                                in1=me.fsum[:, 0:N], op=ALU.mult)
                v.drain()
                for r in range(R):
                    v.tensor_reduce(out=me.rden[:, r:r + 1],
                                    in_=me.ftmp[:, r * nb:r * nb + nb],
                                    axis=AX.X, op=ALU.add)
                v.drain()
                v.reciprocal(out=me.rden[:, 0:R], in_=me.rden[:, 0:R])
                v.drain()
                for r in range(R):
                    v.tensor_scalar(out=me.fac[:, r * nb:r * nb + nb],
                                    in0=me.fac[:, r * nb:r * nb + nb],
                                    scalar1=me.rden[:, r:r + 1], scalar2=None,
                                    op0=ALU.mult)
            v.drain()
            for c in range(nb):
                wdt = min(CW, klen - CW * c)
                iv = None
                for r in range(R):
                    pp_row = rows[r][1]
                    iv = v.tensor_scalar(out=pp_row[:, CW * c:CW * c + wdt],
                                         in0=pp_row[:, CW * c:CW * c + wdt],
                                         scalar1=me.fac[:, r * nb + c:r * nb + c + 1],
                                         scalar2=None, op0=ALU.mult)
                rc_marks.append(me.mark(iv, "v"))
        return rc_marks

    def l0_attn(self):
        me, blk, d = self, self.block, self.d
        with contextlib.ExitStack() as outer:
            kT = self.sb("kT", [128, 8 * T], BF16, outer)

            with contextlib.ExitStack() as ph:
                wbig = self.sb("wbig", [128, 8 * 3072], BF16, ph)
                xlnT = self.sb("xlnT", [128, 8 * 512], BF16, ph)
                wtmp = self.sb("wtmp", [128, 2 * 3072], F32, ph)
                lnb = self.sb("lnb", [128, 2 * 1024], BF16, ph)
                evb = self.sb("evb", [128, 2 * 512], BF16, ph)

                for dc in range(8):
                    half = dc % 2

                    @blk.sync
                    def _(sy):
                        me.w(sy, "v")
                        me.mark(sy.dma_start(
                            out=wtmp[:, half * 3072:half * 3072 + 3072],
                            in_=d["W0"][dc * 128:dc * 128 + 128, :]), "ld")
                    lddone = me.cnt["ld"]

                    @blk.vector
                    def _(v):
                        me.w(v, "ld", lddone)
                        me.mark(v.tensor_copy(
                            out=wbig[:, dc * 3072:dc * 3072 + 3072],
                            in_=wtmp[:, half * 3072:half * 3072 + 3072]), "v")

                lm = {k: {} for k in ("bn", "rstd", "nb", "nrm", "trans",
                                      "evac")}
                for B in range(NB):
                    for ti in range(4):
                        tt = B * 4 + ti
                        half = tt % 2

                        @blk.sync
                        def _(sy):
                            if tt - 2 in lm["nrm"]:
                                sy.wait_ge(me.semh["s"], lm["nrm"][tt - 2])
                                sy.wait_ge(me.semh["v"], lm["bn"][tt - 2])
                            else:
                                me.w(sy, "s")
                            me.mark(sy.dma_start(
                                out=me.xa[:, half * 1024:half * 1024 + 1024],
                                in_=d["x"][tt * 128:tt * 128 + 128, :]),
                                "xa%d" % half)
                        lddone = me.cnt["xa%d" % half]
                        self.ln_tile_to_xlnT(
                            me.xa[:, half * 1024:half * 1024 + 1024], ti, lnb,
                            xlnT, tt=tt, lm=lm, xwait=("xa%d" % half, lddone))

                    for region in ("q", "k"):
                        base = 0 if region == "q" else 1024
                        for fq in range(8):
                            half = fq % 2

                            @blk.tensor
                            def _(t):
                                me.w(t, "v")
                                last = None
                                for dc in range(8):
                                    last = t.matmul(
                                        me.psS[:, half * 512:half * 512 + 512],
                                        wbig[:, dc * 3072 + base + fq * 128:
                                             dc * 3072 + base + fq * 128 + 128],
                                        xlnT[:, dc * 512:dc * 512 + 512],
                                        start=(dc == 0), stop=(dc == 7))
                                me.mark(last, "t")
                            tdone = me.cnt["t"]

                            @blk.vector
                            def _(v):
                                me.w(v, "t", tdone)
                                if region == "k":
                                    me.mark(v.tensor_copy(
                                        out=kT[:, fq * T + B * 512:
                                               fq * T + B * 512 + 512],
                                        in_=me.psS[:, half * 512:half * 512 + 512]),
                                        "v")
                                else:
                                    me.w(v, "ev%d" % half)
                                    me.mark(v.tensor_scalar(
                                        out=evb[:, half * 512:half * 512 + 512],
                                        in0=me.psS[:, half * 512:half * 512 + 512],
                                        scalar1=1.0 / 32.0, scalar2=None,
                                        op0=ALU.mult), "v")
                            vdone = me.cnt["v"]

                            if region == "q":
                                @blk.sync
                                def _(sy):
                                    me.w(sy, "v", vdone)
                                    me.mark(sy.dma_start(
                                        out=d["qT_h"][fq * 128:fq * 128 + 128,
                                                      B * 512:B * 512 + 512],
                                        in_=evb[:, half * 512:half * 512 + 512]),
                                        "ev%d" % half)

                    for ti in range(4):
                        tt = B * 4 + ti
                        for e in range(2):
                            @blk.tensor
                            def _(t):
                                me.w(t, "v")
                                last = None
                                for dc in range(8):
                                    last = t.matmul(
                                        me.psO[:, e * 512:e * 512 + 512],
                                        xlnT[:, dc * 512 + ti * 128:
                                             dc * 512 + ti * 128 + 128],
                                        wbig[:, dc * 3072 + 2048 + e * 512:
                                             dc * 3072 + 2048 + e * 512 + 512],
                                        start=(dc == 0), stop=(dc == 7))
                                me.mark(last, "t")
                            tdone = me.cnt["t"]

                            @blk.vector
                            def _(v):
                                me.w(v, "t", tdone)
                                me.w(v, "ev%d" % e)
                                me.mark(v.tensor_copy(
                                    out=evb[:, e * 512:e * 512 + 512],
                                    in_=me.psO[:, e * 512:e * 512 + 512]), "v")
                            vdone = me.cnt["v"]

                            @blk.sync
                            def _(sy):
                                me.w(sy, "v", vdone)
                                me.mark(sy.dma_start(
                                    out=d["v_h"][tt * 128:tt * 128 + 128,
                                                 e * 512:e * 512 + 512],
                                    in_=evb[:, e * 512:e * 512 + 512]), "ev%d" % e)
                self.barrier()

            with contextlib.ExitStack() as ph:
                vres = self.sb("vres", [128, NT * 1024], BF16, ph)
                pp = self.sb("pp", [128, T], BF16, ph)
                ppT = self.sb("ppT", [128, 2 * 512], BF16, ph)
                qt = self.sb("qt", [128, 2 * 1024], BF16, ph)

                for j in range(NT):
                    @blk.sync
                    def _(sy):
                        me.mark(sy.dma_start(
                            out=vres[:, j * 1024:j * 1024 + 1024],
                            in_=d["v_h"][j * 128:j * 128 + 128, :]), "lb")

                qt_ld = {}
                qS_last = {}

                def emit_qt_ld(i):
                    qhalf = i % 2

                    @blk.sync
                    def _(sy):
                        if i - 1 >= 0:
                            sy.wait_ge(me.semh["t"], qS_last[i - 1])
                        else:
                            me.w(sy, "t")
                        qt_ld[i] = me.mark(sy.dma_start(
                            out=qt[:, qhalf * 1024:qhalf * 1024 + 1024]
                            .rearrange("p (k c) -> p k c", k=8),
                            in_=d["qT_h"][:, i * 128:i * 128 + 128]
                            .rearrange("(k p) c -> p k c", k=8)),
                            "qt%d" % qhalf)

                emit_qt_ld(0)
                for i in range(NT):
                    qhalf = i % 2
                    k0 = 128 * i
                    klen = T - k0

                    qS_last[i] = me.cnt["t"] + (klen + 1023) // 1024
                    if i + 1 < NT:
                        emit_qt_ld(i + 1)

                    rc = self.flash_rows(
                        rows=[(lambda dc: qt[:, qhalf * 1024 + dc * 128:
                                             qhalf * 1024 + dc * 128 + 128],
                               pp[:, 0:klen])],
                        nchunk=8,
                        kT_ap=lambda dc, a, b: kT[:, dc * T + k0 + a:dc * T + k0 + b],
                        klen=klen, diag=("first", me.amask[:]), zerofix=True,
                        pre=[("qt%d" % qhalf, qt_ld[i])], normalize=False)

                    nk = klen // 128
                    ngrp = (nk + 3) // 4
                    t0 = me.cnt["t"]
                    v0 = me.cnt["v"]
                    t_items = []
                    for grp in range(ngrp):
                        t_items.append(("tr", grp))
                        if grp >= 1:
                            t_items.append(("av", grp - 1))
                    t_items.append(("av", ngrp - 1))
                    tr_mark, av_mark = {}, {}
                    for pos, (kind, g2) in enumerate(t_items):
                        (tr_mark if kind == "tr" else av_mark)[g2] = t0 + pos + 1
                    ev_mark = {grp: v0 + grp + 1 for grp in range(ngrp)}

                    @blk.tensor
                    def _(t):
                        me.w(t, "lb")
                        for kind, grp in t_items:
                            gw = min(4, nk - grp * 4)
                            ph2 = grp % 2
                            if kind == "tr":
                                t.wait_ge(me.semh["v"], rc[grp // 2])
                                if grp < 2:
                                    if me.psT_rel[ph2] is not None:
                                        sem, val = me.psT_rel[ph2]
                                        me.w(t, sem, val)
                                else:
                                    t.wait_ge(me.semh["v"], ev_mark[grp - 2])
                                last = None
                                for jj in range(gw):
                                    last = me.trans(
                                        t, me.psT[:, ph2 * 1024 + jj * 128:
                                                  ph2 * 1024 + jj * 128 + 128],
                                        pp[:, (grp * 4 + jj) * 128:
                                           (grp * 4 + jj) * 128 + 128],
                                        start=(jj == 0), stop=(jj == gw - 1))
                                assert me.mark(last, "t") == tr_mark[grp]
                            else:
                                t.wait_ge(me.semh["v"], ev_mark[grp])
                                if grp == 0 and me.psO_rel is not None:
                                    sem, val = me.psO_rel
                                    me.w(t, sem, val)
                                last = None
                                for jj in range(gw):
                                    j = grp * 4 + jj
                                    jabs = i + j
                                    for e in range(2):
                                        last = t.matmul(
                                            me.psO[:, e * 512:e * 512 + 512],
                                            ppT[:, ph2 * 512 + jj * 128:
                                                ph2 * 512 + jj * 128 + 128],
                                            vres[:, jabs * 1024 + e * 512:
                                                 jabs * 1024 + e * 512 + 512],
                                            start=(j == 0), stop=(j == nk - 1))
                                assert me.mark(last, "t") == av_mark[grp]

                    @blk.vector
                    def _(v):
                        for grp in range(ngrp):
                            gw = min(4, nk - grp * 4)
                            ph2 = grp % 2
                            v.wait_ge(me.semh["t"], tr_mark[grp])
                            if grp >= 2:
                                v.wait_ge(me.semh["t"], av_mark[grp - 2])
                            elif me.ppT_rel[ph2] is not None:
                                sem, val = me.ppT_rel[ph2]
                                me.w(v, sem, val)
                            iv = v.tensor_copy(
                                out=ppT[:, ph2 * 512:ph2 * 512 + gw * 128],
                                in_=me.psT[:, ph2 * 1024:ph2 * 1024 + gw * 128])
                            assert me.mark(iv, "v") == ev_mark[grp]
                    for g2 in range(max(0, ngrp - 2), ngrp):
                        me.psT_rel[g2 % 2] = ("v", ev_mark[g2])
                        me.ppT_rel[g2 % 2] = ("t", av_mark[g2])
                    tdone = me.cnt["t"]

                    @blk.vector
                    def _(v):
                        me.w(v, "t", tdone)
                        me.mark(v.nop(), "v")

                    self.postln_residual(me.psO[:, 0:1024], i, src="x", dst="xres")
                    me.psO_rel = ("v", me.last_postln_v)
                self.barrier()

    def l1_attn(self):
        me, blk, d = self, self.block, self.d
        with contextlib.ExitStack() as outer:
            qTb = self.sb("qTb", [128, 8 * T], BF16, outer)
            kl1 = self.sb("kl1", [128, 2 * T], BF16, outer)
            vl1 = self.sb("vl1", [128, 2 * T], BF16, outer)
            wol = self.sb("wol", [128, 8 * 1024], BF16, outer)

            with contextlib.ExitStack() as ph:
                wql = self.sb("wql", [128, 8 * 1024], BF16, ph)
                wkv = self.sb("wkv", [128, 8 * 128], BF16, ph)
                wu = self.sb("wu", [32, 256], BF16, ph)
                xlnT = self.sb("xlnT1", [128, 8 * 512], BF16, ph)
                wtmp = self.sb("wtmp1", [128, 2 * 1024], F32, ph)
                lnb = self.sb("lnb1", [128, 2 * 1024], BF16, ph)
                kst = [self.sb(f"kst{g}", [32, 512], BF16, ph) for g in range(2)]
                vst = [self.sb(f"vst{g}", [32, 512], BF16, ph) for g in range(2)]

                for name, dest, cols in (("wq1", wql, 1024), ("wo1", wol, 1024)):
                    for dc in range(8):
                        half = dc % 2

                        @blk.sync
                        def _(sy):
                            me.w(sy, "v")
                            me.mark(sy.dma_start(
                                out=wtmp[:, half * 1024:half * 1024 + cols],
                                in_=d[name][dc * 128:dc * 128 + 128, :]), "ld")
                        lddone = me.cnt["ld"]

                        @blk.vector
                        def _(v):
                            me.w(v, "ld", lddone)
                            me.mark(v.tensor_copy(
                                out=dest[:, dc * 1024:dc * 1024 + cols],
                                in_=wtmp[:, half * 1024:half * 1024 + cols]), "v")
                for dc in range(8):
                    half = dc % 2

                    @blk.sync
                    def _(sy):
                        me.w(sy, "v")
                        me.mark(sy.dma_start(
                            out=wtmp[:, half * 1024:half * 1024 + 64],
                            in_=d["wkd1"][dc * 128:dc * 128 + 128, :]), "ld")
                        me.mark(sy.dma_start(
                            out=wtmp[:, half * 1024 + 64:half * 1024 + 128],
                            in_=d["wvd1"][dc * 128:dc * 128 + 128, :]), "ld")
                    lddone = me.cnt["ld"]

                    @blk.vector
                    def _(v):
                        me.w(v, "ld", lddone)
                        me.mark(v.tensor_copy(
                            out=wkv[:, dc * 128:dc * 128 + 128],
                            in_=wtmp[:, half * 1024:half * 1024 + 128]), "v")

                @blk.sync
                def _(sy):
                    me.w(sy, "v")
                    me.mark(sy.dma_start(out=wtmp[0:32, 0:128], in_=d["wku1"][:]), "ld")
                    me.mark(sy.dma_start(out=wtmp[0:32, 128:256], in_=d["wvu1"][:]),
                            "ld")
                lddone = me.cnt["ld"]

                @blk.vector
                def _(v):
                    me.w(v, "ld", lddone)
                    me.mark(v.tensor_copy(out=wu[:], in_=wtmp[0:32, 0:256]), "v")

                scl = float(1.0 / np.sqrt(128.0))
                lm = {k: {} for k in ("bn", "rstd", "nb", "nrm", "trans",
                                      "evac")}
                for B in range(NB):
                    for ti in range(4):
                        tt = B * 4 + ti
                        half = tt % 2

                        @blk.sync
                        def _(sy):
                            if tt - 2 in lm["nrm"]:
                                sy.wait_ge(me.semh["s"], lm["nrm"][tt - 2])
                                sy.wait_ge(me.semh["v"], lm["bn"][tt - 2])
                            else:
                                me.w(sy, "s")
                            me.mark(sy.dma_start(
                                out=me.xa[:, half * 1024:half * 1024 + 1024],
                                in_=d["xres"][tt * 128:tt * 128 + 128, :]),
                                "xa%d" % half)
                        lddone = me.cnt["xa%d" % half]
                        self.ln_tile_to_xlnT(
                            me.xa[:, half * 1024:half * 1024 + 1024], ti, lnb,
                            xlnT, tt=tt, lm=lm, xwait=("xa%d" % half, lddone))

                    for h in range(8):
                        half = h % 2

                        @blk.tensor
                        def _(t):
                            me.w(t, "v")
                            last = None
                            for dc in range(8):
                                last = t.matmul(
                                    me.psS[:, half * 512:half * 512 + 512],
                                    wql[:, dc * 1024 + h * 128:
                                        dc * 1024 + h * 128 + 128],
                                    xlnT[:, dc * 512:dc * 512 + 512],
                                    start=(dc == 0), stop=(dc == 7))
                            me.mark(last, "t")
                        tdone = me.cnt["t"]

                        @blk.vector
                        def _(v):
                            me.w(v, "t", tdone)
                            me.mark(v.tensor_scalar(
                                out=qTb[:, h * T + B * 512:h * T + B * 512 + 512],
                                in0=me.psS[:, half * 512:half * 512 + 512],
                                scalar1=scl, scalar2=None, op0=ALU.mult), "v")

                    @blk.tensor
                    def _(t):
                        me.w(t, "v")
                        last = None
                        for dc in range(8):
                            last = t.matmul(me.psM[:, 0:512],
                                            wkv[:, dc * 128:dc * 128 + 128],
                                            xlnT[:, dc * 512:dc * 512 + 512],
                                            start=(dc == 0), stop=(dc == 7))
                        me.mark(last, "t")
                    tdone = me.cnt["t"]

                    @blk.vector
                    def _(v):
                        me.w(v, "t", tdone)
                        v.tensor_copy(out=kst[0][:], in_=me.psM[0:32, 0:512])
                        v.tensor_copy(out=kst[1][:], in_=me.psM[32:64, 0:512])
                        v.tensor_copy(out=vst[0][:], in_=me.psM[64:96, 0:512])
                        me.mark(v.tensor_copy(out=vst[1][:], in_=me.psM[96:128, 0:512]),
                                "v")
                    vdone = me.cnt["v"]

                    for g in range(2):
                        @blk.tensor
                        def _(t):
                            me.w(t, "v")
                            me.mark(t.matmul(me.psN[:, 0:512], wu[0:32, 0:128],
                                             kst[g][:], start=True, stop=True), "t")
                        tdone = me.cnt["t"]

                        @blk.vector
                        def _(v):
                            me.w(v, "t", tdone)
                            me.mark(v.tensor_copy(
                                out=kl1[:, g * T + B * 512:g * T + B * 512 + 512],
                                in_=me.psN[:, 0:512]), "v")

                        for ti in range(4):
                            @blk.tensor
                            def _(t):
                                me.w(t, "v")
                                me.mark(t.matmul(
                                    me.psN[:, 0:128],
                                    vst[g][:, ti * 128:ti * 128 + 128],
                                    wu[0:32, 128:256], start=True, stop=True), "t")
                            tdone = me.cnt["t"]
                            col = (g * NT + B * 4 + ti) * 128

                            @blk.vector
                            def _(v):
                                me.w(v, "t", tdone)
                                me.mark(v.tensor_copy(out=vl1[:, col:col + 128],
                                                      in_=me.psN[:, 0:128]), "v")
                if self.debug:
                    @blk.sync
                    def _(sy):
                        me.w(sy, "v")
                        me.mark(sy.dma_start(out=d["dbg_q"][:], in_=qTb[:]), "lb")
                        me.mark(sy.dma_start(out=d["dbg_k"][:], in_=kl1[:]), "lb")
                        me.mark(sy.dma_start(out=d["dbg_v"][:], in_=vl1[:]), "lb")
                self.barrier()

            with contextlib.ExitStack() as ph:
                pp = self.sb("pp4", [128, 4 * T], BF16, ph)
                ppT = self.sb("ppT1", [128, 2 * 512], BF16, ph)
                oT = self.sb("oT", [128, 1024], BF16, ph)

                for i in range(NT):
                    klen = 128 * (i + 1)
                    nk = i + 1
                    for g in range(2):
                        def qap_for(h):
                            return lambda dc: qTb[:, h * T + i * 128:
                                                  h * T + i * 128 + 128]
                        rc = self.flash_rows(
                            rows=[(qap_for(4 * g + r), pp[:, r * T:r * T + klen])
                                  for r in range(4)],
                            nchunk=1,
                            kT_ap=lambda dc, a, b: kl1[:, g * T + a:g * T + b],
                            klen=klen, diag=("last", me.cmask[:]), zerofix=False)

                        t0 = me.cnt["t"]
                        v0 = me.cnt["v"]
                        t_items = []
                        for j in range(nk):
                            t_items.append(("tr", j))
                            if j >= 1:
                                t_items.append(("av", j - 1))
                        t_items.append(("av", nk - 1))
                        tr_mark, av_mark = {}, {}
                        for pos, (kind, j2) in enumerate(t_items):
                            (tr_mark if kind == "tr" else av_mark)[j2] = t0 + pos + 1
                        ev_mark = {j: v0 + j + 1 for j in range(nk)}

                        @blk.tensor
                        def _(t):
                            for kind, j in t_items:
                                ph2 = j % 2
                                if kind == "tr":
                                    t.wait_ge(me.semh["v"], rc[j // 8])
                                    if j < 2:
                                        if me.psT_rel[ph2] is not None:
                                            sem, val = me.psT_rel[ph2]
                                            me.w(t, sem, val)
                                    else:
                                        t.wait_ge(me.semh["v"], ev_mark[j - 2])
                                    last = None
                                    for r in range(4):
                                        last = me.trans(
                                            t, me.psT[:, ph2 * 1024 + r * 128:
                                                      ph2 * 1024 + r * 128 + 128],
                                            pp[:, r * T + j * 128:
                                               r * T + j * 128 + 128],
                                            start=(r == 0), stop=(r == 3))
                                    assert me.mark(last, "t") == tr_mark[j]
                                else:
                                    t.wait_ge(me.semh["v"], ev_mark[j])
                                    if j == 0 and g == 0 and me.psO_rel is not None:
                                        sem, val = me.psO_rel
                                        me.w(t, sem, val)
                                    iv = t.matmul(
                                        me.psO[:, g * 512:g * 512 + 512],
                                        vl1[:, (g * NT + j) * 128:
                                            (g * NT + j) * 128 + 128],
                                        ppT[:, ph2 * 512:ph2 * 512 + 512],
                                        start=(j == 0), stop=(j == nk - 1))
                                    assert me.mark(iv, "t") == av_mark[j]

                        @blk.vector
                        def _(v):
                            for j in range(nk):
                                ph2 = j % 2
                                v.wait_ge(me.semh["t"], tr_mark[j])
                                if j >= 2:
                                    v.wait_ge(me.semh["t"], av_mark[j - 2])
                                elif me.ppT_rel[ph2] is not None:
                                    sem, val = me.ppT_rel[ph2]
                                    me.w(v, sem, val)
                                iv = v.tensor_copy(
                                    out=ppT[:, ph2 * 512:ph2 * 512 + 512],
                                    in_=me.psT[:, ph2 * 1024:ph2 * 1024 + 512])
                                assert me.mark(iv, "v") == ev_mark[j]
                        for j2 in range(max(0, nk - 2), nk):
                            me.psT_rel[j2 % 2] = ("v", ev_mark[j2])
                            me.ppT_rel[j2 % 2] = ("t", av_mark[j2])
                    tdone = me.cnt["t"]

                    @blk.vector
                    def _(v):
                        me.w(v, "t", tdone)
                        if self.debug:
                            me.w(v, "lb")
                        me.mark(v.tensor_copy(out=oT[:], in_=me.psO[:, 0:1024]), "v")
                    vdone = me.cnt["v"]
                    me.psO_rel = ("v", vdone)
                    if self.debug and i == 0:
                        @blk.sync
                        def _(sy):
                            me.w(sy, "v", vdone)
                            me.mark(sy.dma_start(out=d["dbg_o"][:], in_=oT[:]), "lb")

                    @blk.tensor
                    def _(t):
                        me.w(t, "v", vdone)
                        if me.psO_rel is not None:
                            sem, val = me.psO_rel
                            me.w(t, sem, val)
                        last = None
                        for e in range(2):
                            for h in range(8):
                                last = t.matmul(
                                    me.psO[:, e * 512:e * 512 + 512],
                                    oT[:, h * 128:h * 128 + 128],
                                    wol[:, h * 1024 + e * 512:
                                        h * 1024 + e * 512 + 512],
                                    start=(h == 0), stop=(h == 7))
                        me.mark(last, "t")
                    tdone = me.cnt["t"]

                    @blk.vector
                    def _(v):
                        me.w(v, "t", tdone)
                        me.mark(v.nop(), "v")

                    self.postln_residual(me.psO[:, 0:1024], i, src="xres", dst="xres")
                    me.psO_rel = ("v", me.last_postln_v)
                self.barrier()

    def ffn(self, L):
        me, blk, d = self, self.block, self.d
        w1name = "w1_0" if L == 0 else "w1_1"
        w2name = "w2_0" if L == 0 else "w2_1"
        dstname = "xres" if L == 0 else "out"

        with contextlib.ExitStack() as ph:
            wq1T = self.sb("wq1T%d" % L, [128, 8 * HID], BF16, ph)
            ha = self.sb("ha%d" % L, [128, 4096], F32, ph)
            hb = self.sb("hbuf%d" % L, [128, 4096], F32, ph)
            hq = self.sb("hq%d" % L, [128, 4096], BF16, ph)
            xq = self.sb("xq%d" % L, [128, 1024], BF16, ph)
            xqT = self.sb("xqT%d" % L, [128, 2 * 1024], BF16, ph)
            lna = self.sb("lna%d" % L, [128, 3 * 1024], F32, ph)
            lnx = self.sb("lnx%d" % L, [128, 1024], F32, ph)
            brep = self.sb("brep", [128, 4096], F32, ph) if L == 0 else None
            grep = self.sb("grep", [128, 1024], F32, ph) if L == 0 else None
            erep = self.sb("erep", [128, 1024], F32, ph) if L == 0 else None

            with contextlib.ExitStack() as wph:
                wtmp = self.sb("wtmp2%d" % L, [128, 3 * 1024], F32, wph)
                self.weight_quant_pass(w1name, which=1, dest=wq1T, wtmp=wtmp,
                                       ha=ha, hq=hq, dstv=me.swv)
                if L == 0:
                    self.replicate_vec("b1_0", brep, 4096)
                    self.replicate_vec("gam0", grep, 1024)
                    self.replicate_vec("bet0", erep, 1024)
                self.barrier()

            ffn1_rel = [None, None, None, None]
            carry = {"xa": [None, None], "trq_prev": None, "gel7_prev": None,
                     "hqst": None}
            plans = {}

            _plan = {"v": 0, "s": 0, "t": 0}

            def seg_marks(P, eng, items):
                base = max(me.cnt[eng], _plan[eng])
                mm = {k: base + i + 1 for i, k in enumerate(items)}
                _plan[eng] = base + len(items)
                key = {"v": "vm", "s": "sm", "t": "tm"}[eng]
                P.setdefault(key, {}).update(mm)
                return mm

            def buf(tt):
                p = tt % 2
                return dict(
                    lnap=lna[:, p * 1024:p * 1024 + 1024],
                    ln2p=lna[:, 2048:3072],
                    lnxp=lnx[:, 0:1024],
                    xqp=xq[:, 0:1024],
                    xqTp=xqT[:, p * 1024:p * 1024 + 1024],
                    st6p=me.fst6[:, p * 12:p * 12 + 12],
                    st2p=me.fst2[:, p * 2:p * 2 + 2],
                    rstdp=me.frstd[:, p:p + 1],
                    nbp=me.fnb[:, p:p + 1],
                    lntp=me.flnt[:, p:p + 1],
                    amp=me.fam[:, p:p + 1],
                    sc1p=me.fsact[:, p * 2:p * 2 + 1],
                    sc2p=me.fsact[:, p * 2 + 1:p * 2 + 2],
                    rscp=me.frsc[:, p:p + 1],
                    xap=me.xa[:, p * 1024:p * 1024 + 1024])

            def front(tt):
                P, B = plans[tt], buf(tt)
                p = tt % 2
                v_items = ["bn1", "nb1"]
                if L == 0:
                    v_items += ["bn2", "nb2", "gb"]
                v_items += ["qnt", "clp1", "evq"]
                seg_marks(P, "v", v_items)
                s_items = ["st1", "nrm1"]
                if L == 0:
                    s_items += ["st2", "nrm2"]
                s_items += ["t1", "rnd1"]
                seg_marks(P, "s", s_items)
                seg_marks(P, "t", ["trq"])
                vm, sm = P["vm"], P["sm"]
                lnfin = B["ln2p"] if L == 0 else B["lnap"]

                @blk.sync
                def _(sy):
                    if carry["xa"][p] is not None:
                        vmk, smk = carry["xa"][p]
                        sy.wait_ge(me.semh["v"], vmk)
                        sy.wait_ge(me.semh["s"], smk)
                    me.mark(sy.dma_start(
                        out=B["xap"],
                        in_=d["xres"][tt * 128:tt * 128 + 128, :]), "xa%d" % p)
                P["xadone"] = me.cnt["xa%d" % p]
                carry["xa"][p] = (vm["nb1"], sm["nrm1"])

                @blk.vector
                def _(v):
                    v.wait_ge(me.semh["xa%d" % p], P["xadone"])
                    v.drain()
                    v.bn_stats(out=B["st6p"][:, 0:6], in_=B["xap"][:, 0:512])
                    v.bn_stats(out=B["st6p"][:, 6:12], in_=B["xap"][:, 512:1024])
                    v.drain()
                    assert me.mark(v.bn_aggr(out=B["st2p"], in_=B["st6p"]), "v") == vm["bn1"]
                    v.wait_ge(me.semh["s"], sm["st1"])
                    iv = v.scalar_tensor_tensor(out=B["nbp"], in0=B["st2p"][:, 0:1],
                                                scalar=-1.0, in1=B["rstdp"],
                                                op0=ALU.mult, op1=ALU.mult)
                    assert me.mark(iv, "v") == vm["nb1"]
                    if L == 0:
                        v.wait_ge(me.semh["s"], sm["nrm1"])
                        v.drain()
                        v.bn_stats(out=B["st6p"][:, 0:6], in_=B["lnap"][:, 0:512])
                        v.bn_stats(out=B["st6p"][:, 6:12], in_=B["lnap"][:, 512:1024])
                        v.drain()
                        assert me.mark(v.bn_aggr(out=B["st2p"], in_=B["st6p"]), "v") == vm["bn2"]
                        v.wait_ge(me.semh["s"], sm["st2"])
                        iv = v.scalar_tensor_tensor(out=B["nbp"], in0=B["st2p"][:, 0:1],
                                                    scalar=-1.0, in1=B["rstdp"],
                                                    op0=ALU.mult, op1=ALU.mult)
                        assert me.mark(iv, "v") == vm["nb2"]
                        v.wait_ge(me.semh["s"], sm["nrm2"])
                        v.tensor_tensor(out=B["lnxp"], in0=B["lnxp"], in1=grep[:],
                                        op=ALU.mult)
                        v.drain()
                        iv = v.tensor_tensor(out=lnfin, in0=B["lnxp"], in1=erep[:],
                                             op=ALU.add)
                        assert me.mark(iv, "v") == vm["gb"]
                        v.drain()
                    else:
                        v.wait_ge(me.semh["s"], sm["nrm1"])
                    v.tensor_reduce(out=B["amp"], in_=lnfin, axis=AX.X, op=ALU.max,
                                    apply_absolute_value=True)
                    v.drain()
                    v.tensor_scalar(out=B["amp"], in0=B["amp"], scalar1=EPS,
                                    scalar2=None, op0=ALU.max)
                    v.drain()
                    v.reciprocal(out=B["sc1p"], in_=B["amp"])
                    v.drain()
                    v.tensor_scalar(out=B["sc1p"], in0=B["sc1p"], scalar1=127.0,
                                    scalar2=None, op0=ALU.mult)
                    v.scalar_tensor_tensor(out=B["rscp"], in0=B["amp"],
                                           scalar=1.0 / 127.0, in1=me.swv[:],
                                           op0=ALU.mult, op1=ALU.mult)
                    v.drain()
                    assert me.mark(v.nop(), "v") == vm["qnt"]
                    v.wait_ge(me.semh["s"], sm["rnd1"])
                    if carry["trq_prev"] is not None:
                        v.wait_ge(me.semh["t"], carry["trq_prev"])
                    iv = v.tensor_scalar(out=B["xqp"], in0=B["lnxp"], scalar1=-128.0,
                                         scalar2=127.0, op0=ALU.max, op1=ALU.min)
                    assert me.mark(iv, "v") == vm["clp1"]
                    v.wait_ge(me.semh["t"], P["tm"]["trq"])
                    if plans.get(tt - 2) is not None:
                        v.wait_ge(me.semh["t"], plans[tt - 2]["tm"]["mm7"])
                    assert me.mark(v.tensor_copy(out=B["xqTp"], in_=me.psT[:, 0:1024]),
                                   "v") == vm["evq"]

                @blk.scalar
                def _(s):
                    s.wait_ge(me.semh["v"], vm["bn1"])
                    s.activation(out=B["lntp"], in_=B["st2p"][:, 1:2], func=AF.Ln,
                                 bias=me.epsv[:], scale=1.0)
                    s.drain()
                    iv = s.activation(out=B["rstdp"], in_=B["lntp"], func=AF.Exp,
                                      bias=0.0, scale=-0.5)
                    assert me.mark(iv, "s") == sm["st1"]
                    s.wait_ge(me.semh["v"], vm["nb1"])
                    iv = s.activation(out=B["lnap"], in_=B["xap"], func=AF.Identity,
                                      bias=B["nbp"], scale=B["rstdp"])
                    assert me.mark(iv, "s") == sm["nrm1"]
                    if L == 0:
                        s.wait_ge(me.semh["v"], vm["bn2"])
                        s.activation(out=B["lntp"], in_=B["st2p"][:, 1:2], func=AF.Ln,
                                     bias=me.epsv[:], scale=1.0)
                        s.drain()
                        iv = s.activation(out=B["rstdp"], in_=B["lntp"], func=AF.Exp,
                                          bias=0.0, scale=-0.5)
                        assert me.mark(iv, "s") == sm["st2"]
                        s.wait_ge(me.semh["v"], vm["nb2"])
                        iv = s.activation(out=B["lnxp"], in_=B["lnap"],
                                          func=AF.Identity,
                                          bias=B["nbp"], scale=B["rstdp"])
                        assert me.mark(iv, "s") == sm["nrm2"]
                    s.wait_ge(me.semh["v"], vm["qnt"])
                    iv = s.activation(out=B["lnxp"], in_=lnfin, func=AF.Identity,
                                      bias=me.magic[:], scale=B["sc1p"])
                    assert me.mark(iv, "s") == sm["t1"]
                    s.drain()
                    iv = s.activation(out=B["lnxp"], in_=B["lnxp"], func=AF.Identity,
                                      bias=me.nmagic[:], scale=1.0)
                    assert me.mark(iv, "s") == sm["rnd1"]

                @blk.tensor
                def _(t):
                    t.wait_ge(me.semh["v"], vm["clp1"])
                    if plans.get(tt - 1) is not None:
                        t.wait_ge(me.semh["v"], plans[tt - 1]["vm"]["evq"])
                    last = None
                    for dc in range(8):
                        last = me.trans(t, me.psT[:, dc * 128:dc * 128 + 128],
                                        B["xqp"][:, dc * 128:dc * 128 + 128],
                                        start=(dc == 0), stop=(dc == 7))
                    assert me.mark(last, "t") == P["tm"]["trq"]
                carry["trq_prev"] = P["tm"]["trq"]

            def back_a(tt):
                P, B = plans[tt], buf(tt)
                seg_marks(P, "t", ["mm%d" % k for k in range(8)])
                va = []
                if L == 0:
                    va += ["epi%d" % k for k in range(8)]
                va += ["qn2"]
                seg_marks(P, "v", va)
                seg_marks(P, "s", ["gel%d" % k for k in range(8)] +
                          (["t2", "rnd2"] if L == 0 else ["t2"]))
                vm, sm, tm = P["vm"], P["sm"], P["tm"]

                @blk.tensor
                def _(t):
                    for k in range(8):
                        hf = k % 4
                        t.wait_ge(me.semh["v"], vm["evq"])
                        if ffn1_rel[hf] is not None:
                            sem, val = ffn1_rel[hf]
                            me.w(t, sem, val)
                        last = None
                        for dc in range(8):
                            last = t.matmul(
                                me.psS[:, hf * 512:hf * 512 + 512],
                                B["xqTp"][:, dc * 128:dc * 128 + 128],
                                wq1T[:, dc * HID + k * 512:dc * HID + k * 512 + 512],
                                start=(dc == 0), stop=(dc == 7))
                        assert me.mark(last, "t") == tm["mm%d" % k]
                        ffn1_rel[hf] = ("v", vm["epi%d" % k]) if L == 0 else \
                                       ("s", sm["gel%d" % k])

                @blk.vector
                def _(v):
                    if L == 0:
                        for k in range(8):
                            v.wait_ge(me.semh["t"], tm["mm%d" % k])
                            if k == 0 and carry["gel7_prev"] is not None:
                                v.wait_ge(me.semh["s"], carry["gel7_prev"])
                            iv = v.scalar_tensor_tensor(
                                out=ha[:, k * 512:k * 512 + 512],
                                in0=me.psS[:, (k % 4) * 512:(k % 4) * 512 + 512],
                                scalar=B["rscp"],
                                in1=brep[:, k * 512:k * 512 + 512],
                                op0=ALU.mult, op1=ALU.add)
                            assert me.mark(iv, "v") == vm["epi%d" % k]
                    for k in range(8):
                        v.wait_ge(me.semh["s"], sm["gel%d" % k])
                        if k == 0:
                            v.drain()
                        v.tensor_reduce(out=me.fam8[:, k:k + 1],
                                        in_=hb[:, k * 512:k * 512 + 512],
                                        axis=AX.X, op=ALU.max,
                                        apply_absolute_value=True)
                    v.drain()
                    v.tensor_reduce(out=B["amp"], in_=me.fam8[:], axis=AX.X,
                                    op=ALU.max, apply_absolute_value=True)
                    v.drain()
                    v.tensor_scalar(out=B["amp"], in0=B["amp"], scalar1=EPS,
                                    scalar2=None, op0=ALU.max)
                    v.drain()
                    v.reciprocal(out=B["sc2p"], in_=B["amp"])
                    v.drain()
                    v.tensor_scalar(out=B["sc2p"], in0=B["sc2p"], scalar1=127.0,
                                    scalar2=None, op0=ALU.mult)
                    v.drain()
                    iv = v.tensor_scalar(out=me.rs2all[:, tt:tt + 1], in0=B["amp"],
                                         scalar1=1.0 / 127.0, scalar2=None,
                                         op0=ALU.mult)
                    assert me.mark(iv, "v") == vm["qn2"]

                @blk.scalar
                def _(s):
                    for k in range(8):
                        if L == 0:
                            s.wait_ge(me.semh["v"], vm["epi%d" % k])
                            iv = s.activation(out=hb[:, k * 512:k * 512 + 512],
                                              in_=ha[:, k * 512:k * 512 + 512],
                                              func=AF.Gelu)
                        else:
                            s.wait_ge(me.semh["t"], tm["mm%d" % k])
                            if k == 0:
                                s.wait_ge(me.semh["v"], vm["qnt"])
                                if carry.get("hbfree") is not None:
                                    s.wait_ge(me.semh["v"], carry["hbfree"])
                            iv = s.activation(
                                out=hb[:, k * 512:k * 512 + 512],
                                in_=me.psS[:, (k % 4) * 512:(k % 4) * 512 + 512],
                                func=AF.Gelu, bias=0.0, scale=B["rscp"])
                        assert me.mark(iv, "s") == sm["gel%d" % k]
                    s.wait_ge(me.semh["v"], vm["qn2"])
                    iv = s.activation(out=hb[:], in_=hb[:], func=AF.Identity,
                                      bias=me.magic[:], scale=B["sc2p"])
                    assert me.mark(iv, "s") == sm["t2"]
                    if L == 0:
                        s.drain()
                        iv = s.activation(out=hb[:], in_=hb[:], func=AF.Identity,
                                          bias=me.nmagic[:], scale=1.0)
                        assert me.mark(iv, "s") == sm["rnd2"]
                carry["gel7_prev"] = sm["gel7"]

            def back_b(tt):
                P = plans[tt]
                seg_marks(P, "v", ["clp2"])
                vm, sm = P["vm"], P["sm"]

                @blk.vector
                def _(v):
                    if L == 0:
                        v.wait_ge(me.semh["s"], sm["rnd2"])
                    else:
                        v.wait_ge(me.semh["s"], sm["t2"])
                    if carry.get("hqst") is not None:
                        v.wait_ge(me.semh["ev0"], carry["hqst"])
                    if L == 0:
                        iv = v.tensor_scalar(out=hq[:], in0=hb[:], scalar1=-128.0,
                                             scalar2=127.0, op0=ALU.max,
                                             op1=ALU.min)
                    else:
                        v.drain()
                        v.tensor_scalar(out=hb[:], in0=hb[:],
                                        scalar1=me.magic[:], scalar2=-128.0,
                                        op0=ALU.subtract, op1=ALU.max)
                        v.drain()
                        iv = v.tensor_scalar(out=hq[:], in0=hb[:], scalar1=127.0,
                                             scalar2=None, op0=ALU.min)
                    assert me.mark(iv, "v") == vm["clp2"]
                carry["hbfree"] = vm["clp2"]

                @blk.sync
                def _(sy):
                    sy.wait_ge(me.semh["v"], vm["clp2"])
                    carry["hqst"] = me.mark(sy.dma_start(
                        out=d["h_q"][tt * 128:tt * 128 + 128, :],
                        in_=hq[:]), "ev0")

            plans[0] = {}
            front(0)
            for k in range(NT + 1):
                if k + 1 < NT:
                    plans[k + 1] = {}
                    front(k + 1)
                if k >= 1:
                    back_b(k - 1)
                if k < NT:
                    back_a(k)
            self.barrier()

        with contextlib.ExitStack() as ph:
            wq2T = self.sb("wq2T%d" % L, [128, 32 * 1024], BF16, ph)
            hqs = self.sb("hqs%d" % L, [128, 2 * 4096], BF16, ph)
            ha = self.sb("ha2%d" % L, [128, 4096], F32, ph)
            hq = self.sb("hq2%d" % L, [128, 4096], BF16, ph)
            brep2 = self.sb("brep2", [128, 1024], F32, ph) if L == 0 else None

            with contextlib.ExitStack() as wph:
                wtmp = self.sb("wtmp3%d" % L, [128, 2 * 4096], F32, wph)
                self.weight_quant_pass(w2name, which=2, dest=wq2T, wtmp=wtmp,
                                       ha=ha, hq=hq, dstv=me.swv2)
                if L == 0:
                    self.replicate_vec("b2_0", brep2, 1024)
                self.barrier()
            hqT = self.sb("hqT%d" % L, [128, 2 * 4096], BF16, ph)

            psO_rel = None
            trm = {}
            evm = {}
            mmk = {}
            ldm = {}

            def emit_ld(k):
                kp = k % 2

                @blk.sync
                def _(sy):
                    if k - 2 >= 0:
                        sy.wait_ge(me.semh["t"], trm[k - 2][3])
                    ldm[k] = me.mark(sy.dma_start(
                        out=hqs[:, kp * 4096:kp * 4096 + 4096],
                        in_=d["h_q"][k * 128:k * 128 + 128, :]), "hq%d" % kp)

            emit_ld(0)
            for tt in range(NT):
                p = tt % 2
                if tt + 1 < NT:
                    emit_ld(tt + 1)
                evm[tt] = [me.cnt["v"] + b + 1 for b in range(4)]

                @blk.tensor
                def _(t):
                    t.wait_ge(me.semh["hq%d" % p], ldm[tt])
                    marks = []
                    for b in range(4):
                        hf = b % 2
                        if b >= 2:
                            t.wait_ge(me.semh["v"], evm[tt][b - 2])
                        elif tt - 1 >= 0:
                            t.wait_ge(me.semh["v"], evm[tt - 1][b + 2])
                        last = None
                        for k in range(8):
                            hc = b * 8 + k
                            last = me.trans(t, me.psT[:, hf * 1024 + k * 128:
                                                   hf * 1024 + k * 128 + 128],
                                            hqs[:, p * 4096 + hc * 128:
                                                p * 4096 + hc * 128 + 128],
                                            start=(k == 0), stop=(k == 7))
                        marks.append(me.mark(last, "t"))
                    trm[tt] = marks

                @blk.vector
                def _(v):
                    for b in range(4):
                        hf = b % 2
                        v.wait_ge(me.semh["t"], trm[tt][b])
                        if tt - 2 >= 0 and b == 0:
                            v.wait_ge(me.semh["t"], mmk[tt - 2])
                        iv = v.tensor_copy(
                            out=hqT[:, p * 4096 + b * 1024:
                                    p * 4096 + b * 1024 + 1024],
                            in_=me.psT[:, hf * 1024:hf * 1024 + 1024])
                        assert me.mark(iv, "v") == evm[tt][b]

                @blk.tensor
                def _(t):
                    t.wait_ge(me.semh["v"], evm[tt][3])
                    if psO_rel is not None:
                        for sem, val in psO_rel:
                            me.w(t, sem, val)
                    last = None
                    for e in range(2):
                        for hc in range(32):
                            last = t.matmul(
                                me.psO[:, e * 512:e * 512 + 512],
                                hqT[:, p * 4096 + hc * 128:p * 4096 + hc * 128 + 128],
                                wq2T[:, hc * 1024 + e * 512:hc * 1024 + e * 512 + 512],
                                start=(hc == 0), stop=(hc == 31))
                    mmk[tt] = me.mark(last, "t")
                tdone = mmk[tt]

                @blk.vector
                def _(v):
                    me.w(v, "t", tdone)
                    if me.last_postln_free is not None:
                        for sem, val in me.last_postln_free:
                            me.w(v, sem, val)
                    v.drain()
                    v.tensor_tensor(out=me.rsc1[:], in0=me.rs2all[:, tt:tt + 1],
                                    in1=me.swv2[:], op=ALU.mult)
                    v.drain()
                    if L == 0:
                        iv = v.scalar_tensor_tensor(
                            out=ha[:, 0:1024], in0=me.psO[:, 0:1024],
                            scalar=me.rsc1[:], in1=brep2[:],
                            op0=ALU.mult, op1=ALU.add)
                    else:
                        iv = v.tensor_scalar(
                            out=ha[:, 0:1024], in0=me.psO[:, 0:1024],
                            scalar1=me.rsc1[:], scalar2=None,
                            op0=ALU.mult)
                    me.mark(iv, "v")
                epidone = me.cnt["v"]

                self.postln_residual(ha[:, 0:1024], tt, src="xres", dst=dstname)
                psO_rel = [("v", epidone)]
            self.barrier()

    def replicate_vec(self, name, dest, n):
        me, blk, d = self, self.block, self.d

        @blk.sync
        def _(sy):
            me.w(sy, "v")
            me.mark(sy.dma_start(out=me.vec1[0:1, 0:n],
                                 in_=d[name].rearrange("(o n) -> o n", o=1)), "lb")
        lddone = me.cnt["lb"]
        for b in range((n + 511) // 512):
            wdt = min(512, n - b * 512)

            @blk.tensor
            def _(t):
                me.w(t, "lb", lddone)
                me.w(t, "v")
                me.mark(t.matmul(me.psM[:, 0:wdt], me.ones1[0:1, 0:128],
                                 me.vec1[0:1, b * 512:b * 512 + wdt],
                                 start=True, stop=True), "t")
            tdone = me.cnt["t"]

            @blk.vector
            def _(v):
                me.w(v, "t", tdone)
                me.mark(v.tensor_copy(out=dest[:, b * 512:b * 512 + wdt],
                                      in_=me.psM[:, 0:wdt]), "v")

    def weight_quant_pass(self, name, which, dest, wtmp, ha, hq, dstv):
        me, blk, d = self, self.block, self.d
        w = d[name]
        rows, cols = (HID, D) if which == 1 else (D, HID)
        ntile = rows // 128
        nslot = wtmp.shape[1] // cols

        v0 = me.cnt["v"]

        def lsem(it):
            return ("ld", "lb", "lc")[it % 3]
        ldmA = {}

        @blk.sync
        def _(sy):
            me.w(sy, "v")
            for it in range(ntile):
                if it >= nslot:
                    sy.wait_ge(me.semh["v"], v0 + (it - nslot) + 1)
                sl = (it % nslot) * cols
                ldmA[it] = me.mark(sy.dma_start(out=wtmp[:, sl:sl + cols],
                                                in_=w[it * 128:it * 128 + 128, :]),
                                   lsem(it))

        @blk.vector
        def _(v):
            for it in range(ntile):
                v.wait_ge(me.semh[lsem(it)], ldmA[it])
                sl = (it % nslot) * cols
                v.drain()
                v.tensor_reduce(out=me.am[:], in_=wtmp[:, sl:sl + cols],
                                axis=AX.X, op=ALU.add, apply_absolute_value=True)
                v.drain()
                if it == 0:
                    iv = v.tensor_copy(out=me.acc1[:], in_=me.am[:])
                else:
                    iv = v.tensor_tensor(out=me.acc1[:], in0=me.acc1[:],
                                         in1=me.am[:], op=ALU.add)
                v.drain()
                assert me.mark(iv, "v") == v0 + it + 1
        vdone = me.cnt["v"]

        @blk.tensor
        def _(t):
            me.w(t, "v", vdone)
            me.w(t, "s")
            me.mark(t.matmul(me.psM[0:1, 0:1], me.onesc[:], me.acc1[:],
                             start=True, stop=True), "t")
        tdone = me.cnt["t"]

        @blk.vector
        def _(v):
            me.w(v, "t", tdone)
            me.mark(v.tensor_scalar(out=me.sc11[:], in0=me.psM[0:1, 0:1],
                                    scalar1=1.0 / (rows * cols), scalar2=EPS,
                                    op0=ALU.mult, op1=ALU.max), "v")
        vdone = me.cnt["v"]

        @blk.tensor
        def _(t):
            me.w(t, "v", vdone)
            me.mark(t.matmul(me.psM[:, 0:1], me.ones1[0:1, 0:128],
                             me.sc11[0:1, 0:1], start=True, stop=True), "t")
        tdone = me.cnt["t"]

        @blk.vector
        def _(v):
            me.w(v, "t", tdone)
            v.tensor_copy(out=dstv[:], in_=me.psM[:, 0:1])
            v.drain()
            v.reciprocal(out=me.am[:], in_=dstv[:])
            v.drain()
            me.mark(v.nop(), "v")
        recipdone = me.cnt["v"]

        hslot = min(ha.shape[1] // cols, 4)
        qslot = min(hq.shape[1] // cols, 4)
        g = max(1, cols // 1024)
        s0 = me.cnt["s"]
        v0b = me.cnt["v"]
        t0 = me.cnt["t"]
        ldmB = {}

        def sm_t1(it):
            return s0 + 2 * it + 1

        def sm_rnd(it):
            return s0 + 2 * it + 2

        def vm_clip(it):
            return v0b + it * (1 + g) + 1

        def vm_evac(it, bb):
            return v0b + it * (1 + g) + 2 + bb

        def tm(it, bb):
            return t0 + it * g + bb + 1

        @blk.sync
        def _(sy):
            me.w(sy, "v")
            for it in range(ntile):
                if it >= nslot:
                    sy.wait_ge(me.semh["s"], sm_t1(it - nslot))
                sl = (it % nslot) * cols
                ldmB[it] = me.mark(sy.dma_start(out=wtmp[:, sl:sl + cols],
                                                in_=w[it * 128:it * 128 + 128, :]),
                                   lsem(it))

        @blk.scalar
        def _(s):
            s.wait_ge(me.semh["v"], recipdone)
            for it in range(ntile):
                s.wait_ge(me.semh[lsem(it)], ldmB[it])
                if it >= hslot:
                    s.wait_ge(me.semh["v"], vm_clip(it - hslot))
                sl = (it % nslot) * cols
                hs = (it % hslot) * cols
                iv = s.activation(out=ha[:, hs:hs + cols],
                                  in_=wtmp[:, sl:sl + cols], func=AF.Identity,
                                  bias=me.magic[:], scale=me.am[:])
                assert me.mark(iv, "s") == sm_t1(it)
                s.drain()
                iv = s.activation(out=ha[:, hs:hs + cols],
                                  in_=ha[:, hs:hs + cols], func=AF.Identity,
                                  bias=me.nmagic[:], scale=1.0)
                assert me.mark(iv, "s") == sm_rnd(it)
                s.drain()

        @blk.vector
        def _(v):
            for it in range(ntile):
                v.wait_ge(me.semh["s"], sm_rnd(it))
                if it >= qslot:
                    v.wait_ge(me.semh["t"], tm(it - qslot, g - 1))
                hs = (it % hslot) * cols
                qs = (it % qslot) * cols
                v.drain()
                iv = v.tensor_scalar(out=hq[:, qs:qs + cols],
                                     in0=ha[:, hs:hs + cols],
                                     scalar1=-1.0, scalar2=1.0,
                                     op0=ALU.max, op1=ALU.min)
                assert me.mark(iv, "v") == vm_clip(it)
                for bb in range(g):
                    v.wait_ge(me.semh["t"], tm(it, bb))
                    gi = it * g + bb
                    ph2 = gi % 2
                    v.drain()
                    iv = None
                    for k in range(8):
                        cc = bb * 8 + k
                        if which == 1:
                            o = dest[:, cc * HID + it * 128:
                                     cc * HID + it * 128 + 128]
                        else:
                            o = dest[:, cc * 1024 + it * 128:
                                     cc * 1024 + it * 128 + 128]
                        iv = v.tensor_copy(
                            out=o, in_=me.psT[:, ph2 * 1024 + k * 128:
                                              ph2 * 1024 + k * 128 + 128])
                    assert me.mark(iv, "v") == vm_evac(it, bb)

        @blk.tensor
        def _(t):
            for it in range(ntile):
                t.wait_ge(me.semh["v"], vm_clip(it))
                qs = (it % qslot) * cols
                for bb in range(g):
                    gi = it * g + bb
                    ph2 = gi % 2
                    if gi >= 2:
                        pit, pbb = divmod(gi - 2, g)
                        t.wait_ge(me.semh["v"], vm_evac(pit, pbb))
                    last = None
                    for k in range(8):
                        cc = bb * 8 + k
                        last = me.trans(t, me.psT[:, ph2 * 1024 + k * 128:
                                               ph2 * 1024 + k * 128 + 128],
                                        hq[:, qs + cc * 128:qs + cc * 128 + 128],
                                        start=(k == 0), stop=(k == 7))
                    assert me.mark(last, "t") == tm(it, bb)

    def finish(self):
        me, blk = self, self.block
        snap = dict(me.cnt)

        @blk.sync
        def _(sy):
            for s in me.semh:
                sy.wait_ge(me.semh[s], snap[s])


def build_program(phases=("l0a", "l0f", "l1a", "l1f"), debug=False):
    return MK(phases, debug).build()


def _in_maps(kw):
    x = np.ascontiguousarray(np.asarray(kw["L0_attn_x"], np.float32))
    names = {
        "W0": "L0_attn_W", "w1_0": "L0_ffn_w1", "b1_0": "L0_ffn_b1",
        "w2_0": "L0_ffn_w2", "b2_0": "L0_ffn_b2", "gam0": "L0_ffn_gamma",
        "bet0": "L0_ffn_beta", "wq1": "L1_attn_wq", "wkd1": "L1_attn_wk_down",
        "wvd1": "L1_attn_wv_down", "wku1": "L1_attn_wk_up", "wvu1": "L1_attn_wv_up",
        "wo1": "L1_attn_wo", "w1_1": "L1_ffn_w1", "w2_1": "L1_ffn_w2",
    }
    weights = {k: np.ascontiguousarray(np.asarray(kw[v], np.float32))
               for k, v in names.items()}
    return [dict(weights, x=x[b]) for b in range(8)]


def kernel(**kw):
    nc = build_program()
    res = run_bass_kernel_spmd(nc, _in_maps(kw), core_ids=list(range(8)))
    return np.stack([np.asarray(res.results[b]["out"], np.float32)
                     for b in range(8)], axis=0)


# version 26
# speedup vs baseline: 1.0368x; 1.0136x over previous
import contextlib
import numpy as np
import concourse.bass as bass
import concourse.mybir as mybir
from concourse.bass_utils import run_bass_kernel_spmd

F32 = mybir.dt.float32
BF16 = mybir.dt.bfloat16
AF = mybir.ActivationFunctionType
ALU = mybir.AluOpType
AX = mybir.AxisListType

T = 4096
D = 1024
HID = 4096
NT = T // 128
NB = T // 512
EPS = 1e-5
NEG = -1.0e30


def set_dims(t):
    global T, NT, NB
    T, NT, NB = t, t // 128, t // 512


class MK:
    def __init__(self, phases=("l0a", "l0f", "l1a", "l1f"), debug=False):
        self.phases = phases
        self.debug = debug
        nc = bass.Bass("TRN2", target_bir_lowering=False)
        self.nc = nc
        self.cnt = {}
        self._stack = contextlib.ExitStack()
        self.last_postln_v = None
        self.last_postln_free = None
        self.psS_rel = [None, None]
        self.psO_rel = None
        self.psT_rel = [None, None]
        self.ppT_rel = [None, None]

        d = {}
        ei, eo = "ExternalInput", "ExternalOutput"
        d["x"] = nc.dram_tensor("x", [T, D], F32, kind=ei).ap()
        d["W0"] = nc.dram_tensor("W0", [D, 3 * D], F32, kind=ei).ap()
        d["w1_0"] = nc.dram_tensor("w1_0", [HID, D], F32, kind=ei).ap()
        d["b1_0"] = nc.dram_tensor("b1_0", [HID], F32, kind=ei).ap()
        d["w2_0"] = nc.dram_tensor("w2_0", [D, HID], F32, kind=ei).ap()
        d["b2_0"] = nc.dram_tensor("b2_0", [D], F32, kind=ei).ap()
        d["gam0"] = nc.dram_tensor("gam0", [D], F32, kind=ei).ap()
        d["bet0"] = nc.dram_tensor("bet0", [D], F32, kind=ei).ap()
        d["wq1"] = nc.dram_tensor("wq1", [D, D], F32, kind=ei).ap()
        d["wkd1"] = nc.dram_tensor("wkd1", [D, 64], F32, kind=ei).ap()
        d["wvd1"] = nc.dram_tensor("wvd1", [D, 64], F32, kind=ei).ap()
        d["wku1"] = nc.dram_tensor("wku1", [32, 128], F32, kind=ei).ap()
        d["wvu1"] = nc.dram_tensor("wvu1", [32, 128], F32, kind=ei).ap()
        d["wo1"] = nc.dram_tensor("wo1", [D, D], F32, kind=ei).ap()
        d["w1_1"] = nc.dram_tensor("w1_1", [HID, D], F32, kind=ei).ap()
        d["w2_1"] = nc.dram_tensor("w2_1", [D, HID], F32, kind=ei).ap()
        d["out"] = nc.dram_tensor("out", [T, D], F32, kind=eo).ap()
        d["xres"] = nc.dram_tensor("xres", [T, D], F32,
                                   kind=eo if debug else "Internal").ap()
        d["qT_h"] = nc.dram_tensor("qT_h", [D, T], BF16).ap()
        d["v_h"] = nc.dram_tensor("v_h", [T, D], BF16).ap()
        d["h_q"] = nc.dram_tensor("h_q", [T, HID], BF16).ap()
        if debug:
            d["dbg_q"] = nc.dram_tensor("dbg_q", [128, 8 * T], BF16, kind="ExternalOutput").ap()
            d["dbg_k"] = nc.dram_tensor("dbg_k", [128, 2 * T], BF16, kind="ExternalOutput").ap()
            d["dbg_v"] = nc.dram_tensor("dbg_v", [128, 2 * T], BF16, kind="ExternalOutput").ap()
            d["dbg_o"] = nc.dram_tensor("dbg_o", [128, 1024], BF16, kind="ExternalOutput").ap()
        self.d = d

        def sb(name, shape, dt, stack=None):
            return (stack or self._stack).enter_context(nc.sbuf_tensor(name, shape, dt))
        self.sb = sb

        def ps(name, shape, dt):
            return self._stack.enter_context(nc.psum_tensor(name, shape, dt))

        self.identb = sb("identb", [128, 128], BF16)
        self.cmask = sb("cmask", [128, 128], F32)
        self.amask = sb("amask", [128, 128], F32)
        self.ones1 = sb("ones1", [1, 512], F32)
        self.onesc = sb("onesc", [128, 1], F32)
        self.magic = sb("magic", [128, 1], F32)
        self.nmagic = sb("nmagic", [128, 1], F32)
        self.epsv = sb("epsv", [128, 1], F32)
        self.lnt = sb("lnt", [128, 1], F32)
        self.c127 = sb("c127", [128, 1], F32)
        self.st6 = sb("st6", [128, 12], F32)
        self.fst6 = sb("fst6", [128, 24], F32)
        self.fst2 = sb("fst2", [128, 4], F32)
        self.frstd = sb("frstd", [128, 2], F32)
        self.fnb = sb("fnb", [128, 2], F32)
        self.flnt = sb("flnt", [128, 2], F32)
        self.fam = sb("fam", [128, 2], F32)
        self.fsact = sb("fsact", [128, 4], F32)
        self.frsc = sb("frsc", [128, 2], F32)
        self.st2 = sb("st2", [128, 2], F32)
        self.rstd = sb("rstd", [128, 1], F32)
        self.nbias = sb("nbias", [128, 1], F32)
        self.lnsets = [
            (self.st6, self.st2, self.lnt, self.rstd, self.nbias),
            (sb("st6b", [128, 12], F32), sb("st2b", [128, 2], F32),
             sb("lntb", [128, 1], F32), sb("rstdb", [128, 1], F32),
             sb("nbiasb", [128, 1], F32)),
        ]
        self.negmh = sb("negmh", [128, 32], F32)
        self.negM = sb("negM", [128, 4], F32)
        self.fac = sb("fac", [128, 32], F32)
        self.fsum = sb("fsum", [128, 32], F32)
        self.ftmp = sb("ftmp", [128, 32], F32)
        self.rden = sb("rden", [128, 4], F32)
        self.rden8 = sb("rden8", [128, 8], F32)
        self.fam8 = sb("fam8", [128, 8], F32)
        self.am = sb("am", [128, 1], F32)
        self.sact = sb("sact", [128, 1], F32)
        self.rsc1 = sb("rsc1", [128, 1], F32)
        self.rs2all = sb("rs2all", [128, 32], F32)
        self.swv = sb("swv", [128, 1], F32)
        self.swv2 = sb("swv2", [128, 1], F32)
        self.acc1 = sb("acc1", [128, 1], F32)
        self.sc11 = sb("sc11", [1, 1], F32)
        self.vec1 = sb("vec1", [1, 4096], F32)
        self.xa = sb("xa", [128, 2 * 1024], F32)
        self.xst = sb("xst", [128, 2 * 1024], F32)
        self.lnst = sb("lnst", [128, 2 * 1024], F32)

        self.psS = ps("psS", [128, 2048], F32)
        self.psO = ps("psO", [128, 1024], F32)
        self.psT = ps("psT", [128, 2048], BF16)

        class _ColOff:
            def __init__(self, base, off):
                self.base, self.off = base, off

            def __getitem__(self, idx):
                p, c = idx
                return self.base[p, c.start + self.off:c.stop + self.off]

        self.psM = _ColOff(self.psS, 1024)
        self.psN = _ColOff(self.psS, 1536)

        self.semh = {}
        for s in ("ld", "st", "v", "s", "t", "g", "b", "xa0", "xa1", "xs0", "xs1",
                  "qt0", "qt1", "hq0", "hq1", "ev0", "ev1", "sx0", "sx1", "lb",
                  "lc"):
            self.semh[s] = self._stack.enter_context(nc.semaphore(s))
            self.cnt[s] = 0

        self.block = self._stack.enter_context(nc.Block())

    def mark(self, instr, s, n=None):
        if n is None:
            n = 1 if s in ("v", "s", "t", "g", "b") else 16
        self.cnt[s] += n
        instr.then_inc(self.semh[s], n)
        return self.cnt[s]

    def w(self, eng, s, v=None):
        eng.wait_ge(self.semh[s], self.cnt[s] if v is None else v)

    def barrier(self):
        blk, me = self.block, self

        @blk.sync
        def _(e):
            me.mark(e.nop(), "b")
        snap = dict(me.cnt)
        for sec in (blk.sync, blk.vector, blk.scalar, blk.tensor, blk.gpsimd):
            @sec
            def _(e):
                for s in me.semh:
                    e.wait_ge(me.semh[s], snap[s])
        self.psS_rel = [None, None]
        self.psO_rel = None
        self.psT_rel = [None, None]
        self.ppT_rel = [None, None]
        self.last_postln_v = None

    def trans(self, te, out_ap, in_ap, start, stop):
        return te.matmul(out_ap, in_ap, self.identb[:], is_transpose=True,
                         start=start, stop=stop, skip_group_check=True)

    def ln_stats(self, x_ap):
        me, blk = self, self.block

        @blk.vector
        def _(v):
            v.drain()
            v.bn_stats(out=me.st6[:, 0:6], in_=x_ap[:, 0:512])
            v.bn_stats(out=me.st6[:, 6:12], in_=x_ap[:, 512:1024])
            v.drain()
            me.mark(v.bn_aggr(out=me.st2[:], in_=me.st6[:]), "v")
        vdone = me.cnt["v"]

        @blk.scalar
        def _(s):
            me.w(s, "v", vdone)
            s.activation(out=me.lnt[:], in_=me.st2[:, 1:2], func=AF.Ln,
                         bias=me.epsv[:], scale=1.0)
            s.drain()
            me.mark(s.activation(out=me.rstd[:], in_=me.lnt[:], func=AF.Exp,
                                 bias=0.0, scale=-0.5), "s")
        sdone = me.cnt["s"]

        @blk.vector
        def _(v):
            me.w(v, "s", sdone)
            me.mark(v.scalar_tensor_tensor(out=me.nbias[:], in0=me.st2[:, 0:1],
                                           scalar=-1.0, in1=me.rstd[:],
                                           op0=ALU.mult, op1=ALU.mult), "v")
        return me.cnt["v"]

    def round_(self, v, ap):
        v.tensor_scalar(out=ap, in0=ap, scalar1=self.magic[:], scalar2=None,
                        op0=ALU.add)
        v.drain()
        v.tensor_scalar(out=ap, in0=ap, scalar1=self.magic[:], scalar2=None,
                        op0=ALU.subtract)
        v.drain()

    def build(self):
        self.setup_consts()
        if "l0a" in self.phases:
            self.l0_attn()
        if "l0f" in self.phases:
            self.ffn(0)
        if "l1a" in self.phases:
            self.l1_attn()
        if "l1f" in self.phases:
            self.ffn(1)
        self.finish()
        self._stack.close()
        return self.nc

    def setup_consts(self):
        me, blk = self, self.block

        @blk.vector
        def _(v):
            v.memset(me.identb[:], 0.0)
            v.memset(me.cmask[:], 0.0)
            v.memset(me.amask[:], 0.0)
            v.memset(me.ones1[:], 1.0)
            v.memset(me.onesc[:], 1.0)
            v.memset(me.c127[:], 127.0)
            v.memset(me.epsv[:], EPS)
            me.mark(v.memset(me.magic[:], 16777216.0), "v")
        vdone = me.cnt["v"]

        @blk.gpsimd
        def _(g):
            me.w(g, "v", vdone)
            g.affine_select(out=me.identb[:], in_=me.identb[:], pattern=[[-1, 128]],
                            compare_op=ALU.not_equal, fill=1.0, channel_multiplier=1)
            g.affine_select(out=me.cmask[:], in_=me.cmask[:], pattern=[[-1, 128]],
                            compare_op=ALU.is_ge, fill=NEG, channel_multiplier=1)
            me.mark(g.affine_select(out=me.amask[:], in_=me.amask[:],
                                    pattern=[[1, 128]], compare_op=ALU.is_ge,
                                    fill=NEG, channel_multiplier=-1), "g")
        gdone = me.cnt["g"]

        @blk.vector
        def _(v):
            me.w(v, "g", gdone)
            v.tensor_scalar(out=me.magic[:], in0=me.magic[:], scalar1=1.0,
                            scalar2=16777216.0, op0=ALU.add, op1=ALU.subtract)
            v.drain()
            v.tensor_scalar(out=me.magic[:], in0=me.magic[:],
                            scalar1=(1.5 * 2.0**52 - 1.5 * 2.0**23),
                            scalar2=1.5 * 2.0**23,
                            op0=ALU.mult, op1=ALU.add)
            v.drain()
            me.mark(v.tensor_scalar(out=me.nmagic[:], in0=me.magic[:],
                                    scalar1=-1.0, scalar2=None, op0=ALU.mult), "v")
        self.barrier()

    def ln_tile_to_xlnT(self, x_ap, ti_in_block, lnb, xlnT, tt=0, lm=None,
                        xwait=None):
        me, blk = self, self.block
        if lm is None:
            lm = {k: {} for k in ("bn", "rstd", "nb", "nrm", "trans", "evac")}
        p2 = tt % 2
        st6, st2, lnt, rstd, nbias = me.lnsets[p2]

        @blk.vector
        def _(v):
            if xwait is not None:
                v.wait_ge(me.semh[xwait[0]], xwait[1])
            if tt - 2 in lm["rstd"]:
                v.wait_ge(me.semh["s"], lm["rstd"][tt - 2])
            v.drain()
            v.bn_stats(out=st6[:, 0:6], in_=x_ap[:, 0:512])
            v.bn_stats(out=st6[:, 6:12], in_=x_ap[:, 512:1024])
            v.drain()
            lm["bn"][tt] = me.mark(v.bn_aggr(out=st2[:], in_=st6[:]), "v")

        @blk.scalar
        def _(s):
            s.wait_ge(me.semh["v"], lm["bn"][tt])
            s.activation(out=lnt[:], in_=st2[:, 1:2], func=AF.Ln,
                         bias=me.epsv[:], scale=1.0)
            s.drain()
            lm["rstd"][tt] = me.mark(
                s.activation(out=rstd[:], in_=lnt[:], func=AF.Exp,
                             bias=0.0, scale=-0.5), "s")

        @blk.vector
        def _(v):
            v.wait_ge(me.semh["s"], lm["rstd"][tt])
            lm["nb"][tt] = me.mark(v.scalar_tensor_tensor(
                out=nbias[:], in0=st2[:, 0:1], scalar=-1.0, in1=rstd[:],
                op0=ALU.mult, op1=ALU.mult), "v")

        @blk.scalar
        def _(s):
            s.wait_ge(me.semh["v"], lm["nb"][tt])
            if tt - 2 in lm["trans"]:
                s.wait_ge(me.semh["t"], lm["trans"][tt - 2])
            lm["nrm"][tt] = me.mark(s.activation(
                out=lnb[:, p2 * 1024:p2 * 1024 + 1024], in_=x_ap,
                func=AF.Identity, bias=nbias[:], scale=rstd[:]), "s")

        @blk.tensor
        def _(t):
            t.wait_ge(me.semh["s"], lm["nrm"][tt])
            if tt - 2 in lm["evac"]:
                t.wait_ge(me.semh["v"], lm["evac"][tt - 2])
            last = None
            for dc in range(8):
                last = me.trans(t, me.psT[:, p2 * 1024 + dc * 128:
                                       p2 * 1024 + dc * 128 + 128],
                                lnb[:, p2 * 1024 + dc * 128:
                                    p2 * 1024 + dc * 128 + 128],
                                start=(dc == 0), stop=(dc == 7))
            lm["trans"][tt] = me.mark(last, "t")

        @blk.vector
        def _(v):
            v.wait_ge(me.semh["t"], lm["trans"][tt])
            iv = None
            for dc in range(8):
                iv = v.tensor_copy(
                    out=xlnT[:, dc * 512 + ti_in_block * 128:
                             dc * 512 + ti_in_block * 128 + 128],
                    in_=me.psT[:, p2 * 1024 + dc * 128:
                               p2 * 1024 + dc * 128 + 128])
            lm["evac"][tt] = me.mark(iv, "v")
        return me.cnt["v"]

    def postln_residual(self, y_ap, tt, src, dst):
        me, blk, d = self, self.block, self.d
        half = tt % 2
        v0, s0 = me.cnt["v"], me.cnt["s"]
        vm = {"bn": v0 + 1, "nb": v0 + 2, "add": v0 + 3}
        sm = {"st": s0 + 1, "nrm": s0 + 2}

        @blk.sync
        def _(sy):
            me.w(sy, "v")
            me.w(sy, "sx%d" % half)
            me.mark(sy.dma_start(out=me.xst[:, half * 1024:half * 1024 + 1024],
                                 in_=d[src][tt * 128:tt * 128 + 128, :]),
                    "xs%d" % half)
        lddone = me.cnt["xs%d" % half]

        @blk.vector
        def _(v):
            v.drain()
            v.bn_stats(out=me.st6[:, 0:6], in_=y_ap[:, 0:512])
            v.bn_stats(out=me.st6[:, 6:12], in_=y_ap[:, 512:1024])
            v.drain()
            assert me.mark(v.bn_aggr(out=me.st2[:], in_=me.st6[:]), "v") == vm["bn"]
            v.wait_ge(me.semh["s"], sm["st"])
            iv = v.scalar_tensor_tensor(out=me.nbias[:], in0=me.st2[:, 0:1],
                                        scalar=-1.0, in1=me.rstd[:],
                                        op0=ALU.mult, op1=ALU.mult)
            assert me.mark(iv, "v") == vm["nb"]
            v.wait_ge(me.semh["s"], sm["nrm"])
            v.wait_ge(me.semh["xs%d" % half], lddone)
            iv = v.tensor_tensor(out=me.xst[:, half * 1024:half * 1024 + 1024],
                                 in0=me.xst[:, half * 1024:half * 1024 + 1024],
                                 in1=me.lnst[:, half * 1024:half * 1024 + 1024],
                                 op=ALU.add)
            assert me.mark(iv, "v") == vm["add"]

        @blk.scalar
        def _(s):
            s.wait_ge(me.semh["v"], vm["bn"])
            s.activation(out=me.lnt[:], in_=me.st2[:, 1:2], func=AF.Ln,
                         bias=me.epsv[:], scale=1.0)
            s.drain()
            iv = s.activation(out=me.rstd[:], in_=me.lnt[:], func=AF.Exp,
                              bias=0.0, scale=-0.5)
            assert me.mark(iv, "s") == sm["st"]
            s.wait_ge(me.semh["v"], vm["nb"])
            iv = s.activation(out=me.lnst[:, half * 1024:half * 1024 + 1024],
                              in_=y_ap, func=AF.Identity,
                              bias=me.nbias[:], scale=me.rstd[:])
            assert me.mark(iv, "s") == sm["nrm"]
        self.last_postln_v = vm["add"]
        self.last_postln_free = (("v", vm["bn"]), ("s", sm["nrm"]))

        @blk.sync
        def _(sy):
            me.w(sy, "v", self.last_postln_v)
            me.mark(sy.dma_start(out=d[dst][tt * 128:tt * 128 + 128, :],
                                 in_=me.xst[:, half * 1024:half * 1024 + 1024]),
                    "sx%d" % half)

    def flash_rows(self, rows, nchunk, kT_ap, klen, diag, zerofix, pre=None,
                   normalize=True):
        me, blk = self, self.block
        CW = 1024
        nb = (klen + CW - 1) // CW
        R = len(rows)
        N = R * nb
        slots = [(me.psS, 0), (me.psS, 1024)]

        def step_rc(step):
            return step % R, step // R

        t0, v0, s0 = me.cnt["t"], me.cnt["v"], me.cnt["s"]
        t_mark = [t0 + k + 1 for k in range(N)]
        s_mark = [s0 + k + 1 for k in range(N)]
        v_items = []
        for step in range(N):
            if zerofix and step >= 2:
                v_items.append(("stt", step - 2))
            v_items.append(("red", step))
        if zerofix:
            for j in range(max(0, N - 2), N):
                v_items.append(("stt", j))
        red_mark, stt_mark = {}, {}
        for pos, (kind, j) in enumerate(v_items):
            (red_mark if kind == "red" else stt_mark)[j] = v0 + pos + 1

        def release(step):
            return ("v", stt_mark[step]) if zerofix else ("s", s_mark[step])

        @blk.tensor
        def _(t):
            for sem, val in (pre or []):
                t.wait_ge(me.semh[sem], val)
            for step in range(N):
                r, c = step_rc(step)
                sl = step % 2
                wdt = min(CW, klen - CW * c)
                ten, off = slots[sl]
                if step >= 2:
                    sem, val = release(step - 2)
                    me.w(t, sem, val)
                elif me.psS_rel[sl] is not None:
                    sem, val = me.psS_rel[sl]
                    me.w(t, sem, val)
                qap = rows[r][0]
                last = None
                for sub in range(0, wdt, 512):
                    sw = min(512, wdt - sub)
                    for dc in range(nchunk):
                        last = t.matmul(ten[:, off + sub:off + sub + sw],
                                        qap(dc),
                                        kT_ap(dc, CW * c + sub,
                                              CW * c + sub + sw),
                                        start=(dc == 0),
                                        stop=(dc == nchunk - 1))
                assert me.mark(last, "t") == t_mark[step]
        for k in range(2):
            st = N - 1 - k
            if st >= 0:
                me.psS_rel[st % 2] = release(st)

        @blk.vector
        def _(v):
            for pos, (kind, j) in enumerate(v_items):
                if kind == "red":
                    step = j
                    r, c = step_rc(step)
                    sl = step % 2
                    wdt = min(CW, klen - CW * c)
                    ten, off = slots[sl]
                    v.wait_ge(me.semh["t"], t_mark[step])
                    if step == 0:
                        v.drain()
                        v.memset(me.fsum[:, 0:N], 0.0)
                    if diag[0] == "first" and c == 0:
                        v.tensor_tensor(out=ten[:, off:off + 128],
                                        in0=ten[:, off:off + 128],
                                        in1=diag[1], op=ALU.add)
                        v.drain()
                    if diag[0] == "last" and c == nb - 1:
                        v.tensor_tensor(out=ten[:, off + wdt - 128:off + wdt],
                                        in0=ten[:, off + wdt - 128:off + wdt],
                                        in1=diag[1], op=ALU.add)
                        v.drain()
                    iv = v.reduce_max(out=me.negmh[:, r * nb + c:r * nb + c + 1],
                                      in_=ten[:, off:off + wdt],
                                      axis=AX.X, negate=True)
                    assert me.mark(iv, "v") == red_mark[step]
                else:
                    step = j
                    r, c = step_rc(step)
                    sl = step % 2
                    wdt = min(CW, klen - CW * c)
                    ten, off = slots[sl]
                    pp_row = rows[r][1]
                    v.wait_ge(me.semh["s"], s_mark[step])
                    iv = v.scalar_tensor_tensor(
                        out=pp_row[:, CW * c:CW * c + wdt],
                        in0=ten[:, off:off + wdt], scalar=0.0,
                        in1=pp_row[:, CW * c:CW * c + wdt],
                        op0=ALU.not_equal, op1=ALU.mult)
                    assert me.mark(iv, "v") == stt_mark[step]

        @blk.scalar
        def _(s):
            for step in range(N):
                r, c = step_rc(step)
                sl = step % 2
                wdt = min(CW, klen - CW * c)
                ten, off = slots[sl]
                pp_row = rows[r][1]
                s.wait_ge(me.semh["v"], red_mark[step])
                iv = s.activation(
                    out=pp_row[:, CW * c:CW * c + wdt],
                    in_=ten[:, off:off + wdt], func=AF.Exp,
                    bias=me.negmh[:, r * nb + c:r * nb + c + 1],
                    scale=1.0, accum_out=me.fsum[:, r * nb + c:r * nb + c + 1])
                assert me.mark(iv, "s") == s_mark[step]

        @blk.vector
        def _(v):
            sem, val = me.psS_rel[(N - 1) % 2]
            me.w(v, sem, val)
            v.drain()
            for r in range(R):
                v.tensor_reduce(out=me.negM[:, r:r + 1],
                                in_=me.negmh[:, r * nb:r * nb + nb],
                                axis=AX.X, op=ALU.min)
            v.drain()
            iv = None
            for r in range(R):
                iv = v.tensor_scalar(out=me.fac[:, r * nb:r * nb + nb],
                                     in0=me.negmh[:, r * nb:r * nb + nb],
                                     scalar1=me.negM[:, r:r + 1], scalar2=None,
                                     op0=ALU.subtract)
            me.mark(iv, "v")
        vdone = me.cnt["v"]

        @blk.scalar
        def _(s):
            me.w(s, "v", vdone)
            me.mark(s.activation(out=me.fac[:, 0:N], in_=me.fac[:, 0:N],
                                 func=AF.Exp, bias=0.0, scale=-1.0), "s")
        sdone = me.cnt["s"]

        rc_marks = []

        @blk.vector
        def _(v):
            me.w(v, "s", sdone)
            if normalize:
                v.tensor_tensor(out=me.ftmp[:, 0:N], in0=me.fac[:, 0:N],
                                in1=me.fsum[:, 0:N], op=ALU.mult)
                v.drain()
                for r in range(R):
                    v.tensor_reduce(out=me.rden[:, r:r + 1],
                                    in_=me.ftmp[:, r * nb:r * nb + nb],
                                    axis=AX.X, op=ALU.add)
                v.drain()
                v.reciprocal(out=me.rden[:, 0:R], in_=me.rden[:, 0:R])
                v.drain()
                for r in range(R):
                    v.tensor_scalar(out=me.fac[:, r * nb:r * nb + nb],
                                    in0=me.fac[:, r * nb:r * nb + nb],
                                    scalar1=me.rden[:, r:r + 1], scalar2=None,
                                    op0=ALU.mult)
            v.drain()
            for c in range(nb):
                wdt = min(CW, klen - CW * c)
                iv = None
                for r in range(R):
                    pp_row = rows[r][1]
                    iv = v.tensor_scalar(out=pp_row[:, CW * c:CW * c + wdt],
                                         in0=pp_row[:, CW * c:CW * c + wdt],
                                         scalar1=me.fac[:, r * nb + c:r * nb + c + 1],
                                         scalar2=None, op0=ALU.mult)
                rc_marks.append(me.mark(iv, "v"))
        return rc_marks

    def l0_attn(self):
        me, blk, d = self, self.block, self.d
        with contextlib.ExitStack() as outer:
            kT = self.sb("kT", [128, 8 * T], BF16, outer)

            with contextlib.ExitStack() as ph:
                wbig = self.sb("wbig", [128, 8 * 3072], BF16, ph)
                xlnT = self.sb("xlnT", [128, 8 * 512], BF16, ph)
                wtmp = self.sb("wtmp", [128, 2 * 3072], F32, ph)
                lnb = self.sb("lnb", [128, 2 * 1024], BF16, ph)
                evb = self.sb("evb", [128, 2 * 512], BF16, ph)

                for dc in range(8):
                    half = dc % 2

                    @blk.sync
                    def _(sy):
                        me.w(sy, "v")
                        me.mark(sy.dma_start(
                            out=wtmp[:, half * 3072:half * 3072 + 3072],
                            in_=d["W0"][dc * 128:dc * 128 + 128, :]), "ld")
                    lddone = me.cnt["ld"]

                    @blk.vector
                    def _(v):
                        me.w(v, "ld", lddone)
                        me.mark(v.tensor_copy(
                            out=wbig[:, dc * 3072:dc * 3072 + 3072],
                            in_=wtmp[:, half * 3072:half * 3072 + 3072]), "v")

                lm = {k: {} for k in ("bn", "rstd", "nb", "nrm", "trans",
                                      "evac")}
                for B in range(NB):
                    for ti in range(4):
                        tt = B * 4 + ti
                        half = tt % 2

                        @blk.sync
                        def _(sy):
                            if tt - 2 in lm["nrm"]:
                                sy.wait_ge(me.semh["s"], lm["nrm"][tt - 2])
                                sy.wait_ge(me.semh["v"], lm["bn"][tt - 2])
                            else:
                                me.w(sy, "s")
                            me.mark(sy.dma_start(
                                out=me.xa[:, half * 1024:half * 1024 + 1024],
                                in_=d["x"][tt * 128:tt * 128 + 128, :]),
                                "xa%d" % half)
                        lddone = me.cnt["xa%d" % half]
                        self.ln_tile_to_xlnT(
                            me.xa[:, half * 1024:half * 1024 + 1024], ti, lnb,
                            xlnT, tt=tt, lm=lm, xwait=("xa%d" % half, lddone))

                    for region in ("q", "k"):
                        base = 0 if region == "q" else 1024
                        for fq in range(8):
                            half = fq % 2

                            @blk.tensor
                            def _(t):
                                me.w(t, "v")
                                last = None
                                for dc in range(8):
                                    last = t.matmul(
                                        me.psS[:, half * 512:half * 512 + 512],
                                        wbig[:, dc * 3072 + base + fq * 128:
                                             dc * 3072 + base + fq * 128 + 128],
                                        xlnT[:, dc * 512:dc * 512 + 512],
                                        start=(dc == 0), stop=(dc == 7))
                                me.mark(last, "t")
                            tdone = me.cnt["t"]

                            @blk.vector
                            def _(v):
                                me.w(v, "t", tdone)
                                if region == "k":
                                    me.mark(v.tensor_copy(
                                        out=kT[:, fq * T + B * 512:
                                               fq * T + B * 512 + 512],
                                        in_=me.psS[:, half * 512:half * 512 + 512]),
                                        "v")
                                else:
                                    me.w(v, "ev%d" % half)
                                    me.mark(v.tensor_scalar(
                                        out=evb[:, half * 512:half * 512 + 512],
                                        in0=me.psS[:, half * 512:half * 512 + 512],
                                        scalar1=1.0 / 32.0, scalar2=None,
                                        op0=ALU.mult), "v")
                            vdone = me.cnt["v"]

                            if region == "q":
                                @blk.sync
                                def _(sy):
                                    me.w(sy, "v", vdone)
                                    me.mark(sy.dma_start(
                                        out=d["qT_h"][fq * 128:fq * 128 + 128,
                                                      B * 512:B * 512 + 512],
                                        in_=evb[:, half * 512:half * 512 + 512]),
                                        "ev%d" % half)

                    for ti in range(4):
                        tt = B * 4 + ti
                        for e in range(2):
                            @blk.tensor
                            def _(t):
                                me.w(t, "v")
                                last = None
                                for dc in range(8):
                                    last = t.matmul(
                                        me.psO[:, e * 512:e * 512 + 512],
                                        xlnT[:, dc * 512 + ti * 128:
                                             dc * 512 + ti * 128 + 128],
                                        wbig[:, dc * 3072 + 2048 + e * 512:
                                             dc * 3072 + 2048 + e * 512 + 512],
                                        start=(dc == 0), stop=(dc == 7))
                                me.mark(last, "t")
                            tdone = me.cnt["t"]

                            @blk.vector
                            def _(v):
                                me.w(v, "t", tdone)
                                me.w(v, "ev%d" % e)
                                me.mark(v.tensor_copy(
                                    out=evb[:, e * 512:e * 512 + 512],
                                    in_=me.psO[:, e * 512:e * 512 + 512]), "v")
                            vdone = me.cnt["v"]

                            @blk.sync
                            def _(sy):
                                me.w(sy, "v", vdone)
                                me.mark(sy.dma_start(
                                    out=d["v_h"][tt * 128:tt * 128 + 128,
                                                 e * 512:e * 512 + 512],
                                    in_=evb[:, e * 512:e * 512 + 512]), "ev%d" % e)
                self.barrier()

            with contextlib.ExitStack() as ph:
                vres = self.sb("vres", [128, NT * 1024], BF16, ph)
                pp = self.sb("pp", [128, T], BF16, ph)
                ppT = self.sb("ppT", [128, 2 * 512], BF16, ph)
                qt = self.sb("qt", [128, 2 * 1024], BF16, ph)

                for j in range(NT):
                    @blk.sync
                    def _(sy):
                        me.mark(sy.dma_start(
                            out=vres[:, j * 1024:j * 1024 + 1024],
                            in_=d["v_h"][j * 128:j * 128 + 128, :]), "lb")

                qt_ld = {}
                qS_last = {}

                def emit_qt_ld(i):
                    qhalf = i % 2

                    @blk.sync
                    def _(sy):
                        if i - 1 >= 0:
                            sy.wait_ge(me.semh["t"], qS_last[i - 1])
                        else:
                            me.w(sy, "t")
                        qt_ld[i] = me.mark(sy.dma_start(
                            out=qt[:, qhalf * 1024:qhalf * 1024 + 1024]
                            .rearrange("p (k c) -> p k c", k=8),
                            in_=d["qT_h"][:, i * 128:i * 128 + 128]
                            .rearrange("(k p) c -> p k c", k=8)),
                            "qt%d" % qhalf)

                emit_qt_ld(0)
                for i in range(NT):
                    qhalf = i % 2
                    k0 = 128 * i
                    klen = T - k0

                    qS_last[i] = me.cnt["t"] + (klen + 1023) // 1024
                    if i + 1 < NT:
                        emit_qt_ld(i + 1)

                    rc = self.flash_rows(
                        rows=[(lambda dc: qt[:, qhalf * 1024 + dc * 128:
                                             qhalf * 1024 + dc * 128 + 128],
                               pp[:, 0:klen])],
                        nchunk=8,
                        kT_ap=lambda dc, a, b: kT[:, dc * T + k0 + a:dc * T + k0 + b],
                        klen=klen, diag=("first", me.amask[:]), zerofix=True,
                        pre=[("qt%d" % qhalf, qt_ld[i])], normalize=False)

                    nk = klen // 128
                    ngrp = (nk + 3) // 4
                    t0 = me.cnt["t"]
                    v0 = me.cnt["v"]
                    t_items = []
                    for grp in range(ngrp):
                        t_items.append(("tr", grp))
                        if grp >= 1:
                            t_items.append(("av", grp - 1))
                    t_items.append(("av", ngrp - 1))
                    tr_mark, av_mark = {}, {}
                    for pos, (kind, g2) in enumerate(t_items):
                        (tr_mark if kind == "tr" else av_mark)[g2] = t0 + pos + 1
                    ev_mark = {grp: v0 + grp + 1 for grp in range(ngrp)}

                    @blk.tensor
                    def _(t):
                        me.w(t, "lb")
                        for kind, grp in t_items:
                            gw = min(4, nk - grp * 4)
                            ph2 = grp % 2
                            if kind == "tr":
                                t.wait_ge(me.semh["v"], rc[grp // 2])
                                if grp < 2:
                                    if me.psT_rel[ph2] is not None:
                                        sem, val = me.psT_rel[ph2]
                                        me.w(t, sem, val)
                                else:
                                    t.wait_ge(me.semh["v"], ev_mark[grp - 2])
                                last = None
                                for jj in range(gw):
                                    last = me.trans(
                                        t, me.psT[:, ph2 * 1024 + jj * 128:
                                                  ph2 * 1024 + jj * 128 + 128],
                                        pp[:, (grp * 4 + jj) * 128:
                                           (grp * 4 + jj) * 128 + 128],
                                        start=(jj == 0), stop=(jj == gw - 1))
                                assert me.mark(last, "t") == tr_mark[grp]
                            else:
                                t.wait_ge(me.semh["v"], ev_mark[grp])
                                if grp == 0 and me.psO_rel is not None:
                                    sem, val = me.psO_rel
                                    me.w(t, sem, val)
                                last = None
                                for jj in range(gw):
                                    j = grp * 4 + jj
                                    jabs = i + j
                                    for e in range(2):
                                        last = t.matmul(
                                            me.psO[:, e * 512:e * 512 + 512],
                                            ppT[:, ph2 * 512 + jj * 128:
                                                ph2 * 512 + jj * 128 + 128],
                                            vres[:, jabs * 1024 + e * 512:
                                                 jabs * 1024 + e * 512 + 512],
                                            start=(j == 0), stop=(j == nk - 1))
                                assert me.mark(last, "t") == av_mark[grp]

                    @blk.vector
                    def _(v):
                        for grp in range(ngrp):
                            gw = min(4, nk - grp * 4)
                            ph2 = grp % 2
                            v.wait_ge(me.semh["t"], tr_mark[grp])
                            if grp >= 2:
                                v.wait_ge(me.semh["t"], av_mark[grp - 2])
                            elif me.ppT_rel[ph2] is not None:
                                sem, val = me.ppT_rel[ph2]
                                me.w(v, sem, val)
                            iv = v.tensor_copy(
                                out=ppT[:, ph2 * 512:ph2 * 512 + gw * 128],
                                in_=me.psT[:, ph2 * 1024:ph2 * 1024 + gw * 128])
                            assert me.mark(iv, "v") == ev_mark[grp]
                    for g2 in range(max(0, ngrp - 2), ngrp):
                        me.psT_rel[g2 % 2] = ("v", ev_mark[g2])
                        me.ppT_rel[g2 % 2] = ("t", av_mark[g2])
                    tdone = me.cnt["t"]

                    @blk.vector
                    def _(v):
                        me.w(v, "t", tdone)
                        me.mark(v.nop(), "v")

                    self.postln_residual(me.psO[:, 0:1024], i, src="x", dst="xres")
                    me.psO_rel = ("v", me.last_postln_v)
                self.barrier()

    def l1_attn(self):
        me, blk, d = self, self.block, self.d
        with contextlib.ExitStack() as outer:
            qTb = self.sb("qTb", [128, 8 * T], BF16, outer)
            kl1 = self.sb("kl1", [128, 2 * T], BF16, outer)
            vl1 = self.sb("vl1", [128, 2 * T], BF16, outer)
            wol = self.sb("wol", [128, 8 * 1024], BF16, outer)

            with contextlib.ExitStack() as ph:
                wql = self.sb("wql", [128, 8 * 1024], BF16, ph)
                wkv = self.sb("wkv", [128, 8 * 128], BF16, ph)
                wu = self.sb("wu", [32, 256], BF16, ph)
                xlnT = self.sb("xlnT1", [128, 8 * 512], BF16, ph)
                wtmp = self.sb("wtmp1", [128, 2 * 1024], F32, ph)
                lnb = self.sb("lnb1", [128, 2 * 1024], BF16, ph)
                kst = [self.sb(f"kst{g}", [32, 512], BF16, ph) for g in range(2)]
                vst = [self.sb(f"vst{g}", [32, 512], BF16, ph) for g in range(2)]

                for name, dest, cols in (("wq1", wql, 1024), ("wo1", wol, 1024)):
                    for dc in range(8):
                        half = dc % 2

                        @blk.sync
                        def _(sy):
                            me.w(sy, "v")
                            me.mark(sy.dma_start(
                                out=wtmp[:, half * 1024:half * 1024 + cols],
                                in_=d[name][dc * 128:dc * 128 + 128, :]), "ld")
                        lddone = me.cnt["ld"]

                        @blk.vector
                        def _(v):
                            me.w(v, "ld", lddone)
                            me.mark(v.tensor_copy(
                                out=dest[:, dc * 1024:dc * 1024 + cols],
                                in_=wtmp[:, half * 1024:half * 1024 + cols]), "v")
                for dc in range(8):
                    half = dc % 2

                    @blk.sync
                    def _(sy):
                        me.w(sy, "v")
                        me.mark(sy.dma_start(
                            out=wtmp[:, half * 1024:half * 1024 + 64],
                            in_=d["wkd1"][dc * 128:dc * 128 + 128, :]), "ld")
                        me.mark(sy.dma_start(
                            out=wtmp[:, half * 1024 + 64:half * 1024 + 128],
                            in_=d["wvd1"][dc * 128:dc * 128 + 128, :]), "ld")
                    lddone = me.cnt["ld"]

                    @blk.vector
                    def _(v):
                        me.w(v, "ld", lddone)
                        me.mark(v.tensor_copy(
                            out=wkv[:, dc * 128:dc * 128 + 128],
                            in_=wtmp[:, half * 1024:half * 1024 + 128]), "v")

                @blk.sync
                def _(sy):
                    me.w(sy, "v")
                    me.mark(sy.dma_start(out=wtmp[0:32, 0:128], in_=d["wku1"][:]), "ld")
                    me.mark(sy.dma_start(out=wtmp[0:32, 128:256], in_=d["wvu1"][:]),
                            "ld")
                lddone = me.cnt["ld"]

                @blk.vector
                def _(v):
                    me.w(v, "ld", lddone)
                    me.mark(v.tensor_copy(out=wu[:], in_=wtmp[0:32, 0:256]), "v")

                scl = float(1.0 / np.sqrt(128.0))
                lm = {k: {} for k in ("bn", "rstd", "nb", "nrm", "trans",
                                      "evac")}
                for B in range(NB):
                    for ti in range(4):
                        tt = B * 4 + ti
                        half = tt % 2

                        @blk.sync
                        def _(sy):
                            if tt - 2 in lm["nrm"]:
                                sy.wait_ge(me.semh["s"], lm["nrm"][tt - 2])
                                sy.wait_ge(me.semh["v"], lm["bn"][tt - 2])
                            else:
                                me.w(sy, "s")
                            me.mark(sy.dma_start(
                                out=me.xa[:, half * 1024:half * 1024 + 1024],
                                in_=d["xres"][tt * 128:tt * 128 + 128, :]),
                                "xa%d" % half)
                        lddone = me.cnt["xa%d" % half]
                        self.ln_tile_to_xlnT(
                            me.xa[:, half * 1024:half * 1024 + 1024], ti, lnb,
                            xlnT, tt=tt, lm=lm, xwait=("xa%d" % half, lddone))

                    for h in range(8):
                        half = h % 2

                        @blk.tensor
                        def _(t):
                            me.w(t, "v")
                            last = None
                            for dc in range(8):
                                last = t.matmul(
                                    me.psS[:, half * 512:half * 512 + 512],
                                    wql[:, dc * 1024 + h * 128:
                                        dc * 1024 + h * 128 + 128],
                                    xlnT[:, dc * 512:dc * 512 + 512],
                                    start=(dc == 0), stop=(dc == 7))
                            me.mark(last, "t")
                        tdone = me.cnt["t"]

                        @blk.vector
                        def _(v):
                            me.w(v, "t", tdone)
                            me.mark(v.tensor_scalar(
                                out=qTb[:, h * T + B * 512:h * T + B * 512 + 512],
                                in0=me.psS[:, half * 512:half * 512 + 512],
                                scalar1=scl, scalar2=None, op0=ALU.mult), "v")

                    @blk.tensor
                    def _(t):
                        me.w(t, "v")
                        last = None
                        for dc in range(8):
                            last = t.matmul(me.psM[:, 0:512],
                                            wkv[:, dc * 128:dc * 128 + 128],
                                            xlnT[:, dc * 512:dc * 512 + 512],
                                            start=(dc == 0), stop=(dc == 7))
                        me.mark(last, "t")
                    tdone = me.cnt["t"]

                    @blk.vector
                    def _(v):
                        me.w(v, "t", tdone)
                        v.tensor_copy(out=kst[0][:], in_=me.psM[0:32, 0:512])
                        v.tensor_copy(out=kst[1][:], in_=me.psM[32:64, 0:512])
                        v.tensor_copy(out=vst[0][:], in_=me.psM[64:96, 0:512])
                        me.mark(v.tensor_copy(out=vst[1][:], in_=me.psM[96:128, 0:512]),
                                "v")
                    vdone = me.cnt["v"]

                    for g in range(2):
                        @blk.tensor
                        def _(t):
                            me.w(t, "v")
                            me.mark(t.matmul(me.psN[:, 0:512], wu[0:32, 0:128],
                                             kst[g][:], start=True, stop=True), "t")
                        tdone = me.cnt["t"]

                        @blk.vector
                        def _(v):
                            me.w(v, "t", tdone)
                            me.mark(v.tensor_copy(
                                out=kl1[:, g * T + B * 512:g * T + B * 512 + 512],
                                in_=me.psN[:, 0:512]), "v")

                        for ti in range(4):
                            @blk.tensor
                            def _(t):
                                me.w(t, "v")
                                me.mark(t.matmul(
                                    me.psN[:, 0:128],
                                    vst[g][:, ti * 128:ti * 128 + 128],
                                    wu[0:32, 128:256], start=True, stop=True), "t")
                            tdone = me.cnt["t"]
                            col = (g * NT + B * 4 + ti) * 128

                            @blk.vector
                            def _(v):
                                me.w(v, "t", tdone)
                                me.mark(v.tensor_copy(out=vl1[:, col:col + 128],
                                                      in_=me.psN[:, 0:128]), "v")
                if self.debug:
                    @blk.sync
                    def _(sy):
                        me.w(sy, "v")
                        me.mark(sy.dma_start(out=d["dbg_q"][:], in_=qTb[:]), "lb")
                        me.mark(sy.dma_start(out=d["dbg_k"][:], in_=kl1[:]), "lb")
                        me.mark(sy.dma_start(out=d["dbg_v"][:], in_=vl1[:]), "lb")
                self.barrier()

            with contextlib.ExitStack() as ph:
                pp = self.sb("pp4", [128, 4 * T], BF16, ph)
                ppT = self.sb("ppT1", [128, 2 * 512], BF16, ph)
                oT = self.sb("oT", [128, 1024], BF16, ph)

                for i in range(NT):
                    klen = 128 * (i + 1)
                    nk = i + 1
                    for g in range(2):
                        def qap_for(h):
                            return lambda dc: qTb[:, h * T + i * 128:
                                                  h * T + i * 128 + 128]
                        rc = self.flash_rows(
                            rows=[(qap_for(4 * g + r), pp[:, r * T:r * T + klen])
                                  for r in range(4)],
                            nchunk=1,
                            kT_ap=lambda dc, a, b: kl1[:, g * T + a:g * T + b],
                            klen=klen, diag=("last", me.cmask[:]), zerofix=False)

                        t0 = me.cnt["t"]
                        v0 = me.cnt["v"]
                        t_items = []
                        for j in range(nk):
                            t_items.append(("tr", j))
                            if j >= 1:
                                t_items.append(("av", j - 1))
                        t_items.append(("av", nk - 1))
                        tr_mark, av_mark = {}, {}
                        for pos, (kind, j2) in enumerate(t_items):
                            (tr_mark if kind == "tr" else av_mark)[j2] = t0 + pos + 1
                        ev_mark = {j: v0 + j + 1 for j in range(nk)}

                        @blk.tensor
                        def _(t):
                            for kind, j in t_items:
                                ph2 = j % 2
                                if kind == "tr":
                                    t.wait_ge(me.semh["v"], rc[j // 8])
                                    if j < 2:
                                        if me.psT_rel[ph2] is not None:
                                            sem, val = me.psT_rel[ph2]
                                            me.w(t, sem, val)
                                    else:
                                        t.wait_ge(me.semh["v"], ev_mark[j - 2])
                                    last = None
                                    for r in range(4):
                                        last = me.trans(
                                            t, me.psT[:, ph2 * 1024 + r * 128:
                                                      ph2 * 1024 + r * 128 + 128],
                                            pp[:, r * T + j * 128:
                                               r * T + j * 128 + 128],
                                            start=(r == 0), stop=(r == 3))
                                    assert me.mark(last, "t") == tr_mark[j]
                                else:
                                    t.wait_ge(me.semh["v"], ev_mark[j])
                                    if j == 0 and g == 0 and me.psO_rel is not None:
                                        sem, val = me.psO_rel
                                        me.w(t, sem, val)
                                    iv = t.matmul(
                                        me.psO[:, g * 512:g * 512 + 512],
                                        vl1[:, (g * NT + j) * 128:
                                            (g * NT + j) * 128 + 128],
                                        ppT[:, ph2 * 512:ph2 * 512 + 512],
                                        start=(j == 0), stop=(j == nk - 1))
                                    assert me.mark(iv, "t") == av_mark[j]

                        @blk.vector
                        def _(v):
                            for j in range(nk):
                                ph2 = j % 2
                                v.wait_ge(me.semh["t"], tr_mark[j])
                                if j >= 2:
                                    v.wait_ge(me.semh["t"], av_mark[j - 2])
                                elif me.ppT_rel[ph2] is not None:
                                    sem, val = me.ppT_rel[ph2]
                                    me.w(v, sem, val)
                                iv = v.tensor_copy(
                                    out=ppT[:, ph2 * 512:ph2 * 512 + 512],
                                    in_=me.psT[:, ph2 * 1024:ph2 * 1024 + 512])
                                assert me.mark(iv, "v") == ev_mark[j]
                        for j2 in range(max(0, nk - 2), nk):
                            me.psT_rel[j2 % 2] = ("v", ev_mark[j2])
                            me.ppT_rel[j2 % 2] = ("t", av_mark[j2])
                    tdone = me.cnt["t"]

                    @blk.vector
                    def _(v):
                        me.w(v, "t", tdone)
                        if self.debug:
                            me.w(v, "lb")
                        me.mark(v.tensor_copy(out=oT[:], in_=me.psO[:, 0:1024]), "v")
                    vdone = me.cnt["v"]
                    me.psO_rel = ("v", vdone)
                    if self.debug and i == 0:
                        @blk.sync
                        def _(sy):
                            me.w(sy, "v", vdone)
                            me.mark(sy.dma_start(out=d["dbg_o"][:], in_=oT[:]), "lb")

                    @blk.tensor
                    def _(t):
                        me.w(t, "v", vdone)
                        if me.psO_rel is not None:
                            sem, val = me.psO_rel
                            me.w(t, sem, val)
                        last = None
                        for e in range(2):
                            for h in range(8):
                                last = t.matmul(
                                    me.psO[:, e * 512:e * 512 + 512],
                                    oT[:, h * 128:h * 128 + 128],
                                    wol[:, h * 1024 + e * 512:
                                        h * 1024 + e * 512 + 512],
                                    start=(h == 0), stop=(h == 7))
                        me.mark(last, "t")
                    tdone = me.cnt["t"]

                    @blk.vector
                    def _(v):
                        me.w(v, "t", tdone)
                        me.mark(v.nop(), "v")

                    self.postln_residual(me.psO[:, 0:1024], i, src="xres", dst="xres")
                    me.psO_rel = ("v", me.last_postln_v)
                self.barrier()

    def ffn(self, L):
        me, blk, d = self, self.block, self.d
        w1name = "w1_0" if L == 0 else "w1_1"
        w2name = "w2_0" if L == 0 else "w2_1"
        dstname = "xres" if L == 0 else "out"

        with contextlib.ExitStack() as ph:
            wq1T = self.sb("wq1T%d" % L, [128, 8 * HID], BF16, ph)
            ha = self.sb("ha%d" % L, [128, 4096], F32, ph)
            hb = self.sb("hbuf%d" % L, [128, 4096], F32, ph)
            hq = self.sb("hq%d" % L, [128, 4096], BF16, ph)
            xq = self.sb("xq%d" % L, [128, 1024], BF16, ph)
            xqT = self.sb("xqT%d" % L, [128, 2 * 1024], BF16, ph)
            lna = self.sb("lna%d" % L, [128, 3 * 1024], F32, ph)
            lnx = self.sb("lnx%d" % L, [128, 1024], F32, ph)
            brep = self.sb("brep", [128, 4096], F32, ph) if L == 0 else None
            grep = self.sb("grep", [128, 1024], F32, ph) if L == 0 else None
            erep = self.sb("erep", [128, 1024], F32, ph) if L == 0 else None

            with contextlib.ExitStack() as wph:
                wtmp = self.sb("wtmp2%d" % L, [128, 3 * 1024], F32, wph)
                self.weight_quant_pass(w1name, which=1, dest=wq1T, wtmp=wtmp,
                                       ha=ha, hq=hq, dstv=me.swv)
                if L == 0:
                    self.replicate_vec("b1_0", brep, 4096)
                    self.replicate_vec("gam0", grep, 1024)
                    self.replicate_vec("bet0", erep, 1024)
                self.barrier()

            ffn1_rel = [None, None, None, None]
            carry = {"xa": [None, None], "trq_prev": None, "gel7_prev": None,
                     "hqst": None}
            plans = {}

            _plan = {"v": 0, "s": 0, "t": 0}

            def seg_marks(P, eng, items):
                base = max(me.cnt[eng], _plan[eng])
                mm = {k: base + i + 1 for i, k in enumerate(items)}
                _plan[eng] = base + len(items)
                key = {"v": "vm", "s": "sm", "t": "tm"}[eng]
                P.setdefault(key, {}).update(mm)
                return mm

            def buf(tt):
                p = tt % 2
                return dict(
                    lnap=lna[:, p * 1024:p * 1024 + 1024],
                    ln2p=lna[:, 2048:3072],
                    lnxp=lnx[:, 0:1024],
                    xqp=xq[:, 0:1024],
                    xqTp=xqT[:, p * 1024:p * 1024 + 1024],
                    st6p=me.fst6[:, p * 12:p * 12 + 12],
                    st2p=me.fst2[:, p * 2:p * 2 + 2],
                    rstdp=me.frstd[:, p:p + 1],
                    nbp=me.fnb[:, p:p + 1],
                    lntp=me.flnt[:, p:p + 1],
                    amp=me.fam[:, p:p + 1],
                    sc1p=me.fsact[:, p * 2:p * 2 + 1],
                    sc2p=me.fsact[:, p * 2 + 1:p * 2 + 2],
                    rscp=me.frsc[:, p:p + 1],
                    xap=me.xa[:, p * 1024:p * 1024 + 1024])

            def front(tt):
                P, B = plans[tt], buf(tt)
                p = tt % 2
                v_items = ["bn1", "nb1"]
                if L == 0:
                    v_items += ["bn2", "nb2", "gb"]
                v_items += ["qnt", "clp1", "evq"]
                seg_marks(P, "v", v_items)
                s_items = ["st1", "nrm1"]
                if L == 0:
                    s_items += ["st2", "nrm2"]
                s_items += ["t1", "rnd1"]
                seg_marks(P, "s", s_items)
                seg_marks(P, "t", ["trq"])
                vm, sm = P["vm"], P["sm"]
                lnfin = B["ln2p"] if L == 0 else B["lnap"]

                @blk.sync
                def _(sy):
                    if carry["xa"][p] is not None:
                        vmk, smk = carry["xa"][p]
                        sy.wait_ge(me.semh["v"], vmk)
                        sy.wait_ge(me.semh["s"], smk)
                    me.mark(sy.dma_start(
                        out=B["xap"],
                        in_=d["xres"][tt * 128:tt * 128 + 128, :]), "xa%d" % p)
                P["xadone"] = me.cnt["xa%d" % p]
                carry["xa"][p] = (vm["nb1"], sm["nrm1"])

                @blk.vector
                def _(v):
                    v.wait_ge(me.semh["xa%d" % p], P["xadone"])
                    v.drain()
                    v.bn_stats(out=B["st6p"][:, 0:6], in_=B["xap"][:, 0:512])
                    v.bn_stats(out=B["st6p"][:, 6:12], in_=B["xap"][:, 512:1024])
                    v.drain()
                    assert me.mark(v.bn_aggr(out=B["st2p"], in_=B["st6p"]), "v") == vm["bn1"]
                    v.wait_ge(me.semh["s"], sm["st1"])
                    iv = v.scalar_tensor_tensor(out=B["nbp"], in0=B["st2p"][:, 0:1],
                                                scalar=-1.0, in1=B["rstdp"],
                                                op0=ALU.mult, op1=ALU.mult)
                    assert me.mark(iv, "v") == vm["nb1"]
                    if L == 0:
                        v.wait_ge(me.semh["s"], sm["nrm1"])
                        v.drain()
                        v.bn_stats(out=B["st6p"][:, 0:6], in_=B["lnap"][:, 0:512])
                        v.bn_stats(out=B["st6p"][:, 6:12], in_=B["lnap"][:, 512:1024])
                        v.drain()
                        assert me.mark(v.bn_aggr(out=B["st2p"], in_=B["st6p"]), "v") == vm["bn2"]
                        v.wait_ge(me.semh["s"], sm["st2"])
                        iv = v.scalar_tensor_tensor(out=B["nbp"], in0=B["st2p"][:, 0:1],
                                                    scalar=-1.0, in1=B["rstdp"],
                                                    op0=ALU.mult, op1=ALU.mult)
                        assert me.mark(iv, "v") == vm["nb2"]
                        v.wait_ge(me.semh["s"], sm["nrm2"])
                        v.tensor_tensor(out=B["lnxp"], in0=B["lnxp"], in1=grep[:],
                                        op=ALU.mult)
                        v.drain()
                        iv = v.tensor_tensor(out=lnfin, in0=B["lnxp"], in1=erep[:],
                                             op=ALU.add)
                        assert me.mark(iv, "v") == vm["gb"]
                        v.drain()
                    else:
                        v.wait_ge(me.semh["s"], sm["nrm1"])
                    v.tensor_reduce(out=B["amp"], in_=lnfin, axis=AX.X, op=ALU.max,
                                    apply_absolute_value=True)
                    v.drain()
                    v.tensor_scalar(out=B["amp"], in0=B["amp"], scalar1=EPS,
                                    scalar2=None, op0=ALU.max)
                    v.drain()
                    v.reciprocal(out=B["sc1p"], in_=B["amp"])
                    v.drain()
                    v.tensor_scalar(out=B["sc1p"], in0=B["sc1p"], scalar1=127.0,
                                    scalar2=None, op0=ALU.mult)
                    v.scalar_tensor_tensor(out=B["rscp"], in0=B["amp"],
                                           scalar=1.0 / 127.0, in1=me.swv[:],
                                           op0=ALU.mult, op1=ALU.mult)
                    v.drain()
                    assert me.mark(v.nop(), "v") == vm["qnt"]
                    v.wait_ge(me.semh["s"], sm["rnd1"])
                    if carry["trq_prev"] is not None:
                        v.wait_ge(me.semh["t"], carry["trq_prev"])
                    iv = v.tensor_scalar(out=B["xqp"], in0=B["lnxp"], scalar1=-128.0,
                                         scalar2=127.0, op0=ALU.max, op1=ALU.min)
                    assert me.mark(iv, "v") == vm["clp1"]
                    v.wait_ge(me.semh["t"], P["tm"]["trq"])
                    if plans.get(tt - 2) is not None:
                        v.wait_ge(me.semh["t"], plans[tt - 2]["tm"]["mm7"])
                    assert me.mark(v.tensor_copy(out=B["xqTp"], in_=me.psT[:, 0:1024]),
                                   "v") == vm["evq"]

                @blk.scalar
                def _(s):
                    s.wait_ge(me.semh["v"], vm["bn1"])
                    s.activation(out=B["lntp"], in_=B["st2p"][:, 1:2], func=AF.Ln,
                                 bias=me.epsv[:], scale=1.0)
                    s.drain()
                    iv = s.activation(out=B["rstdp"], in_=B["lntp"], func=AF.Exp,
                                      bias=0.0, scale=-0.5)
                    assert me.mark(iv, "s") == sm["st1"]
                    s.wait_ge(me.semh["v"], vm["nb1"])
                    iv = s.activation(out=B["lnap"], in_=B["xap"], func=AF.Identity,
                                      bias=B["nbp"], scale=B["rstdp"])
                    assert me.mark(iv, "s") == sm["nrm1"]
                    if L == 0:
                        s.wait_ge(me.semh["v"], vm["bn2"])
                        s.activation(out=B["lntp"], in_=B["st2p"][:, 1:2], func=AF.Ln,
                                     bias=me.epsv[:], scale=1.0)
                        s.drain()
                        iv = s.activation(out=B["rstdp"], in_=B["lntp"], func=AF.Exp,
                                          bias=0.0, scale=-0.5)
                        assert me.mark(iv, "s") == sm["st2"]
                        s.wait_ge(me.semh["v"], vm["nb2"])
                        iv = s.activation(out=B["lnxp"], in_=B["lnap"],
                                          func=AF.Identity,
                                          bias=B["nbp"], scale=B["rstdp"])
                        assert me.mark(iv, "s") == sm["nrm2"]
                    s.wait_ge(me.semh["v"], vm["qnt"])
                    iv = s.activation(out=B["lnxp"], in_=lnfin, func=AF.Identity,
                                      bias=me.magic[:], scale=B["sc1p"])
                    assert me.mark(iv, "s") == sm["t1"]
                    s.drain()
                    iv = s.activation(out=B["lnxp"], in_=B["lnxp"], func=AF.Identity,
                                      bias=me.nmagic[:], scale=1.0)
                    assert me.mark(iv, "s") == sm["rnd1"]

                @blk.tensor
                def _(t):
                    t.wait_ge(me.semh["v"], vm["clp1"])
                    if plans.get(tt - 1) is not None:
                        t.wait_ge(me.semh["v"], plans[tt - 1]["vm"]["evq"])
                    last = None
                    for dc in range(8):
                        last = me.trans(t, me.psT[:, dc * 128:dc * 128 + 128],
                                        B["xqp"][:, dc * 128:dc * 128 + 128],
                                        start=(dc == 0), stop=(dc == 7))
                    assert me.mark(last, "t") == P["tm"]["trq"]
                carry["trq_prev"] = P["tm"]["trq"]

            def back_a(tt):
                P, B = plans[tt], buf(tt)
                seg_marks(P, "t", ["mm%d" % k for k in range(8)])
                va = []
                if L == 0:
                    va += ["epi%d" % k for k in range(8)]
                va += ["qn2"]
                seg_marks(P, "v", va)
                seg_marks(P, "s", ["gel%d" % k for k in range(8)] +
                          (["t2", "rnd2"] if L == 0 else ["t2"]))
                vm, sm, tm = P["vm"], P["sm"], P["tm"]

                @blk.tensor
                def _(t):
                    for k in range(8):
                        hf = k % 4
                        t.wait_ge(me.semh["v"], vm["evq"])
                        if ffn1_rel[hf] is not None:
                            sem, val = ffn1_rel[hf]
                            me.w(t, sem, val)
                        last = None
                        for dc in range(8):
                            last = t.matmul(
                                me.psS[:, hf * 512:hf * 512 + 512],
                                B["xqTp"][:, dc * 128:dc * 128 + 128],
                                wq1T[:, dc * HID + k * 512:dc * HID + k * 512 + 512],
                                start=(dc == 0), stop=(dc == 7))
                        assert me.mark(last, "t") == tm["mm%d" % k]
                        ffn1_rel[hf] = ("v", vm["epi%d" % k]) if L == 0 else \
                                       ("s", sm["gel%d" % k])

                @blk.vector
                def _(v):
                    if L == 0:
                        for k in range(8):
                            v.wait_ge(me.semh["t"], tm["mm%d" % k])
                            if k == 0 and carry["gel7_prev"] is not None:
                                v.wait_ge(me.semh["s"], carry["gel7_prev"])
                            iv = v.scalar_tensor_tensor(
                                out=ha[:, k * 512:k * 512 + 512],
                                in0=me.psS[:, (k % 4) * 512:(k % 4) * 512 + 512],
                                scalar=B["rscp"],
                                in1=brep[:, k * 512:k * 512 + 512],
                                op0=ALU.mult, op1=ALU.add)
                            assert me.mark(iv, "v") == vm["epi%d" % k]
                    for k in range(8):
                        v.wait_ge(me.semh["s"], sm["gel%d" % k])
                        if k == 0:
                            v.drain()
                        v.tensor_reduce(out=me.fam8[:, k:k + 1],
                                        in_=hb[:, k * 512:k * 512 + 512],
                                        axis=AX.X, op=ALU.max,
                                        apply_absolute_value=True)
                    v.drain()
                    v.tensor_reduce(out=B["amp"], in_=me.fam8[:], axis=AX.X,
                                    op=ALU.max, apply_absolute_value=True)
                    v.drain()
                    v.tensor_scalar(out=B["amp"], in0=B["amp"], scalar1=EPS,
                                    scalar2=None, op0=ALU.max)
                    v.drain()
                    v.reciprocal(out=B["sc2p"], in_=B["amp"])
                    v.drain()
                    v.tensor_scalar(out=B["sc2p"], in0=B["sc2p"], scalar1=127.0,
                                    scalar2=None, op0=ALU.mult)
                    v.drain()
                    iv = v.tensor_scalar(out=me.rs2all[:, tt:tt + 1], in0=B["amp"],
                                         scalar1=1.0 / 127.0, scalar2=None,
                                         op0=ALU.mult)
                    assert me.mark(iv, "v") == vm["qn2"]

                @blk.scalar
                def _(s):
                    for k in range(8):
                        if L == 0:
                            s.wait_ge(me.semh["v"], vm["epi%d" % k])
                            iv = s.activation(out=hb[:, k * 512:k * 512 + 512],
                                              in_=ha[:, k * 512:k * 512 + 512],
                                              func=AF.Gelu)
                        else:
                            s.wait_ge(me.semh["t"], tm["mm%d" % k])
                            if k == 0:
                                s.wait_ge(me.semh["v"], vm["qnt"])
                                if carry.get("hbfree") is not None:
                                    s.wait_ge(me.semh["v"], carry["hbfree"])
                            iv = s.activation(
                                out=hb[:, k * 512:k * 512 + 512],
                                in_=me.psS[:, (k % 4) * 512:(k % 4) * 512 + 512],
                                func=AF.Gelu, bias=0.0, scale=B["rscp"])
                        assert me.mark(iv, "s") == sm["gel%d" % k]
                    s.wait_ge(me.semh["v"], vm["qn2"])
                    iv = s.activation(out=hb[:], in_=hb[:], func=AF.Identity,
                                      bias=me.magic[:], scale=B["sc2p"])
                    assert me.mark(iv, "s") == sm["t2"]
                    if L == 0:
                        s.drain()
                        iv = s.activation(out=hb[:], in_=hb[:], func=AF.Identity,
                                          bias=me.nmagic[:], scale=1.0)
                        assert me.mark(iv, "s") == sm["rnd2"]
                carry["gel7_prev"] = sm["gel7"]

            def back_b(tt):
                P = plans[tt]
                seg_marks(P, "v", ["clp2"])
                vm, sm = P["vm"], P["sm"]

                @blk.vector
                def _(v):
                    if L == 0:
                        v.wait_ge(me.semh["s"], sm["rnd2"])
                    else:
                        v.wait_ge(me.semh["s"], sm["t2"])
                    if carry.get("hqst") is not None:
                        v.wait_ge(me.semh["ev0"], carry["hqst"])
                    if L == 0:
                        iv = v.tensor_scalar(out=hq[:], in0=hb[:], scalar1=-128.0,
                                             scalar2=127.0, op0=ALU.max,
                                             op1=ALU.min)
                    else:
                        v.drain()
                        v.tensor_scalar(out=hb[:], in0=hb[:],
                                        scalar1=me.magic[:], scalar2=-128.0,
                                        op0=ALU.subtract, op1=ALU.max)
                        v.drain()
                        iv = v.tensor_scalar(out=hq[:], in0=hb[:], scalar1=127.0,
                                             scalar2=None, op0=ALU.min)
                    assert me.mark(iv, "v") == vm["clp2"]
                carry["hbfree"] = vm["clp2"]

                @blk.sync
                def _(sy):
                    sy.wait_ge(me.semh["v"], vm["clp2"])
                    carry["hqst"] = me.mark(sy.dma_start(
                        out=d["h_q"][tt * 128:tt * 128 + 128, :],
                        in_=hq[:]), "ev0")

            plans[0] = {}
            front(0)
            for k in range(NT + 1):
                if k + 1 < NT:
                    plans[k + 1] = {}
                    front(k + 1)
                if k >= 1:
                    back_b(k - 1)
                if k < NT:
                    back_a(k)
            self.barrier()

        with contextlib.ExitStack() as ph:
            wq2T = self.sb("wq2T%d" % L, [128, 32 * 1024], BF16, ph)
            hqs = self.sb("hqs%d" % L, [128, 2 * 4096], BF16, ph)
            ha = self.sb("ha2%d" % L, [128, 4096], F32, ph)
            hq = self.sb("hq2%d" % L, [128, 4096], BF16, ph)
            brep2 = self.sb("brep2", [128, 1024], F32, ph) if L == 0 else None

            with contextlib.ExitStack() as wph:
                wtmp = self.sb("wtmp3%d" % L, [128, 2 * 4096], F32, wph)
                self.weight_quant_pass(w2name, which=2, dest=wq2T, wtmp=wtmp,
                                       ha=ha, hq=hq, dstv=me.swv2)
                if L == 0:
                    self.replicate_vec("b2_0", brep2, 1024)
                self.barrier()
            hqT = self.sb("hqT%d" % L, [128, 2 * 4096], BF16, ph)

            psO_rel = None
            trm = {}
            evm = {}
            mmk = {}
            ldm = {}

            def emit_ld(k):
                kp = k % 2

                @blk.sync
                def _(sy):
                    if k - 2 >= 0:
                        sy.wait_ge(me.semh["t"], trm[k - 2][3])
                    ldm[k] = me.mark(sy.dma_start(
                        out=hqs[:, kp * 4096:kp * 4096 + 4096],
                        in_=d["h_q"][k * 128:k * 128 + 128, :]), "hq%d" % kp)

            def emit_tr(tt):
                p = tt % 2
                evm[tt] = [me.cnt["v"] + b + 1 for b in range(4)]

                @blk.tensor
                def _(t):
                    t.wait_ge(me.semh["hq%d" % p], ldm[tt])
                    marks = []
                    for b in range(4):
                        hf = b % 2
                        if b >= 2:
                            t.wait_ge(me.semh["v"], evm[tt][b - 2])
                        elif tt - 1 >= 0:
                            t.wait_ge(me.semh["v"], evm[tt - 1][b + 2])
                        last = None
                        for k in range(8):
                            hc = b * 8 + k
                            last = me.trans(t, me.psT[:, hf * 1024 + k * 128:
                                                   hf * 1024 + k * 128 + 128],
                                            hqs[:, p * 4096 + hc * 128:
                                                p * 4096 + hc * 128 + 128],
                                            start=(k == 0), stop=(k == 7))
                        marks.append(me.mark(last, "t"))
                    trm[tt] = marks

                @blk.vector
                def _(v):
                    for b in range(4):
                        hf = b % 2
                        v.wait_ge(me.semh["t"], trm[tt][b])
                        if tt - 2 >= 0 and b == 0:
                            v.wait_ge(me.semh["t"], mmk[tt - 2])
                        iv = v.tensor_copy(
                            out=hqT[:, p * 4096 + b * 1024:
                                    p * 4096 + b * 1024 + 1024],
                            in_=me.psT[:, hf * 1024:hf * 1024 + 1024])
                        assert me.mark(iv, "v") == evm[tt][b]

            def emit_mm(tt):
                nonlocal psO_rel
                p = tt % 2

                @blk.tensor
                def _(t):
                    t.wait_ge(me.semh["v"], evm[tt][3])
                    if psO_rel is not None:
                        for sem, val in psO_rel:
                            me.w(t, sem, val)
                    last = None
                    for e in range(2):
                        for hc in range(32):
                            last = t.matmul(
                                me.psO[:, e * 512:e * 512 + 512],
                                hqT[:, p * 4096 + hc * 128:p * 4096 + hc * 128 + 128],
                                wq2T[:, hc * 1024 + e * 512:hc * 1024 + e * 512 + 512],
                                start=(hc == 0), stop=(hc == 31))
                    mmk[tt] = me.mark(last, "t")
                tdone = mmk[tt]

                @blk.vector
                def _(v):
                    me.w(v, "t", tdone)
                    if me.last_postln_free is not None:
                        for sem, val in me.last_postln_free:
                            me.w(v, sem, val)
                    v.drain()
                    v.tensor_tensor(out=me.rsc1[:], in0=me.rs2all[:, tt:tt + 1],
                                    in1=me.swv2[:], op=ALU.mult)
                    v.drain()
                    if L == 0:
                        iv = v.scalar_tensor_tensor(
                            out=ha[:, 0:1024], in0=me.psO[:, 0:1024],
                            scalar=me.rsc1[:], in1=brep2[:],
                            op0=ALU.mult, op1=ALU.add)
                    else:
                        iv = v.tensor_scalar(
                            out=ha[:, 0:1024], in0=me.psO[:, 0:1024],
                            scalar1=me.rsc1[:], scalar2=None,
                            op0=ALU.mult)
                    me.mark(iv, "v")
                epidone = me.cnt["v"]

                self.postln_residual(ha[:, 0:1024], tt, src="xres", dst=dstname)
                psO_rel = [("v", epidone)]

            emit_ld(0)
            emit_ld(1)
            emit_tr(0)
            for tt in range(NT):
                if tt + 2 < NT:
                    emit_ld(tt + 2)
                if tt + 1 < NT:
                    emit_tr(tt + 1)
                emit_mm(tt)
            self.barrier()

    def replicate_vec(self, name, dest, n):
        me, blk, d = self, self.block, self.d

        @blk.sync
        def _(sy):
            me.w(sy, "v")
            me.mark(sy.dma_start(out=me.vec1[0:1, 0:n],
                                 in_=d[name].rearrange("(o n) -> o n", o=1)), "lb")
        lddone = me.cnt["lb"]
        for b in range((n + 511) // 512):
            wdt = min(512, n - b * 512)

            @blk.tensor
            def _(t):
                me.w(t, "lb", lddone)
                me.w(t, "v")
                me.mark(t.matmul(me.psM[:, 0:wdt], me.ones1[0:1, 0:128],
                                 me.vec1[0:1, b * 512:b * 512 + wdt],
                                 start=True, stop=True), "t")
            tdone = me.cnt["t"]

            @blk.vector
            def _(v):
                me.w(v, "t", tdone)
                me.mark(v.tensor_copy(out=dest[:, b * 512:b * 512 + wdt],
                                      in_=me.psM[:, 0:wdt]), "v")

    def weight_quant_pass(self, name, which, dest, wtmp, ha, hq, dstv):
        me, blk, d = self, self.block, self.d
        w = d[name]
        rows, cols = (HID, D) if which == 1 else (D, HID)
        ntile = rows // 128
        nslot = wtmp.shape[1] // cols

        v0 = me.cnt["v"]

        def lsem(it):
            return ("ld", "lb", "lc")[it % 3]
        ldmA = {}

        @blk.sync
        def _(sy):
            me.w(sy, "v")
            for it in range(ntile):
                if it >= nslot:
                    sy.wait_ge(me.semh["v"], v0 + (it - nslot) + 1)
                sl = (it % nslot) * cols
                ldmA[it] = me.mark(sy.dma_start(out=wtmp[:, sl:sl + cols],
                                                in_=w[it * 128:it * 128 + 128, :]),
                                   lsem(it))

        @blk.vector
        def _(v):
            for it in range(ntile):
                v.wait_ge(me.semh[lsem(it)], ldmA[it])
                sl = (it % nslot) * cols
                v.drain()
                v.tensor_reduce(out=me.am[:], in_=wtmp[:, sl:sl + cols],
                                axis=AX.X, op=ALU.add, apply_absolute_value=True)
                v.drain()
                if it == 0:
                    iv = v.tensor_copy(out=me.acc1[:], in_=me.am[:])
                else:
                    iv = v.tensor_tensor(out=me.acc1[:], in0=me.acc1[:],
                                         in1=me.am[:], op=ALU.add)
                v.drain()
                assert me.mark(iv, "v") == v0 + it + 1
        vdone = me.cnt["v"]

        @blk.tensor
        def _(t):
            me.w(t, "v", vdone)
            me.w(t, "s")
            me.mark(t.matmul(me.psM[0:1, 0:1], me.onesc[:], me.acc1[:],
                             start=True, stop=True), "t")
        tdone = me.cnt["t"]

        @blk.vector
        def _(v):
            me.w(v, "t", tdone)
            me.mark(v.tensor_scalar(out=me.sc11[:], in0=me.psM[0:1, 0:1],
                                    scalar1=1.0 / (rows * cols), scalar2=EPS,
                                    op0=ALU.mult, op1=ALU.max), "v")
        vdone = me.cnt["v"]

        @blk.tensor
        def _(t):
            me.w(t, "v", vdone)
            me.mark(t.matmul(me.psM[:, 0:1], me.ones1[0:1, 0:128],
                             me.sc11[0:1, 0:1], start=True, stop=True), "t")
        tdone = me.cnt["t"]

        @blk.vector
        def _(v):
            me.w(v, "t", tdone)
            v.tensor_copy(out=dstv[:], in_=me.psM[:, 0:1])
            v.drain()
            v.reciprocal(out=me.am[:], in_=dstv[:])
            v.drain()
            me.mark(v.nop(), "v")
        recipdone = me.cnt["v"]

        hslot = min(ha.shape[1] // cols, 4)
        qslot = min(hq.shape[1] // cols, 4)
        g = max(1, cols // 1024)
        s0 = me.cnt["s"]
        v0b = me.cnt["v"]
        t0 = me.cnt["t"]
        ldmB = {}

        def sm_t1(it):
            return s0 + 2 * it + 1

        def sm_rnd(it):
            return s0 + 2 * it + 2

        def vm_clip(it):
            return v0b + it * (1 + g) + 1

        def vm_evac(it, bb):
            return v0b + it * (1 + g) + 2 + bb

        def tm(it, bb):
            return t0 + it * g + bb + 1

        @blk.sync
        def _(sy):
            me.w(sy, "v")
            for it in range(ntile):
                if it >= nslot:
                    sy.wait_ge(me.semh["s"], sm_t1(it - nslot))
                sl = (it % nslot) * cols
                ldmB[it] = me.mark(sy.dma_start(out=wtmp[:, sl:sl + cols],
                                                in_=w[it * 128:it * 128 + 128, :]),
                                   lsem(it))

        @blk.scalar
        def _(s):
            s.wait_ge(me.semh["v"], recipdone)
            for it in range(ntile):
                s.wait_ge(me.semh[lsem(it)], ldmB[it])
                if it >= hslot:
                    s.wait_ge(me.semh["v"], vm_clip(it - hslot))
                sl = (it % nslot) * cols
                hs = (it % hslot) * cols
                iv = s.activation(out=ha[:, hs:hs + cols],
                                  in_=wtmp[:, sl:sl + cols], func=AF.Identity,
                                  bias=me.magic[:], scale=me.am[:])
                assert me.mark(iv, "s") == sm_t1(it)
                s.drain()
                iv = s.activation(out=ha[:, hs:hs + cols],
                                  in_=ha[:, hs:hs + cols], func=AF.Identity,
                                  bias=me.nmagic[:], scale=1.0)
                assert me.mark(iv, "s") == sm_rnd(it)
                s.drain()

        @blk.vector
        def _(v):
            for it in range(ntile):
                v.wait_ge(me.semh["s"], sm_rnd(it))
                if it >= qslot:
                    v.wait_ge(me.semh["t"], tm(it - qslot, g - 1))
                hs = (it % hslot) * cols
                qs = (it % qslot) * cols
                v.drain()
                iv = v.tensor_scalar(out=hq[:, qs:qs + cols],
                                     in0=ha[:, hs:hs + cols],
                                     scalar1=-1.0, scalar2=1.0,
                                     op0=ALU.max, op1=ALU.min)
                assert me.mark(iv, "v") == vm_clip(it)
                for bb in range(g):
                    v.wait_ge(me.semh["t"], tm(it, bb))
                    gi = it * g + bb
                    ph2 = gi % 2
                    v.drain()
                    iv = None
                    for k in range(8):
                        cc = bb * 8 + k
                        if which == 1:
                            o = dest[:, cc * HID + it * 128:
                                     cc * HID + it * 128 + 128]
                        else:
                            o = dest[:, cc * 1024 + it * 128:
                                     cc * 1024 + it * 128 + 128]
                        iv = v.tensor_copy(
                            out=o, in_=me.psT[:, ph2 * 1024 + k * 128:
                                              ph2 * 1024 + k * 128 + 128])
                    assert me.mark(iv, "v") == vm_evac(it, bb)

        @blk.tensor
        def _(t):
            for it in range(ntile):
                t.wait_ge(me.semh["v"], vm_clip(it))
                qs = (it % qslot) * cols
                for bb in range(g):
                    gi = it * g + bb
                    ph2 = gi % 2
                    if gi >= 2:
                        pit, pbb = divmod(gi - 2, g)
                        t.wait_ge(me.semh["v"], vm_evac(pit, pbb))
                    last = None
                    for k in range(8):
                        cc = bb * 8 + k
                        last = me.trans(t, me.psT[:, ph2 * 1024 + k * 128:
                                               ph2 * 1024 + k * 128 + 128],
                                        hq[:, qs + cc * 128:qs + cc * 128 + 128],
                                        start=(k == 0), stop=(k == 7))
                    assert me.mark(last, "t") == tm(it, bb)

    def finish(self):
        me, blk = self, self.block
        snap = dict(me.cnt)

        @blk.sync
        def _(sy):
            for s in me.semh:
                sy.wait_ge(me.semh[s], snap[s])


def build_program(phases=("l0a", "l0f", "l1a", "l1f"), debug=False):
    return MK(phases, debug).build()


def _in_maps(kw):
    x = np.ascontiguousarray(np.asarray(kw["L0_attn_x"], np.float32))
    names = {
        "W0": "L0_attn_W", "w1_0": "L0_ffn_w1", "b1_0": "L0_ffn_b1",
        "w2_0": "L0_ffn_w2", "b2_0": "L0_ffn_b2", "gam0": "L0_ffn_gamma",
        "bet0": "L0_ffn_beta", "wq1": "L1_attn_wq", "wkd1": "L1_attn_wk_down",
        "wvd1": "L1_attn_wv_down", "wku1": "L1_attn_wk_up", "wvu1": "L1_attn_wv_up",
        "wo1": "L1_attn_wo", "w1_1": "L1_ffn_w1", "w2_1": "L1_ffn_w2",
    }
    weights = {k: np.ascontiguousarray(np.asarray(kw[v], np.float32))
               for k, v in names.items()}
    return [dict(weights, x=x[b]) for b in range(8)]


def kernel(**kw):
    nc = build_program()
    res = run_bass_kernel_spmd(nc, _in_maps(kw), core_ids=list(range(8)))
    return np.stack([np.asarray(res.results[b]["out"], np.float32)
                     for b in range(8)], axis=0)
